# version 02
# baseline (speedup 1.0000x reference)
import numpy as np
from contextlib import ExitStack

import concourse.bass as bass
import concourse.mybir as mybir
from concourse.bass_utils import run_bass_kernel_spmd
from concourse.expressions_rust import RuntimeValue as _RV

F32 = mybir.dt.float32
BF16 = mybir.dt.bfloat16
AF = mybir.ActivationFunctionType
ALU = mybir.AluOpType
AXX = mybir.AxisListType.X

NCORES = 8
T = 8192
TOK = T // NCORES
D = 2048
NCH = D // 128
TC = 512
NTC = TOK // TC

SUBLAYERS = ("ffn0a", "attn0", "ffn0b", "ffn1a", "attn1", "ffn1b")

LAUNCH_GROUPS = [list(SUBLAYERS)]


class Plan:
    ENGS = ("t", "v", "s", "g", "y")

    def __init__(self):
        self.q = {e: [] for e in self.ENGS}
        self.cnt = {}

    def val(self, sem):
        return self.cnt.get(sem, 0)

    def add(self, eng, fn, wait=(), inc=(), nops=1):
        wait = tuple((s_, v) for (s_, v) in wait if v > 0)
        if eng == "v":
            assert not any(s_ == "ve" for s_, _ in inc)
            self.cnt["ve"] = self.cnt.get("ve", 0) + nops
        elif eng == "s":
            assert not any(s_ == "se" for s_, _ in inc)
            self.cnt["se"] = self.cnt.get("se", 0) + nops
        for s_, n in inc:
            self.cnt[s_] = self.cnt.get(s_, 0) + n
        self.q[eng].append((wait, fn, tuple(inc), nops))


class Ctx:
    pass


class _SerialEng:

    _PASS = {"wait_ge", "nop", "maybe_wait_ge", "maybe_nop_then_inc"}

    def __init__(self, eng, sem):
        self._eng = eng
        self._sem = sem
        self._n = 0

    def __getattr__(self, name):
        attr = getattr(self._eng, name)
        if name in self._PASS or not callable(attr):
            return attr

        def wrapped(*a, **k):
            if self._n > 0:
                self._eng.wait_ge(self._sem, self._n)
            inst = attr(*a, **k)
            inst.then_inc(self._sem, 1)
            self._n += 1
            return inst
        return wrapped


def _realize(nc, plan, sems):
    block_cm = nc.Block()
    block = block_cm.__enter__()

    def runner(engname):
        def run(eng):
            serial = engname in ("v", "s")
            if serial:
                eng = _SerialEng(eng, sems["ve" if engname == "v" else "se"])
            for waits, fn, incs, nops in plan.q[engname]:
                for s_, v in waits:
                    eng.wait_ge(sems[s_], v)
                before = eng._n if serial else 0
                inst = fn(eng)
                if serial:
                    assert eng._n - before == nops, (
                        f"{engname} item emitted {eng._n - before} ops, "
                        f"declared nops={nops}")
                else:
                    for s_, n in incs:
                        inst.then_inc(sems[s_], n)
        return run

    block.tensor(runner("t"))
    block.vector(runner("v"))
    block.scalar(runner("s"))
    block.gpsimd(runner("g"))
    block.sync(runner("y"))
    block_cm.__exit__(None, None, None)


def mm(C, out, lhsT, rhs, start=True, stop=True, wait=(), inc=()):
    C.plan.add("t",
               (lambda out=out, lhsT=lhsT, rhs=rhs, start=start, stop=stop:
                lambda e: e.matmul(out, lhsT, rhs, start=start, stop=stop))(),
               wait=wait, inc=inc)


def ts(i, n=128):
    return slice(i * n, (i + 1) * n)


def cstcol(C, name):
    j = C.const_idx[name]
    return C.consts[:, j:j + 1]


def cst1(C, name):
    j = C.const_idx[name]
    return C.consts[0:1, j:j + 1]


def emit_ln(C):
    p = C.plan
    for tc in range(NTC):
        stat = C.psum[7]
        xw = [("ve", C.x_ready_ve), ("dy", C.x_ready_dy)]
        for c in range(NCH):
            buf = C.sq_bf[:, 0:TC]
            w = list(xw) if (tc == 0 and c == 0) else []
            if C.ln_marks:
                w.append(("pe", C.ln_marks[-1]))
            p.add("s", (lambda buf=buf, c=c, tc=tc: lambda e: e.activation(
                buf, C.x[:, c, ts(tc, TC)], AF.Square))(),
                wait=w)
            w2 = list(xw) if c == 0 else []
            if c == 0:
                w2.append(("ve", C.ln_stats_ve))
            mm(C, stat[0:1, 0:TC], C.ones_inv2048_f32, C.x[:, c, ts(tc, TC)],
               start=(c == 0), stop=(c == NCH - 1), wait=w2)
            mm(C, stat[64:65, 0:TC], C.ones_inv2048_bf, buf,
               start=(c == 0), stop=(c == NCH - 1),
               wait=[("se", p.val("se"))], inc=[("pe", 1)])
            C.ln_marks.append(p.val("pe"))
        pe_stats = p.val("pe")

        def stats_fn(tc=tc, stat=stat):
            def f(e):
                msl = C.stat_m[0:1, ts(tc, TC)]
                e.tensor_copy(msl, stat[0:1, 0:TC])
                e.tensor_tensor(C.row_a[0:1, 0:TC], msl, msl, ALU.mult)
                return e.tensor_tensor(C.row_b[0:1, 0:TC], stat[64:65, 0:TC],
                                       C.row_a[0:1, 0:TC], ALU.subtract)
            return f
        p.add("v", stats_fn(), wait=[("pe", pe_stats)], nops=3)
        C.ln_stats_ve = p.val("ve")
        p.add("s", (lambda: lambda e: e.activation(
            C.row_a[0:1, 0:TC], C.row_b[0:1, 0:TC], AF.Sqrt,
            bias=cst1(C, "ln_eps")))(),
            wait=[("ve", p.val("ve"))])
        p.add("v", (lambda tc=tc: lambda e: e.reciprocal(
            C.stat_r[0:1, ts(tc, TC)], C.row_a[0:1, 0:TC]))(),
            wait=[("se", p.val("se"))])
        ve_rstd = p.val("ve")

        mb, rb = C.psum[5], C.psum[6]
        mm(C, mb[:, 0:TC], C.ones1_f32, C.stat_m[0:1, ts(tc, TC)],
           wait=[("ve", ve_rstd), ("se", C.b56_se)])
        mm(C, rb[:, 0:TC], C.ones1_f32, C.stat_r[0:1, ts(tc, TC)],
           inc=[("pe", 1)])
        pe_bcast = p.val("pe")

        for c in range(NCH):
            def norm_fn(c=c, tc=tc, mb=mb, rb=rb):
                def f(e):
                    e.tensor_tensor(C.f32scr[:, 0:TC], C.x[:, c, ts(tc, TC)],
                                    mb[:, 0:TC], ALU.subtract)
                    return e.tensor_tensor(C.ln[:, c, ts(tc, TC)],
                                           C.f32scr[:, 0:TC], rb[:, 0:TC],
                                           ALU.mult)
                return f
            p.add("v", norm_fn(),
                  wait=[("pe", pe_bcast)] if c == 0 else (), nops=2)
    C.ln_ready_ve = p.val("ve")


def emit_gates(C, which):
    p = C.plan
    wT = C.t_wrT if which == "l0" else C.t_gwT
    bias_t = C.t_br if which == "l0" else C.t_rb

    p.add("g", (lambda wT=wT: lambda e: e.dma_start(
        out=C.rw_sb[:], in_=wT.rearrange("(c p) e -> p c e", p=128)))(),
        wait=[("pe", p.val("pe"))], inc=[("dg", 16)])
    p.add("g", (lambda bias_t=bias_t: lambda e: e.dma_start(
        out=C.row8[0:1, :], in_=bias_t))(), inc=[("dg", 16)])
    dg0 = p.val("dg")

    for c in range(NCH):
        mm(C, C.psum[7][0:1, 0:8], C.ones_f32, C.rw_sb[:, c, :],
           start=(c == 0), stop=(c == NCH - 1),
           wait=[("dg", dg0), ("ve", C.ln_ready_ve)] if c == 0 else (),
           inc=[("pe", 1)] if c == NCH - 1 else ())
    pe_ws = p.val("pe")
    p.add("s", lambda e: e.copy(C.row_a[0:1, 0:8], C.psum[7][0:1, 0:8]),
          wait=[("pe", pe_ws)])
    se_ws = p.val("se")
    mm(C, C.psum[5][:, 0:8], C.ones1_f32, C.row_a[0:1, 0:8],
       wait=[("se", se_ws), ("ve", C.ln_ready_ve)])
    mm(C, C.psum[5][:, 8:16], C.ones1_f32, C.row8[0:1, :],
       wait=[("dg", dg0)], inc=[("pe", 1)])
    pe_bc = p.val("pe")
    p.add("s", lambda e: e.copy(C.gate_wsb[:, 0:8], C.psum[5][:, 0:8]),
          wait=[("pe", pe_bc)])
    p.add("s", lambda e: e.copy(C.gate_bb[:, 0:8], C.psum[5][:, 8:16]))
    C.b56_se = p.val("se")
    se_bc = p.val("se")

    for tt in range(TOK // 128):
        lg = C.psum[4]
        for c in range(NCH):
            w = [("se", p.val("se"))] if c == 0 else []
            mm(C, lg[:, 0:8], C.x[:, c, ts(tt)], C.rw_sb[:, c, :],
               start=(c == 0), stop=(c == NCH - 1), wait=w)
        mm(C, lg[:, 8:9], C.stat_m[0:1, ts(tt)], C.ones11_f32)
        mm(C, lg[:, 9:10], C.stat_r[0:1, ts(tt)], C.ones11_f32, inc=[("pe", 1)])
        pe_lg = p.val("pe")

        def logits_fn(which=which, lg=lg):
            def f(e):
                mcol = C.colscr[:, 0:1]
                rcol = C.colscr[:, 1:2]
                e.tensor_copy(mcol, lg[:, 8:9])
                e.tensor_copy(rcol, lg[:, 9:10])
                e.tensor_scalar_mul(C.g8, C.gate_wsb[:, 0:8], mcol)
                e.tensor_tensor(C.g8, lg[:, 0:8], C.g8, ALU.subtract)
                e.tensor_scalar_mul(C.g8, C.g8, rcol)
                if which == "l0":
                    e.tensor_tensor(C.g8, C.g8, C.gate_bb[:, 0:8], ALU.add)
                    e.reduce_max(mcol, C.g8, axis=AXX)
                    e.tensor_single_scalar(C.g8b, C.g8, mcol, ALU.is_equal)
                    return e.tensor_scalar_mul(rcol, mcol, -1.0)
                else:
                    return e.tensor_copy(C.g8d, C.g8)
            return f
        p.add("v", logits_fn(), wait=[("pe", pe_lg), ("se", se_bc)],
              nops=(9 if which == "l0" else 6))
        ve1 = p.val("ve")

        if which == "l0":
            p.add("s", lambda e: e.activation(
                C.g8c, C.g8, AF.Exp, bias=C.colscr[:, 1:2]),
                wait=[("ve", ve1)])
            se2 = p.val("se")

            def fin(e):
                scol = C.colscr[:, 2:3]
                e.tensor_reduce(scol, C.g8c, AXX, ALU.add)
                e.reciprocal(scol, scol)
                e.tensor_scalar_mul(C.g8b, C.g8b, scol)
                return e.tensor_scalar_mul(C.g8b, C.g8b, 0.5 / 64.0)
            p.add("v", fin, wait=[("se", se2)], nops=4)
        else:
            p.add("s", lambda e: e.activation(C.g8c, C.g8d, AF.Sigmoid),
                  wait=[("ve", ve1)])
            se2 = p.val("se")

            def fin(tt=tt):
                def f(e):
                    m1c = C.colscr[:, 0:1]
                    m2c = C.colscr[:, 1:2]
                    dcol = C.colscr[:, 2:3]
                    e.tensor_tensor(C.g8, C.g8c, C.gate_bb[:, 0:8], ALU.add)
                    e.reduce_max(m1c, C.g8, axis=AXX)
                    e.tensor_single_scalar(C.g8b, C.g8, m1c, ALU.is_equal)
                    e.tensor_scalar(C.g8d, C.g8b, 1e30, None, ALU.mult)
                    e.tensor_tensor(C.g8d, C.g8, C.g8d, ALU.subtract)
                    if DEBUG_DUMP and tt == 0:
                        e.tensor_copy(C.dbg_sb[:, 0:8], C.g8c)
                        e.tensor_copy(C.dbg_sb[:, 8:16], C.g8)
                        e.tensor_copy(C.dbg_sb[:, 16:24], C.g8b)
                        e.tensor_copy(C.dbg_sb[:, 24:32], C.g8d)
                    e.reduce_max(m2c, C.g8d, axis=AXX)
                    e.tensor_single_scalar(C.g8d, C.g8d, m2c, ALU.is_equal)
                    e.tensor_tensor(C.g8b, C.g8b, C.g8d, ALU.add)
                    e.tensor_tensor(C.g8b, C.g8b, C.g8c, ALU.mult)
                    if DEBUG_DUMP and tt == 0:
                        e.tensor_copy(C.dbg_sb[:, 32:40], C.g8d)
                        e.tensor_copy(C.dbg_sb[:, 40:48], C.g8b)
                    e.tensor_reduce(dcol, C.g8b, AXX, ALU.add)
                    e.tensor_scalar_max(dcol, dcol, 1e-12)
                    e.reciprocal(dcol, dcol)
                    e.tensor_scalar_mul(C.g8b, C.g8b, dcol)
                    if DEBUG_DUMP and tt == 0:
                        e.tensor_copy(C.dbg_sb[:, 48:49], dcol)
                    return e.tensor_scalar_mul(C.g8b, C.g8b, 0.5)
                return f
            p.add("v", fin(), wait=[("se", se2)],
                  nops=14 + (7 if (DEBUG_DUMP and tt == 0) else 0))
        ve_g = p.val("ve")

        p.add("t", (lambda: lambda e: e.transpose(
            C.psum[4][0:8, 16:144], C.g8b, C.ident_f32))(),
            wait=[("ve", ve_g)], inc=[("pe", 1)])
        p.add("s", (lambda tt=tt: lambda e: e.copy(
            C.gT[0:8, ts(tt)], C.psum[4][0:8, 16:144]))(),
            wait=[("pe", p.val("pe"))])
    C.gates_se = p.val("se")


def emit_ffn0(C):
    p = C.plan
    emit_gates(C, "l0")

    a2_sb = C.arena_a[:, 0:8192].rearrange("p (c r) -> p c r", c=NCH)
    b2_sb = C.arena_a[:, 8192:16384].rearrange("p (r o) -> p r o", r=4)
    gb_sb = C.arena_b[:, 0:2048].rearrange("p (r t) -> p r t", r=4)
    hp_sb = C.arena_b[:, 2048:4096].rearrange("p (r t) -> p r t", r=4)

    p.add("g", (lambda a2_sb=a2_sb: lambda e: e.dma_start(
        out=a2_sb, in_=C.t_A2.rearrange("(c p) r -> p c r", p=128)))(),
        wait=[("pe", p.val("pe"))], inc=[("dg", 16)])
    p.add("g", (lambda b2_sb=b2_sb: lambda e: e.dma_start(
        out=b2_sb, in_=C.t_B2.rearrange("(c p) o -> p c o", p=128)))(),
        inc=[("dg", 16)])
    p.add("g", lambda e: e.dma_start(out=C.esel_sb[0:8, :], in_=C.t_esel),
          inc=[("dg", 16)])
    dgA = p.val("dg")

    for tc in range(NTC):
        for rt in range(4):
            w = [("se", p.val("se"))]
            if tc == 0 and rt == 0:
                w += [("se", C.gates_se), ("dg", dgA)]
            mm(C, C.psum[4][:, 0:TC], C.esel_sb[0:8, ts(rt)],
               C.gT[0:8, ts(tc, TC)], wait=w, inc=[("pe", 1)])
            p.add("s", (lambda rt=rt, gb_sb=gb_sb: lambda e: e.copy(
                gb_sb[:, rt, 0:TC], C.psum[4][:, 0:TC]))(),
                wait=[("pe", p.val("pe"))])
        se_gb = p.val("se")

        for rt in range(4):
            hb = C.psum[rt % 2]
            for c in range(NCH):
                w = []
                if rt == 0 and c == 0:
                    w = [("ve", C.ln_ready_ve)]
                if rt >= 2 and c == 0:
                    w = [("ve", p.val("ve"))]
                mm(C, hb[:, 0:TC], a2_sb[:, c, ts(rt)],
                   C.ln[:, c, ts(tc, TC)], start=(c == 0), stop=(c == NCH - 1),
                   wait=w, inc=([("pe", 1)] if c == NCH - 1 else ()))
            p.add("v", (lambda rt=rt, hb=hb, hp_sb=hp_sb, gb_sb=gb_sb:
                        lambda e: e.tensor_tensor(
                            hp_sb[:, rt, 0:TC], hb[:, 0:TC],
                            gb_sb[:, rt, 0:TC], ALU.mult))(),
                  wait=[("pe", p.val("pe")), ("se", se_gb)])
        ve_hp = p.val("ve")

        for ot in range(NCH):
            yb = C.psum[2 + (ot % 2)]
            for rt in range(4):
                w = []
                if ot == 0 and rt == 0:
                    w = [("ve", ve_hp)]
                if ot >= 2 and rt == 0:
                    w = [("ve", p.val("ve"))]
                mm(C, yb[:, 0:TC], b2_sb[:, rt, ts(ot)],
                   hp_sb[:, rt, 0:TC], start=(rt == 0), stop=(rt == 3),
                   wait=w, inc=([("pe", 1)] if rt == 3 else ()))
            p.add("v", (lambda ot=ot, tc=tc, yb=yb: lambda e: e.tensor_add(
                C.x[:, ot, ts(tc, TC)], C.x[:, ot, ts(tc, TC)], yb[:, 0:TC]))(),
                wait=[("pe", p.val("pe"))])
    C.x_ready_ve = p.val("ve")


def emit_attn0(C):
    p = C.plan
    qp_sb = C.arena_a.rearrange("p (c t) -> p c t", c=NCH)
    kp_sb = C.arena_b.rearrange("p (c t) -> p c t", c=NCH)
    v_ch = C.sq_bf
    kv_sb = C.arena_f[:, 520:2632].rearrange("p (h f) -> p h f", h=16)
    zb_sb = C.arena_f[:, 2632:3144]
    wsl = [C.wslab[:, ts(i, 2048)].rearrange("p (c m) -> p c m", c=NCH)
           for i in range(2)]
    kp_tm = C.arena_f[:, 0:256].bitcast(BF16).rearrange(
        "p (i m) -> p i m", i=4)
    v_tm = C.arena_f[:, 256:520].bitcast(BF16).rearrange(
        "p (i m) -> p i m", i=4)

    p.add("v", (lambda kv_sb=kv_sb: lambda e: e.memset(kv_sb, 0.0))(),
          wait=[("dg", C.plan.val("dg"))])
    ve_kv0 = p.val("ve")

    bank_rel = {}

    pb4 = C.psum[4].bitcast(BF16)
    pb5 = C.psum[5].bitcast(BF16)

    def proj(kind, wtensor, bias_sb):
        for ot in range(NCH):
            buf = ot % 2
            p.add("g", (lambda ot=ot, buf=buf, wtensor=wtensor: lambda e:
                        e.dma_start(out=wsl[buf],
                                    in_=wtensor[ot].rearrange(
                                        "(c p) m -> p c m", p=128)))(),
                  wait=[("pe", C.wslab_rel.get(buf, 0))], inc=[("dg", 16)])
            dg_sl = p.val("dg")
            for tc in range(NTC):
                bi = (ot % 2) * 2 + tc
                bk = C.psum[bi]
                for c in range(NCH):
                    w = []
                    if c == 0:
                        w = [("dg", dg_sl),
                             ("ve", bank_rel.get(bi, 0)),
                             ("se", C.sbank_rel.get(bi, 0))]
                        if kind == "q" and ot == 0 and tc == 0:
                            w.append(("ve", C.ln_ready_ve))
                    mm(C, bk[:, 0:TC], wsl[buf][:, c, :],
                       C.ln[:, c, ts(tc, TC)], start=(c == 0),
                       stop=(c == NCH - 1), wait=w,
                       inc=([("pe", 1)] if c == NCH - 1 else ()))
                C.wslab_rel[buf] = p.val("pe")
                pe_pr = p.val("pe")
                if kind == "v":
                    p.add("s", (lambda ot=ot, tc=tc, bk=bk, bias_sb=bias_sb:
                                lambda e: e.activation(
                                    v_ch[:, 0:TC], bk[:, 0:TC],
                                    AF.Identity, bias=bias_sb[:, ot:ot + 1]))(),
                          wait=[("pe", pe_pr), ("pe", C.vch_pe_rel.get(0, 0))])
                    C.sbank_rel[bi] = p.val("se")
                    se_vch = p.val("se")
                    h = ot
                    for i in range(4):
                        w0 = []
                        if i == 0:
                            w0 = [("se", se_vch), ("ge", ge_ones),
                                  ("se", C.tr_rel)]
                        p.add("t", (lambda h=h, tc=tc, i=i: lambda e:
                                    e.transpose(
                                        pb4[:, ts(i)],
                                        kp_sb[:, h, tc * TC + i * 128:
                                              tc * TC + (i + 1) * 128],
                                        C.ident_bf))(),
                            wait=w0)
                        p.add("t", (lambda tc=tc, i=i: lambda e: e.transpose(
                            pb5[:, ts(i)], v_ch[:, ts(i)],
                            C.ident_bf))(),
                            inc=[("pe", 1)] if i == 3 else ())
                    pe_tr = p.val("pe")
                    C.vch_pe_rel[0] = pe_tr
                    p.add("s", (lambda: lambda e: e.copy(
                        kp_tm, pb4[:, 0:512].rearrange(
                            "p (i m) -> p i m", i=4)))(),
                        wait=[("pe", pe_tr)])
                    p.add("s", (lambda: lambda e: e.copy(
                        v_tm[:, :, 0:128],
                        pb5[:, 0:512].rearrange("p (i m) -> p i m", i=4)))())
                    C.tr_rel = p.val("se")
                    se_tm = p.val("se")
                    for i in range(4):
                        mm(C, C.psum[6][:, 0:132], kp_tm[:, i, :],
                           v_tm[:, i, 0:132],
                           start=(i == 0), stop=(i == 3),
                           wait=([("se", se_tm), ("ve", C.kvbank_rel)]
                                 if i == 0 else ()),
                           inc=([("pe", 1)] if i == 3 else ()))
                    p.add("v", (lambda h=h, kv_sb=kv_sb: lambda e:
                                e.tensor_add(kv_sb[:, h, :], kv_sb[:, h, :],
                                             C.psum[6][:, 0:132]))(),
                        wait=[("pe", p.val("pe")), ("ve", ve_kv0)])
                    C.kvbank_rel = p.val("ve")
                else:
                    dst = qp_sb if kind == "q" else kp_sb
                    p.add("s", (lambda bk=bk, bias_sb=bias_sb, ot=ot:
                                lambda e: e.activation(
                                    C.f32scr[:, 0:TC], bk[:, 0:TC], AF.Exp,
                                    bias=bias_sb[:, ot:ot + 1]))(),
                          wait=[("pe", pe_pr), ("ve", C.expscr_rel)])
                    C.sbank_rel[bi] = p.val("se")

                    def evac(bk=bk, ot=ot, tc=tc, dst=dst, bias_sb=bias_sb):
                        def f(e):
                            e.tensor_scalar(C.f32scr2[:, 0:TC], bk[:, 0:TC],
                                            bias_sb[:, ot:ot + 1], 0.0,
                                            ALU.add, ALU.max)
                            e.tensor_single_scalar(C.f32scr[:, 0:TC],
                                                   C.f32scr[:, 0:TC], 1.0,
                                                   ALU.min)
                            return e.tensor_add(dst[:, ot, ts(tc, TC)],
                                                C.f32scr[:, 0:TC],
                                                C.f32scr2[:, 0:TC])
                        return f
                    p.add("v", evac(), wait=[("se", p.val("se"))], nops=3)
                    bank_rel[bi] = p.val("ve")
                    C.expscr_rel = p.val("ve")

    C.expscr_rel = 0
    C.vch_pe_rel = {}
    C.wslab_rel = {0: p.val("pe"), 1: p.val("pe")}
    ge_ones = 0
    proj("q", C.t_qwr, C.qb_sb)
    proj("k", C.t_kwr, C.kb_sb)
    p.add("g", (lambda v_tm=v_tm: lambda e: e.memset(v_tm[:, :, 128:132], 1.0))(),
          wait=[("ve", p.val("ve"))], inc=[("ge", 1)])
    ge_ones = p.val("ge")
    proj("v", C.t_vwr, C.vb_sb)

    p.add("y", (lambda kv_sb=kv_sb: lambda e: e.dma_start(
        out=C.t_kvpart.rearrange("(h p) f -> p h f", p=128), in_=kv_sb))(),
        wait=[("ve", p.val("ve"))], inc=[("dy", 16)])
    p.add("g", lambda e: e.collective_compute(
        "AllReduce", ALU.add, replica_groups=[list(range(NCORES))],
        ins=[C.t_kvpart[:, :]], outs=[C.t_kvred[:, :]]),
        wait=[("dy", p.val("dy"))], inc=[("cc", 1)])
    p.add("g", lambda e: e.dma_start(
        out=C.kvbf_sb[:], in_=C.t_kvred.rearrange("(h p) f -> p h f", p=128)),
        wait=[("cc", p.val("cc"))], inc=[("dg", 16)])
    dg_kv = p.val("dg")

    for h in range(NCH):
        for tc in range(NTC):
            ab = C.psum[(h * NTC + tc) % 2]
            mm(C, ab[:, 0:TC], C.kvbf_sb[:, h, 0:128], qp_sb[:, h, ts(tc, TC)],
               wait=[("dg", dg_kv), ("ve", C.abank_rel.get((h * NTC + tc) % 2, 0))],
               inc=[("pe", 1)])
            mm(C, C.psum[2][0:1, 0:TC], C.kvbf_sb[:, h, 128:129],
               qp_sb[:, h, ts(tc, TC)],
               wait=[("se", C.zrow_rel)], inc=[("pe", 1)])
            pe_az = p.val("pe")
            p.add("s", (lambda: lambda e: e.activation(
                C.row_a[0:1, 0:TC], C.psum[2][0:1, 0:TC], AF.Identity,
                bias=cst1(C, "z_eps")))(),
                wait=[("pe", pe_az)])
            C.zrow_rel = p.val("se")
            p.add("v", (lambda: lambda e: e.reciprocal(
                C.row_b[0:1, 0:TC], C.row_a[0:1, 0:TC]))(),
                wait=[("se", p.val("se"))])
            mm(C, C.psum[3][:, 0:TC], C.ones1_f32, C.row_b[0:1, 0:TC],
               wait=[("ve", p.val("ve")), ("se", C.zb_rel)], inc=[("pe", 1)])
            p.add("s", (lambda zb_sb=zb_sb: lambda e: e.copy(
                zb_sb, C.psum[3][:, 0:TC]))(),
                wait=[("pe", p.val("pe"))])
            C.zb_rel = p.val("se")
            p.add("v", (lambda h=h, tc=tc, ab=ab, qp_sb=qp_sb, zb_sb=zb_sb:
                        lambda e: e.tensor_tensor(
                            qp_sb[:, h, ts(tc, TC)], ab[:, 0:TC], zb_sb,
                            ALU.mult))(),
                  wait=[("se", p.val("se"))])
            C.abank_rel[(h * NTC + tc) % 2] = p.val("ve")
    ve_attn = p.val("ve")

    for ot in range(NCH):
        buf = ot % 2
        p.add("g", (lambda ot=ot, buf=buf: lambda e: e.dma_start(
            out=wsl[buf], in_=C.t_owr[ot].rearrange("(c p) m -> p c m", p=128)))(),
            wait=[("pe", C.wslab_rel.get(buf, 0))], inc=[("dg", 16)])
        dg_sl = p.val("dg")
        for tc in range(NTC):
            bk = C.psum[(ot % 2) * 2 + tc]
            for c in range(NCH):
                w = []
                if c == 0:
                    w = [("dg", dg_sl), ("ve", ve_attn),
                         ("ve", C.obank_rel.get((ot % 2) * 2 + tc, 0))]
                mm(C, bk[:, 0:TC], wsl[buf][:, c, :], qp_sb[:, c, ts(tc, TC)],
                   start=(c == 0), stop=(c == NCH - 1), wait=w,
                   inc=([("pe", 1)] if c == NCH - 1 else ()))
            C.wslab_rel[buf] = p.val("pe")
            p.add("v", (lambda ot=ot, tc=tc, bk=bk: lambda e:
                        e.scalar_tensor_tensor(
                            C.x[:, ot, ts(tc, TC)], bk[:, 0:TC],
                            C.ob_sb[:, ot:ot + 1], C.x[:, ot, ts(tc, TC)],
                            ALU.add, ALU.add))(),
                  wait=[("pe", p.val("pe"))])
            C.obank_rel[(ot % 2) * 2 + tc] = p.val("ve")
    C.x_ready_ve = p.val("ve")


FFN1_PARTS = "both"
DEBUG_DUMP = False


def emit_ffn1(C):
    p = C.plan
    emit_gates(C, "l1")

    act_e = C.arena_a[:, 0:11264].rearrange("p (j t) -> p j t", j=11)
    gb_sb = C.arena_a[:, 11264:12288]
    act_sh = C.arena_b[:, 0:11264].rearrange("p (j t) -> p j t", j=11 * 2)
    sl = [C.wslab[:, ts(i, 2048)].rearrange("p (c m) -> p c m", c=NCH)
          for i in range(4)]
    sl2 = [C.wslab[:, ts(i, 2816)].rearrange("p (c m) -> p c m", c=22)
           for i in range(2)]
    sl3 = [C.wslab[:, i * 1408:(i + 1) * 1408].rearrange(
        "p (c m) -> p c m", c=11) for i in range(2)]

    relA = {}
    relS = {}
    gates_se = C.gates_se
    p.add("g", lambda e: e.dma_start(out=C.oneh8_sb, in_=C.t_oneh8),
          wait=[("pe", p.val("pe"))], inc=[("dg", 16)])
    C.oneh8_dg = p.val("dg")
    pe_now = p.val("pe")
    C.slrel = {i: pe_now for i in range(4)}
    C.sl2rel = {i: pe_now for i in range(2)}
    C.sl3rel = {i: pe_now for i in range(2)}

    def slab_dma(dst, src_ap, buf_rel):
        p.add("g", (lambda dst=dst, src_ap=src_ap: lambda e: e.dma_start(
            out=dst, in_=src_ap.rearrange("(c p) m -> p c m", p=128)))(),
            wait=[("pe", buf_rel)], inc=[("dg", 16)])
        return p.val("dg")

    C.actsh_pe_rel = 0
    C.acte_pe_rel = 0
    for tc in (range(NTC) if FFN1_PARTS in ("both", "shared") else []):
        fence_h = p.val("pe")
        actsh_fence = C.actsh_pe_rel
        for j in range(22):
            b1 = C.psum[(j % 2) * 2]
            b3 = C.psum[(j % 2) * 2 + 1]
            bufa, bufb = (j % 2) * 2, (j % 2) * 2 + 1
            dg1 = slab_dma(sl[bufa], C.t_sw1r[j],
                           max(C.slrel.get(bufa, 0), fence_h))
            dg3 = slab_dma(sl[bufb], C.t_sw3r[j],
                           max(C.slrel.get(bufb, 0), fence_h))
            for c in range(NCH):
                w = []
                if c == 0:
                    w = [("dg", dg1), ("ve", relA.get((j % 2) * 2, 0)),
                         ("se", relS.get((j % 2) * 2, 0))]
                    if tc == 0 and j == 0:
                        w.append(("ve", C.ln_ready_ve))
                mm(C, b1[:, 0:TC], sl[bufa][:, c, :], C.ln[:, c, ts(tc, TC)],
                   start=(c == 0), stop=(c == NCH - 1), wait=w,
                   inc=([("pe", 1)] if c == NCH - 1 else ()))
            C.slrel[bufa] = p.val("pe")
            pe_h1 = p.val("pe")
            for c in range(NCH):
                w = []
                if c == 0:
                    w = [("dg", dg3), ("ve", relA.get((j % 2) * 2 + 1, 0))]
                mm(C, b3[:, 0:TC], sl[bufb][:, c, :], C.ln[:, c, ts(tc, TC)],
                   start=(c == 0), stop=(c == NCH - 1), wait=w,
                   inc=([("pe", 1)] if c == NCH - 1 else ()))
            C.slrel[bufb] = p.val("pe")
            pe_h3 = p.val("pe")
            p.add("s", (lambda b1=b1: lambda e: e.activation(
                C.f32scr2, b1[:, 0:TC], AF.Silu))(),
                wait=[("pe", pe_h1), ("ve", C.silu_rel)])
            relS[(j % 2) * 2] = p.val("se")
            p.add("v", (lambda j=j, tc=tc, b3=b3: lambda e:
                        e.scalar_tensor_tensor(
                            act_sh[:, j, 0:TC],
                            b3[:, 0:TC], 0.5, C.f32scr2,
                            ALU.mult, ALU.mult))(),
                wait=[("se", p.val("se")), ("pe", pe_h3),
                      ("pe", actsh_fence)])
            relA[(j % 2) * 2] = p.val("ve")
            relA[(j % 2) * 2 + 1] = p.val("ve")
            C.silu_rel = p.val("ve")
        ve_acts = p.val("ve")
        fence_sw2 = p.val("pe")
        for ot in range(NCH):
            buf = ot % 2
            dg2 = slab_dma(sl2[buf], C.t_sw2r[ot],
                           max(C.sl2rel.get(buf, 0), fence_sw2))
            bk = C.psum[5 + (ot % 2)]
            for jj in range(22):
                w = []
                if jj == 0:
                    w = [("dg", dg2), ("ve", ve_acts),
                         ("ve", relA.get(5 + (ot % 2), 0))]
                mm(C, bk[:, 0:TC], sl2[buf][:, jj, :], act_sh[:, jj, 0:TC],
                   start=(jj == 0), stop=(jj == 21), wait=w,
                   inc=([("pe", 1)] if jj == 21 else ()))
            C.sl2rel[buf] = p.val("pe")
            p.add("v", (lambda ot=ot, tc=tc, bk=bk: lambda e: e.tensor_add(
                C.x[:, ot, ts(tc, TC)], C.x[:, ot, ts(tc, TC)], bk[:, 0:TC]))(),
                wait=[("pe", p.val("pe"))])
            relA[5 + (ot % 2)] = p.val("ve")
        C.actsh_pe_rel = p.val("pe")

    if DEBUG_DUMP:
        C.t_gtdump = C.nc.dram_tensor("gT_dump", [8, TOK], BF16,
                                      kind="ExternalOutput").ap()
        p.add("y", lambda e: e.dma_start(out=C.t_gtdump, in_=C.gT[0:8, :]),
              wait=[("se", C.gates_se)], inc=[("dy", 16)])
        C.t_dbgdump = C.nc.dram_tensor("dbg_dump", [128, 64], F32,
                                       kind="ExternalOutput").ap()
        p.add("y", lambda e: e.dma_start(out=C.t_dbgdump, in_=C.dbg_sb),
              wait=[("se", C.gates_se)], inc=[("dy", 16)])

    for ex in (range(8) if FFN1_PARTS in ("both", "experts") else []):
        for tc in range(NTC):
            mm(C, C.psum[4][:, 0:TC], C.oneh8_sb[0:8, ts(ex)],
               C.gT[0:8, ts(tc, TC)],
               wait=[("se", p.val("se")), ("se", gates_se),
                     ("dg", C.oneh8_dg)], inc=[("pe", 1)])
            p.add("s", (lambda ex=ex, tc=tc: lambda e: e.copy(
                gb_sb[:, ts(tc, TC)], C.psum[4][:, 0:TC]))(),
                wait=[("pe", p.val("pe")), ("ve", C.gb_rel)])
        se_gb = p.val("se")
        fence_h = p.val("pe")
        acte_fence = C.acte_pe_rel
        for j in range(11):
            bufa, bufb = (j % 2) * 2, (j % 2) * 2 + 1
            dg1 = slab_dma(sl[bufa], C.t_w1r[ex, j],
                           max(C.slrel.get(bufa, 0), fence_h))
            dg3 = slab_dma(sl[bufb], C.t_w3r[ex, j],
                           max(C.slrel.get(bufb, 0), fence_h))
            for tc in range(NTC):
                b1 = C.psum[tc]
                for c in range(NCH):
                    w = []
                    if c == 0:
                        w = [("dg", dg1), ("ve", relA.get(tc, 0)),
                             ("se", relS.get(tc, 0))]
                    mm(C, b1[:, 0:TC], sl[bufa][:, c, :],
                       C.ln[:, c, ts(tc, TC)], start=(c == 0),
                       stop=(c == NCH - 1), wait=w,
                       inc=([("pe", 1)] if c == NCH - 1 else ()))
            C.slrel[bufa] = p.val("pe")
            for tc in range(NTC):
                b3 = C.psum[2 + tc]
                for c in range(NCH):
                    w = []
                    if c == 0:
                        w = [("dg", dg3), ("ve", relA.get(2 + tc, 0))]
                    mm(C, b3[:, 0:TC], sl[bufb][:, c, :],
                       C.ln[:, c, ts(tc, TC)], start=(c == 0),
                       stop=(c == NCH - 1), wait=w,
                       inc=([("pe", 1)] if c == NCH - 1 else ()))
            C.slrel[bufb] = p.val("pe")
            pe_h = p.val("pe")
            for tc in range(NTC):
                p.add("s", (lambda tc=tc: lambda e: e.activation(
                    C.f32scr2, C.psum[tc][:, 0:TC], AF.Silu))(),
                    wait=[("pe", pe_h), ("ve", C.silu_rel)])
                relS[tc] = p.val("se")

                def actfn(j=j, tc=tc):
                    def f(e):
                        e.tensor_tensor(C.f32scr, C.f32scr2,
                                        C.psum[2 + tc][:, 0:TC], ALU.mult)
                        return e.tensor_tensor(
                            act_e[:, j, ts(tc, TC)], C.f32scr,
                            gb_sb[:, ts(tc, TC)], ALU.mult)
                    return f
                p.add("v", actfn(), wait=[("se", p.val("se")), ("se", se_gb),
                                          ("pe", acte_fence)], nops=2)
                relA[tc] = p.val("ve")
                relA[2 + tc] = p.val("ve")
                C.silu_rel = p.val("ve")
        ve_acts = p.val("ve")
        C.gb_rel = ve_acts
        fence_w2 = p.val("pe")
        for ot in range(NCH):
            buf = ot % 2
            dg2 = slab_dma(sl3[buf], C.t_w2r[ex, ot],
                           max(C.sl3rel.get(buf, 0), fence_w2))
            for tc in range(NTC):
                bk = C.psum[5 + (ot % 2)]
                for jj in range(11):
                    w = []
                    if jj == 0:
                        w = [("dg", dg2), ("ve", ve_acts),
                             ("ve", relA.get(5 + (ot % 2), 0))]
                    mm(C, bk[:, 0:TC], sl3[buf][:, jj, :],
                       act_e[:, jj, ts(tc, TC)],
                       start=(jj == 0), stop=(jj == 10), wait=w,
                       inc=([("pe", 1)] if jj == 10 else ()))
                p.add("v", (lambda ot=ot, tc=tc, bk=bk: lambda e:
                            e.tensor_add(C.x[:, ot, ts(tc, TC)],
                                         C.x[:, ot, ts(tc, TC)],
                                         bk[:, 0:TC]))(),
                    wait=[("pe", p.val("pe"))])
                relA[5 + (ot % 2)] = p.val("ve")
            C.sl3rel[buf] = p.val("pe")
        C.acte_pe_rel = p.val("pe")
    C.x_ready_ve = p.val("ve")



def emit_attn1(C):
    p = C.plan
    ln_flat = C.ln.rearrange("p c t -> p (c t)")
    K_sb = C.arena_a.rearrange("p (h t) -> p h t", h=2)
    V_sb = C.arena_b.rearrange("p (h t) -> p h t", h=2)
    qa_bf = C.arena_a[:, 0:6144].rearrange("p (o t) -> p o t", o=6)
    ckv_raw = C.arena_a[:, 6144:8192].rearrange("p (o t) -> p o t", o=4)
    cos1_sb = C.arena_b[0:64, 0:2048].bitcast(F32)
    sin1_sb = C.arena_b[0:64, 2048:4096].bitcast(F32)
    wsl = [C.wslab[:, ts(i, 2048)].rearrange("p (c m) -> p c m", c=NCH)
           for i in range(2)]
    cos2_sb = C.wslab[:, 4096:5120].bitcast(F32)
    sin2_sb = C.wslab[:, 5120:6144].bitcast(F32)
    qan_sb = ln_flat[:, 0:6144].rearrange("p (o t) -> p o t", o=6)
    ckv_sb = ln_flat[:, 0:4096].rearrange("p (o t) -> p o t", o=4)
    att_sb = ln_flat[:, 0:16384].rearrange("p (c t) -> p c t", c=NCH)
    nrm = [C.nrm_bf[:, ts(i, TC)] for i in range(2)]
    kr_bf = C.kr_bf
    qro_bf = C.nrm_bf[:, 0:TC]
    pe0 = p.val("pe")
    C.wslab_rel = {0: pe0, 1: pe0}
    C.cos2_rel = pe0
    C.attn1_sb_rel = {}
    p.add("y", lambda e: e.dma_start(out=cos1_sb, in_=C.t_cos1),
          wait=[("pe", pe0), ("ve", C.x_ready_ve)], inc=[("dy", 16)])
    p.add("y", lambda e: e.dma_start(out=sin1_sb, in_=C.t_sin1),
          inc=[("dy", 16)])
    C.cs1_dy = p.val("dy")

    def rms_block(n_tiles, wdram, gam, inv_name, dst_row0, raw_sb, tc,
                  stat_slice):
        for o in range(n_tiles):
            buf = o % 2
            p.add("g", (lambda o=o, buf=buf, wdram=wdram: lambda e:
                        e.dma_start(out=wsl[buf], in_=wdram[o].rearrange(
                            "(c p) m -> p c m", p=128)))(),
                  wait=[("pe", C.wslab_rel.get(buf, 0))], inc=[("dg", 16)])
            dg_sl = p.val("dg")
            bk = C.psum[o % 2]
            for c in range(NCH):
                w = []
                if c == 0:
                    w = [("dg", dg_sl), ("se", C.attn1_sb_rel.get(o % 2, 0))]
                    if o < 2:
                        w.append(("ve", C.ln_ready_ve))
                mm(C, bk[:, 0:TC], wsl[buf][:, c, :], C.ln[:, c, ts(tc, TC)],
                   start=(c == 0), stop=(c == NCH - 1), wait=w,
                   inc=([("pe", 1)] if c == NCH - 1 else ()))
            C.wslab_rel[buf] = p.val("pe")
            pe_pr = p.val("pe")
            p.add("s", (lambda o=o, bk=bk, raw_sb=raw_sb: lambda e: e.copy(
                raw_sb[:, o, 0:TC], bk[:, 0:TC]))(), wait=[("pe", pe_pr)])
            p.add("s", (lambda o=o, raw_sb=raw_sb: lambda e: e.activation(
                C.sq_bf[:, 0:TC], raw_sb[:, o, 0:TC], AF.Square))(),
                wait=[("pe", C.sqmm_rel)])
            se_sq = p.val("se")
            C.attn1_sb_rel[o % 2] = se_sq
            mm(C, C.psum[7][stat_slice, 0:TC], C.ones_bf, C.sq_bf[:, 0:TC],
               start=(o == 0), stop=(o == n_tiles - 1),
               wait=[("se", se_sq)] + ([("ve", C.stat7_rel)] if o == 0 else []),
               inc=[("pe", 1)])
            C.sqmm_rel = p.val("pe")
        pe_stats = p.val("pe")
        p.add("s", (lambda inv_name=inv_name, stat_slice=stat_slice:
                    lambda e: e.activation(
                        C.row_a[0:1, 0:TC], C.psum[7][stat_slice, 0:TC],
                        AF.Sqrt, bias=cst1(C, "rms_eps"),
                        scale=cst1(C, inv_name)))(),
            wait=[("pe", pe_stats)])
        p.add("v", (lambda: lambda e: e.reciprocal(
            C.row_b[0:1, 0:TC], C.row_a[0:1, 0:TC]))(),
            wait=[("se", p.val("se"))])
        C.stat7_rel = p.val("ve")
        mm(C, C.psum[5][:, 0:TC], C.ones1_f32, C.row_b[0:1, 0:TC],
           wait=[("ve", p.val("ve")), ("se", C.b5_rel)], inc=[("pe", 1)])
        p.add("s", (lambda: lambda e: e.copy(C.f32scr[:, 0:TC],
                                             C.psum[5][:, 0:TC]))(),
            wait=[("pe", p.val("pe"))])
        C.b5_rel = p.val("se")
        se_rb = p.val("se")
        for o in range(n_tiles):
            nb = nrm[o % 2]

            def normf(o=o, nb=nb, raw_sb=raw_sb, gam=gam):
                def f(e):
                    e.tensor_tensor(C.f32scr2[:, 0:TC], raw_sb[:, o, 0:TC],
                                    C.f32scr[:, 0:TC], ALU.mult)
                    return e.tensor_scalar_mul(nb, C.f32scr2[:, 0:TC],
                                               gam[:, o:o + 1])
                return f
            p.add("v", normf(), wait=[("se", se_rb),
                                      ("dy", C.nrm_dy_rel.get(o % 2, 0))],
                  nops=2)
            ve_n = p.val("ve")
            p.add("y", (lambda o=o, nb=nb, dst_row0=dst_row0, tc=tc:
                        lambda e: e.dma_start(
                            out=C.t_contrib[dst_row0 + o * 128:
                                            dst_row0 + (o + 1) * 128,
                                            ts(tc, TC)],
                            in_=nb))(),
                  wait=[("ve", ve_n)], inc=[("dy", 16)])
            C.nrm_dy_rel[o % 2] = p.val("dy")

    C.sqmm_rel = 0
    C.stat7_rel = 0
    C.b5_rel = 0
    C.nrm_dy_rel = {}
    for tc in range(NTC):
        rms_block(6, C.t_qar, C.qgam_sb, "inv768", 0, qa_bf, tc,
                  slice(0, 1))
        rms_block(4, C.t_kvar, C.kgam_sb, "inv512", 768, ckv_raw, tc,
                  slice(64, 65))
        p.add("g", (lambda: lambda e: e.dma_start(
            out=wsl[0], in_=C.t_kvar[4].rearrange("(c p) m -> p c m", p=128)))(),
            wait=[("pe", C.wslab_rel.get(0, 0))], inc=[("dg", 16)])
        dg_sl = p.val("dg")
        bk = C.psum[0]
        for c in range(NCH):
            w = [("dg", dg_sl), ("se", C.attn1_sb_rel.get(0, 0))] if c == 0 else []
            mm(C, bk[:, 0:TC], wsl[0][:, c, :], C.ln[:, c, ts(tc, TC)],
               start=(c == 0), stop=(c == NCH - 1), wait=w,
               inc=([("pe", 1)] if c == NCH - 1 else ()))
        C.wslab_rel[0] = p.val("pe")
        pe_kr = p.val("pe")
        p.add("s", (lambda bk=bk: lambda e: e.copy(
            kr_bf[0:64, 0:TC], bk[0:64, 0:TC]))(),
            wait=[("pe", pe_kr), ("dy", C.kro_dy_rel)])
        C.attn1_sb_rel[0] = p.val("se")
        se_kr = p.val("se")
        mm(C, C.psum[6][0:64, 0:TC], C.perm1_sb, kr_bf[0:64, 0:TC],
           wait=[("se", se_kr), ("ve", C.rot_rel)], inc=[("pe", 1)])
        pe_rot = p.val("pe")

        def ropef(tc=tc):
            def f(e):
                e.tensor_tensor(C.f32scr[0:64, 0:TC], kr_bf[0:64, 0:TC],
                                cos1_sb[0:64, ts(tc, TC)], ALU.mult)
                e.tensor_tensor(C.f32scr2[0:64, 0:TC], C.psum[6][0:64, 0:TC],
                                sin1_sb[0:64, ts(tc, TC)], ALU.mult)
                return e.tensor_add(kr_bf[0:64, 0:TC], C.f32scr[0:64, 0:TC],
                                    C.f32scr2[0:64, 0:TC])
            return f
        p.add("v", ropef(), wait=[("pe", pe_rot), ("dy", C.kro_dy_rel),
                                  ("dy", C.cs1_dy)], nops=3)
        C.rot_rel = p.val("ve")
        p.add("y", (lambda tc=tc: lambda e: e.dma_start(
            out=C.t_contrib[1280:1344, ts(tc, TC)], in_=kr_bf[0:64, 0:TC]))(),
            wait=[("ve", p.val("ve"))], inc=[("dy", 16)])
        C.kro_dy_rel = p.val("dy")

    p.add("g", lambda e: e.collective_compute(
        "AllGather", ALU.bypass, replica_groups=[list(range(NCORES))],
        ins=[C.t_contrib[:, :]], outs=[C.t_gath[:, :]]),
        wait=[("dy", p.val("dy"))], inc=[("cc", 1)])
    cc_gath = p.val("cc")

    wslab_free = max(C.wslab_rel.get(0, 0), C.wslab_rel.get(1, 0))
    p.add("g", (lambda: lambda e: e.dma_start(
        out=C.kvb_sb, in_=C.t_kvbr.rearrange("o (c p) m -> p o c m", p=128)))(),
        wait=[("pe", wslab_free)], inc=[("dg", 16)])
    p.add("g", (lambda: lambda e: e.dma_start(
        out=C.qbr_sb, in_=C.t_qbr.rearrange("o (c p) m -> p o c m", p=128)))(),
        inc=[("dg", 16)])
    p.add("g", lambda e: e.dma_start(out=C.maskq_sb, in_=C.t_maskq),
          wait=[("cc", cc_gath)], inc=[("dg", 16)])
    dg_w = p.val("dg")
    p.add("v", lambda e: e.memset(C.kra_sb[64:65, :], 1.0),
          wait=[("cc", cc_gath)])
    ve_kra1 = p.val("ve")

    pb4 = C.psum[4].bitcast(BF16)
    C.ckv_pe_rel = 0
    for r in range(NCORES):
        p.add("y", (lambda r=r: lambda e: e.dma_start(
            out=ckv_sb, in_=C.t_gath[r * 1344 + 768:r * 1344 + 1280, :]
            .rearrange("(c p) t -> p c t", p=128)))(),
            wait=[("cc", cc_gath), ("pe", C.ckv_pe_rel), ("ve", ve_kra1)],
            inc=[("dy", 16)])
        p.add("y", (lambda r=r: lambda e: e.dma_start(
            out=C.kra_sb[0:64, r * TOK:(r + 1) * TOK],
            in_=C.t_gath[r * 1344 + 1280:r * 1344 + 1344, :]))(),
            wait=[("cc", cc_gath)], inc=[("dy", 16)])
        dy_ck = p.val("dy")
        for o4 in range(4):
            h, is_v = o4 // 2, o4 % 2
            for tc2 in range(NTC):
                bk = C.psum[o4 % 2]
                for c4 in range(4):
                    w = []
                    if c4 == 0:
                        w = [("dy", dy_ck), ("dg", dg_w),
                             ("se", C.attn1_sb_rel.get(o4 % 2, 0))]
                    mm(C, bk[:, 0:TC], C.kvb_sb[:, o4, c4, :],
                       ckv_sb[:, c4, ts(tc2, TC)],
                       start=(c4 == 0), stop=(c4 == 3), wait=w,
                       inc=([("pe", 1)] if c4 == 3 else ()))
                pe_up = p.val("pe")
                col0 = r * TOK + tc2 * TC
                if not is_v:
                    p.add("s", (lambda h=h, col0=col0, bk=bk: lambda e:
                                e.copy(K_sb[:, h, col0:col0 + TC],
                                       bk[:, 0:TC]))(),
                          wait=[("pe", pe_up)])
                    C.attn1_sb_rel[o4 % 2] = p.val("se")
                else:
                    p.add("s", (lambda bk=bk: lambda e: e.copy(
                        C.sq_bf[:, 0:TC], bk[:, 0:TC]))(),
                          wait=[("pe", pe_up), ("pe", C.vtr_rel)])
                    C.attn1_sb_rel[o4 % 2] = p.val("se")
                    se_v = p.val("se")
                    for i in range(4):
                        p.add("t", (lambda i=i: lambda e: e.transpose(
                            pb4[:, ts(i)], C.sq_bf[:, ts(i)], C.ident_bf))(),
                            wait=([("se", se_v), ("se", C.vtr_se_rel)]
                                  if i == 0 else ()),
                            inc=([("pe", 1)] if i == 3 else ()))
                    C.vtr_rel = p.val("pe")
                    p.add("s", (lambda h=h, col0=col0: lambda e: e.copy(
                        V_sb[:, h, col0:col0 + TC],
                        pb4[:, 0:512]))(), wait=[("pe", C.vtr_rel)])
                    C.vtr_se_rel = p.val("se")
        C.ckv_pe_rel = p.val("pe")

    maskq = C.maskq_sb.rearrange("p (r t) -> p r t", r=4)
    PTv = [C.pt_bf[:, ts(i, TC)] for i in range(2)]
    last_flash_pe = p.val("pe")
    for qt in range(16):
        r, tc2 = qt // 2, qt % 2
        if tc2 == 0:
            p.add("y", (lambda r=r: lambda e: e.dma_start(
                out=qan_sb, in_=C.t_gath[r * 1344:r * 1344 + 768, :]
                .rearrange("(c p) t -> p c t", p=128)))(),
                wait=[("cc", cc_gath), ("pe", p.val("pe"))], inc=[("dy", 16)])
        dy_qan = p.val("dy")
        p.add("y", (lambda qt=qt: lambda e: e.dma_start(
            out=cos2_sb, in_=C.t_cos2[:, ts(qt, TC)]))(),
            wait=[("pe", C.cos2_rel)], inc=[("dy", 16)])
        p.add("y", (lambda qt=qt: lambda e: e.dma_start(
            out=sin2_sb, in_=C.t_sin2[:, ts(qt, TC)]))(),
            inc=[("dy", 16)])
        dy_cs = p.val("dy")
        for o3 in range(3):
            bk = C.psum[6]
            for c6 in range(6):
                w = []
                if c6 == 0:
                    w = [("dy", dy_qan), ("dg", dg_w),
                         ("se", C.qup_se_rel)]
                mm(C, bk[:, 0:TC], C.qbr_sb[:, o3, c6, :],
                   qan_sb[:, c6, ts(tc2, TC)],
                   start=(c6 == 0), stop=(c6 == 5), wait=w,
                   inc=([("pe", 1)] if c6 == 5 else ()))
            pe_q = p.val("pe")
            if o3 < 2:
                p.add("s", (lambda o3=o3, bk=bk: lambda e: e.activation(
                    C.qtn_sb[:, o3, :], bk[:, 0:TC], AF.Identity,
                    scale=cstcol(C, "qscale")))(),
                    wait=[("pe", pe_q), ("pe", last_flash_pe)])
                C.qup_se_rel = p.val("se")
            else:
                p.add("s", (lambda bk=bk: lambda e: e.activation(
                    C.qr_bf[:, 0:TC], bk[:, 0:TC], AF.Identity,
                    scale=cstcol(C, "qscale")))(),
                    wait=[("pe", pe_q), ("ve", C.qro_ve_rel)])
                C.qup_se_rel = p.val("se")
                se_qr = p.val("se")
                mm(C, C.psum[7][:, 0:TC], C.perm2_sb, C.qr_bf[:, 0:TC],
                   wait=[("se", se_qr), ("ve", C.qro_ve_rel)],
                   inc=[("pe", 1)])
                pe_rot = p.val("pe")

                def qropef():
                    def f(e):
                        e.tensor_tensor(C.f32scr[:, 0:TC], C.qr_bf[:, 0:TC],
                                        cos2_sb, ALU.mult)
                        e.tensor_tensor(C.f32scr2[:, 0:TC],
                                        C.psum[7][:, 0:TC], sin2_sb, ALU.mult)
                        return e.tensor_add(qro_bf, C.f32scr[:, 0:TC],
                                            C.f32scr2[:, 0:TC])
                    return f
                p.add("v", qropef(), wait=[("pe", pe_rot), ("dy", dy_cs),
                                           ("se", C.qra_se_rel)], nops=3)
                C.qro_ve_rel = p.val("ve")
                p.add("s", (lambda: lambda e: e.copy(
                    C.qra0[0:64, 0:TC], qro_bf[0:64, 0:TC]))(),
                    wait=[("ve", p.val("ve")), ("pe", last_flash_pe)])
                p.add("s", (lambda: lambda e: e.copy(
                    C.qra1[0:64, 0:TC], qro_bf[64:128, 0:TC]))())
                C.qra_se_rel = p.val("se")
        se_q = p.val("se")

        for h in range(2):
            qra = C.qra0 if h == 0 else C.qra1
            for js in range(4):
                kcd = qt * 4 + js
                mm(C, C.psum[4][:, 0:128],
                   C.qtn_sb[:, h, ts(js)], K_sb[:, h, ts(kcd)],
                   start=True, stop=False,
                   wait=[("se", se_q), ("ve", C.m0_ve_rel)])
                mm(C, C.psum[4][:, 0:128], qra[0:64, ts(js)],
                   C.kra_sb[0:64, ts(kcd)], start=False, stop=True,
                   inc=[("pe", 1)])
                pe_qk = p.val("pe")

                def m0f(js=js):
                    def f(e):
                        e.tensor_add(C.psum[4][:, 0:128], C.psum[4][:, 0:128],
                                     C.trilneg_sb)
                        return e.reduce_max(C.colscr2[:, 0:1],
                                            C.psum[4][:, 0:128], axis=AXX)
                    return f
                p.add("v", m0f(), wait=[("pe", pe_qk)], nops=2)
                C.m0_ve_rel = p.val("ve")
                mm(C, C.psum[5][0:1, ts(js)], C.colscr2[:, 0:1], C.ident_f32,
                   wait=[("ve", C.m0_ve_rel), ("se", C.m0row_se_rel)],
                   inc=[("pe", 1)])
            pe_m0 = p.val("pe")
            p.add("s", (lambda qra=qra: lambda e: e.mul(
                qra[64:65, 0:TC], C.psum[5][0:1, 0:TC], -1.0))(),
                wait=[("pe", pe_m0)])
            C.m0row_se_rel = p.val("se")
            se_aug = p.val("se")

            nkc = qt * 4 + 4
            for kc in range(nkc):
                sb_ = C.psum[kc % 2]
                mm(C, sb_[:, 0:TC], K_sb[:, h, ts(kc)], C.qtn_sb[:, h, :],
                   start=True, stop=False,
                   wait=[("se", se_aug), ("se", C.pt_se_rel.get(kc % 2, 0))])
                mm(C, sb_[:, 0:TC], C.kra_sb[0:65, ts(kc)], qra[0:65, 0:TC],
                   start=False, stop=True, inc=[("pe", 1)])
                pe_sc = p.val("pe")
                p.add("s", (lambda kc=kc, sb_=sb_: lambda e: e.activation(
                    PTv[kc % 2], sb_[:, 0:TC], AF.Exp))(),
                    wait=[("pe", pe_sc), ("ve", C.pt_ve_rel.get(kc % 2, 0)),
                          ("pe", C.pt_pe_rel.get(kc % 2, 0))])
                C.pt_se_rel[kc % 2] = p.val("se")
                se_pt = p.val("se")
                wpv = [("se", se_pt)]
                if kc >= qt * 4:
                    p.add("v", (lambda kc=kc, qt=qt: lambda e: e.tensor_mul(
                        PTv[kc % 2], PTv[kc % 2], maskq[:, kc - qt * 4, :]))(),
                        wait=[("se", se_pt)])
                    C.pt_ve_rel[kc % 2] = p.val("ve")
                    wpv = [("ve", p.val("ve"))]
                if kc == 0:
                    wpv = wpv + [("ve", C.acc_ve_rel)]
                mm(C, C.psum[2][:, 0:TC], V_sb[:, h, ts(kc)], PTv[kc % 2],
                   start=(kc == 0), stop=(kc == nkc - 1), wait=wpv)
                mm(C, C.psum[3][0:1, 0:TC], C.ones_bf, PTv[kc % 2],
                   start=(kc == 0), stop=(kc == nkc - 1),
                   wait=([("se", C.sum_se_rel)] if kc == 0 else ()),
                   inc=[("pe", 1)])
                C.pt_pe_rel[kc % 2] = p.val("pe")
            pe_fl = p.val("pe")
            last_flash_pe = pe_fl
            p.add("s", (lambda: lambda e: e.copy(
                C.row_a[0:1, 0:TC], C.psum[3][0:1, 0:TC]))(),
                wait=[("pe", pe_fl)])
            p.add("v", (lambda: lambda e: e.reciprocal(
                C.row_b[0:1, 0:TC], C.row_a[0:1, 0:TC]))(),
                wait=[("se", p.val("se"))])
            mm(C, C.psum[3][:, 0:TC], C.ones1_f32, C.row_b[0:1, 0:TC],
               wait=[("ve", p.val("ve"))], inc=[("pe", 1)])
            p.add("s", (lambda: lambda e: e.copy(
                C.f32scr[:, 0:TC], C.psum[3][:, 0:TC]))(),
                wait=[("pe", p.val("pe"))])
            se_rb2 = p.val("se")
            C.sum_se_rel = se_rb2
            p.add("v", (lambda: lambda e: e.tensor_tensor(
                C.attno_bf[:, 0:TC], C.psum[2][:, 0:TC], C.f32scr[:, 0:TC],
                ALU.mult))(),
                wait=[("se", se_rb2), ("dy", C.atto_dy_rel)])
            ve_at = p.val("ve")
            C.acc_ve_rel = ve_at
            p.add("y", (lambda h=h, qt=qt: lambda e: e.dma_start(
                out=C.t_attc[h * 128:(h + 1) * 128, ts(qt, TC)],
                in_=C.attno_bf[:, 0:TC]))(),
                wait=[("ve", ve_at)], inc=[("dy", 16)])
            C.atto_dy_rel = p.val("dy")

    p.add("g", lambda e: e.collective_compute(
        "AllGather", ALU.bypass, replica_groups=[list(range(NCORES))],
        ins=[C.t_attc[:, :]], outs=[C.t_gath2[:, :]]),
        wait=[("dy", p.val("dy"))], inc=[("cc", 1)])
    cc_g2 = p.val("cc")
    p.add("y", lambda e: e.dma_start(out=C.coff_sb[:], in_=C.t_coff),
          inc=[("dy", 16)])
    dy_off = p.val("dy")

    def att_load(idx, dst):
        def f(e):
            with e.register(f"roff{idx}") as r:
                e.reg_load(r, C.coff_sb[0:1, idx:idx + 1])
                src = bass.AP(C.t_gath2_h, _RV(r), [[T, 128], [1, TC]])
                return e.dma_start(out=dst, in_=src)
        return f
    for tc in range(NTC):
        for c in range(NCH):
            p.add("y", att_load(c * 2 + tc, att_sb[:, c, ts(tc, TC)]),
                  wait=[("cc", cc_g2), ("dy", dy_off),
                        ("pe", p.val("pe"))] if (tc == 0 and c == 0) else
                       [("cc", cc_g2)],
                  inc=[("dy", 16)])
    dy_att = p.val("dy")

    C.wslab_rel = {0: p.val("pe"), 1: p.val("pe")}
    for ot in range(NCH):
        buf = ot % 2
        p.add("g", (lambda ot=ot, buf=buf: lambda e: e.dma_start(
            out=wsl[buf], in_=C.t_wor[ot].rearrange("(c p) m -> p c m",
                                                    p=128)))(),
            wait=[("pe", C.wslab_rel.get(buf, 0))], inc=[("dg", 16)])
        dg_sl = p.val("dg")
        for tc in range(NTC):
            bk = C.psum[(ot % 2) * 2 + tc]
            for c in range(NCH):
                w = []
                if c == 0:
                    w = [("dg", dg_sl), ("dy", dy_att),
                         ("ve", C.oproj_rel.get((ot % 2) * 2 + tc, 0))]
                mm(C, bk[:, 0:TC], wsl[buf][:, c, :], att_sb[:, c, ts(tc, TC)],
                   start=(c == 0), stop=(c == NCH - 1), wait=w,
                   inc=([("pe", 1)] if c == NCH - 1 else ()))
            C.wslab_rel[buf] = p.val("pe")
            p.add("v", (lambda ot=ot, tc=tc, bk=bk: lambda e: e.tensor_add(
                C.x[:, ot, ts(tc, TC)], C.x[:, ot, ts(tc, TC)], bk[:, 0:TC]))(),
                wait=[("pe", p.val("pe"))])
            C.oproj_rel[(ot % 2) * 2 + tc] = p.val("ve")
    C.x_ready_ve = p.val("ve")


def build_group(parts):
    nc = bass.Bass("TRN2", target_bir_lowering=False,
                   detect_race_conditions=False)
    C = Ctx()
    C.nc = nc
    C.plan = Plan()
    C.ln_marks = []
    C.wslab_rel = {}
    C.sbank_rel = {}
    C.abank_rel = {}
    C.obank_rel = {}
    C.tr_rel = 0
    C.kvbank_rel = 0
    C.zrow_rel = 0
    C.zb_rel = 0
    C.b56_se = 0
    C.gb_rel = 0
    C.silu_rel = 0
    C.pt_se_rel = {}
    C.pt_ve_rel = {}
    C.pt_pe_rel = {}
    C.sum_se_rel = 0
    C.m0_ve_rel = 0
    C.m0row_se_rel = 0
    C.qup_se_rel = 0
    C.qro_ve_rel = 0
    C.qra_se_rel = 0
    C.atto_dy_rel = 0
    C.acc_ve_rel = 0
    C.oproj_rel = {}
    C.vtr_rel = 0
    C.vtr_se_rel = 0
    C.rot_rel = 0
    C.kro_dy_rel = 0
    p = C.plan

    need = set(parts)
    C.t_xin = nc.dram_tensor("x_in", [D, TOK], F32, kind="ExternalInput").ap()
    C.t_xout = nc.dram_tensor("x_out", [D, TOK], F32, kind="ExternalOutput").ap()
    C.t_ident = nc.dram_tensor("ident", [128, 128], F32, kind="ExternalInput").ap()

    if {"ffn0a", "ffn0b"} & need:
        C.t_A2 = nc.dram_tensor("A2", [D, 512], F32, kind="ExternalInput").ap()
        C.t_B2 = nc.dram_tensor("B2", [512, D], F32, kind="ExternalInput").ap()
        C.t_wrT = nc.dram_tensor("WrT", [D, 8], F32, kind="ExternalInput").ap()
        C.t_br = nc.dram_tensor("br", [1, 8], F32, kind="ExternalInput").ap()
        C.t_esel = nc.dram_tensor("Esel", [8, 512], F32, kind="ExternalInput").ap()
    if {"ffn1a", "ffn1b"} & need:
        C.t_oneh8 = nc.dram_tensor("oneh8", [8, 1024], F32, kind="ExternalInput").ap()
        C.t_gwT = nc.dram_tensor("gwT", [D, 8], F32, kind="ExternalInput").ap()
        C.t_rb = nc.dram_tensor("rb", [1, 8], F32, kind="ExternalInput").ap()
        C.t_w1r = nc.dram_tensor("W1r", [8, 11, D, 128], F32, kind="ExternalInput").ap()
        C.t_w3r = nc.dram_tensor("W3r", [8, 11, D, 128], F32, kind="ExternalInput").ap()
        C.t_w2r = nc.dram_tensor("W2r", [8, NCH, 1408, 128], F32, kind="ExternalInput").ap()
        C.t_sw1r = nc.dram_tensor("sw1r", [22, D, 128], F32, kind="ExternalInput").ap()
        C.t_sw3r = nc.dram_tensor("sw3r", [22, D, 128], F32, kind="ExternalInput").ap()
        C.t_sw2r = nc.dram_tensor("sw2r", [NCH, 2816, 128], F32, kind="ExternalInput").ap()
    if "attn1" in need:
        C.t_qar = nc.dram_tensor("qar", [6, D, 128], F32, kind="ExternalInput").ap()
        C.t_kvar = nc.dram_tensor("kvar", [5, D, 128], F32, kind="ExternalInput").ap()
        C.t_qbr = nc.dram_tensor("qbr", [3, 768, 128], F32, kind="ExternalInput").ap()
        C.t_kvbr = nc.dram_tensor("kvbr", [4, 512, 128], F32, kind="ExternalInput").ap()
        C.t_wor = nc.dram_tensor("wor", [NCH, D, 128], F32, kind="ExternalInput").ap()
        C.t_qgam = nc.dram_tensor("qgam", [128, 6], F32, kind="ExternalInput").ap()
        C.t_kgam = nc.dram_tensor("kgam", [128, 4], F32, kind="ExternalInput").ap()
        C.t_cos1 = nc.dram_tensor("cos1", [64, TOK], F32, kind="ExternalInput").ap()
        C.t_sin1 = nc.dram_tensor("sin1", [64, TOK], F32, kind="ExternalInput").ap()
        C.t_cos2 = nc.dram_tensor("cos2", [128, T], F32, kind="ExternalInput").ap()
        C.t_sin2 = nc.dram_tensor("sin2", [128, T], F32, kind="ExternalInput").ap()
        C.t_perm1 = nc.dram_tensor("perm1", [64, 64], F32, kind="ExternalInput").ap()
        C.t_perm2 = nc.dram_tensor("perm2", [128, 128], F32, kind="ExternalInput").ap()
        C.t_trilneg = nc.dram_tensor("trilneg", [128, 128], F32, kind="ExternalInput").ap()
        C.t_maskq = nc.dram_tensor("maskq", [128, 4 * TC], F32, kind="ExternalInput").ap()
        C.t_coff = nc.dram_tensor("coff", [1, 32], mybir.dt.int32, kind="ExternalInput").ap()
        C.t_contrib = nc.dram_tensor("contrib", [1344, TOK], BF16).ap()
        C.t_gath = nc.dram_tensor("gath", [NCORES * 1344, TOK], BF16).ap()
        C.t_attc = nc.dram_tensor("attc", [256, T], BF16).ap()
        C.t_gath2 = nc.dram_tensor("gath2", [NCORES * 256, T], BF16).ap()
        C.t_gath2_h = C.t_gath2.tensor
    if "attn0" in need:
        C.t_qwr = nc.dram_tensor("qwr", [NCH, D, 128], F32, kind="ExternalInput").ap()
        C.t_kwr = nc.dram_tensor("kwr", [NCH, D, 128], F32, kind="ExternalInput").ap()
        C.t_vwr = nc.dram_tensor("vwr", [NCH, D, 128], F32, kind="ExternalInput").ap()
        C.t_owr = nc.dram_tensor("owr", [NCH, D, 128], F32, kind="ExternalInput").ap()
        C.t_qb = nc.dram_tensor("qb2", [128, NCH], F32, kind="ExternalInput").ap()
        C.t_kb = nc.dram_tensor("kb2", [128, NCH], F32, kind="ExternalInput").ap()
        C.t_vb = nc.dram_tensor("vb2", [128, NCH], F32, kind="ExternalInput").ap()
        C.t_ob = nc.dram_tensor("ob2", [128, NCH], F32, kind="ExternalInput").ap()
        C.t_kvpart = nc.dram_tensor("kvpart", [D, 132], F32).ap()
        C.t_kvred = nc.dram_tensor("kvred", [D, 132], F32).ap()

    ctx = ExitStack()

    def sb(name, shape, dt):
        h = ctx.enter_context(nc.sbuf_tensor(name, shape, dt))
        return h[(slice(None),) * len(shape)]

    C.x = sb("x_sb", [128, NCH, TOK], F32)
    C.ln = sb("ln_sb", [128, NCH, TOK], BF16)
    C.arena_a = sb("arena_a", [128, 16384], BF16)
    C.arena_b = sb("arena_b", [128, 16384], BF16)
    C.arena_f = sb("arena_f", [128, 3144], F32)
    C.wslab = sb("wslab", [128, 8192], BF16)
    C.sq_bf = sb("sq_bf", [128, TC], BF16)
    C.f32scr = C.arena_f[:, 0:TC]
    C.f32scr2 = C.arena_f[:, 2632:3144]
    C.consts = sb("consts", [128, 16], F32)
    C.row_a = sb("row_a", [1, TC], F32)
    C.row_b = sb("row_b", [1, TC], F32)
    C.stat_m = C.arena_f[0:1, 520:1544]
    C.stat_r = C.arena_f[0:1, 1544:2568]
    C.row8 = sb("row8", [1, 8], F32)
    C.ones_f32 = sb("ones_f32", [128, 1], F32)
    C.ones_inv2048_f32 = sb("ones_i2048f", [128, 1], F32)
    C.ones_inv2048_bf = sb("ones_i2048b", [128, 1], BF16)
    C.ones1_f32 = sb("ones1_f32", [1, 128], F32)
    C.ones1_bf = sb("ones1_bf", [1, 128], BF16)
    C.ones11_f32 = sb("ones11_f32", [1, 1], F32)
    C.ident_f32 = sb("ident_f32", [128, 128], F32)
    C.ident_bf = sb("ident_bf", [128, 128], BF16)

    if {"ffn0a", "ffn0b", "ffn1a", "ffn1b"} & need:
        C.rw_sb = sb("rw_sb", [128, NCH, 8], F32)
        C.gate_wsb = sb("gate_wsb", [128, 8], F32)
        C.gate_bb = sb("gate_bb", [128, 8], F32)
        C.g8 = sb("g8", [128, 8], F32)
        C.g8b = sb("g8b", [128, 8], F32)
        C.g8c = sb("g8c", [128, 8], F32)
        C.g8d = sb("g8d", [128, 8], F32)
        C.colscr = sb("colscr", [128, 4], F32)
        if DEBUG_DUMP:
            C.dbg_sb = C.arena_b[:, 12288:12416].bitcast(F32)
        C.gT = sb("gT", [8, TOK], BF16)
        sel8 = sb("sel8", [8, 1024], BF16)
        C.esel_sb = sel8[0:8, 0:512]
        C.oneh8_sb = sel8[0:8, 0:1024]
    if "attn0" in need:
        C.qb_sb = sb("qb_sb", [128, NCH], F32)
        C.kb_sb = sb("kb_sb", [128, NCH], F32)
        C.vb_sb = sb("vb_sb", [128, NCH], F32)
        C.ob_sb = sb("ob_sb", [128, NCH], F32)
        C.kvbf_sb = C.arena_f[:, 520:1576].bitcast(BF16).rearrange(
            "p (h f) -> p h f", h=NCH)

    if "attn1" in need:
        C.kra_sb = C.ln.rearrange("p c t -> p (c t)")[0:65, 6144:14336]
        C.qtn_sb = sb("qtn_sb", [128, 2, TC], BF16)
        C.qra0 = sb("qra0", [65, TC], BF16)
        C.qra1 = sb("qra1", [65, TC], BF16)
        C.nrm_bf = C.arena_f[:, 520:1032].bitcast(BF16)
        C.pt_bf = C.arena_f[:, 1032:1544].bitcast(BF16)
        C.qr_bf = C.arena_f[:, 1544:1800].bitcast(BF16)
        C.attno_bf = C.arena_f[:, 1800:2056].bitcast(BF16)
        C.kr_bf = C.arena_f[:, 2056:2312].bitcast(BF16)
        C.kvb_sb = C.wslab[:, 6144:8192].rearrange(
            "p (o c m) -> p o c m", o=4, c=4)
        C.qbr_sb = C.wslab[:, 0:2304].rearrange(
            "p (o c m) -> p o c m", o=3, c=6)
        C.perm1_sb = sb("perm1_sb", [64, 64], BF16)
        C.perm2_sb = sb("perm2_sb", [128, 128], BF16)
        C.trilneg_sb = sb("trilneg_sb", [128, 128], F32)
        C.maskq_sb = C.ln.rearrange("p c t -> p (c t)")[:, 14336:16384]
        C.qgam_sb = sb("qgam_sb", [128, 6], F32)
        C.kgam_sb = sb("kgam_sb", [128, 4], F32)
        C.coff_sb = sb("coff_sb", [1, 32], mybir.dt.int32)
    C.colscr2 = sb("colscr2", [128, 2], F32)
    C.ones_bf = sb("ones_bf", [128, 1], BF16)

    C.psum = [ctx.enter_context(nc.psum_tensor(f"bank{i}", [128, 512], F32))[:, :]
              for i in range(8)]

    sems = {}
    for name in ("pe", "ve", "se", "ge", "dg", "dy", "cc"):
        sems[name] = ctx.enter_context(nc.semaphore(name))

    const_vals = {
        "ln_eps": 1e-5, "rms_eps": 1e-6, "z_eps": 1e-6,
        "inv768": 1.0 / 768.0, "inv512": 1.0 / 512.0,
        "qscale": float(1.0 / np.sqrt(192.0)),
    }
    C.const_idx = {k: i for i, k in enumerate(const_vals)}

    def init_fn(e):
        for k, v in const_vals.items():
            e.memset(C.consts[:, C.const_idx[k]:C.const_idx[k] + 1], v)
        e.memset(C.ones_f32[:], 1.0)
        e.memset(C.ones_inv2048_f32[:], 1.0 / 2048.0)
        e.memset(C.ones_inv2048_bf[:], 1.0 / 2048.0)
        e.memset(C.ones1_f32[:], 1.0)
        e.memset(C.ones1_bf[:], 1.0)
        e.memset(C.ones_bf[:], 1.0)
        return e.memset(C.ones11_f32[:], 1.0)
    p.add("g", init_fn, inc=[("ge", 1)])
    p.add("g", lambda e: e.dma_start(out=C.ident_f32[:], in_=C.t_ident),
          inc=[("dg", 16)])
    p.add("g", lambda e: e.dma_start(out=C.ident_bf[:], in_=C.t_ident),
          inc=[("dg", 16)])
    if "attn1" in need:
        p.add("g", lambda e: e.dma_start(out=C.perm1_sb, in_=C.t_perm1), inc=[("dg", 16)])
        p.add("g", lambda e: e.dma_start(out=C.perm2_sb, in_=C.t_perm2), inc=[("dg", 16)])
        p.add("g", lambda e: e.dma_start(out=C.trilneg_sb, in_=C.t_trilneg), inc=[("dg", 16)])
        p.add("g", lambda e: e.dma_start(out=C.qgam_sb, in_=C.t_qgam), inc=[("dg", 16)])
        p.add("g", lambda e: e.dma_start(out=C.kgam_sb, in_=C.t_kgam), inc=[("dg", 16)])
    if "attn0" in need:
        p.add("g", lambda e: e.dma_start(out=C.qb_sb, in_=C.t_qb), inc=[("dg", 16)])
        p.add("g", lambda e: e.dma_start(out=C.kb_sb, in_=C.t_kb), inc=[("dg", 16)])
        p.add("g", lambda e: e.dma_start(out=C.vb_sb, in_=C.t_vb), inc=[("dg", 16)])
        p.add("g", lambda e: e.dma_start(out=C.ob_sb, in_=C.t_ob), inc=[("dg", 16)])
    fence = [("ge", p.val("ge")), ("dg", p.val("dg"))]
    p.add("t", lambda e: e.nop(), wait=fence)
    p.add("v", lambda e: e.nop(), wait=fence, nops=0)
    p.add("s", lambda e: e.nop(), wait=fence, nops=0)
    p.add("y", lambda e: e.dma_start(out=C.x[:], in_=C.t_xin.rearrange(
        "(c p) t -> p c t", p=128)), inc=[("dy", 16)])
    C.x_ready_dy = p.val("dy")
    C.x_ready_ve = 0
    C.ln_ready_ve = 0
    C.ln_stats_ve = 0

    for part in parts:
        emit_ln(C)
        if part in ("ffn0a", "ffn0b"):
            emit_ffn0(C)
        elif part == "attn0":
            emit_attn0(C)
        elif part in ("ffn1a", "ffn1b"):
            emit_ffn1(C)
        elif part == "attn1":
            emit_attn1(C)
        else:
            raise NotImplementedError(part)

    p.add("y", lambda e: e.dma_start(out=C.t_xout.rearrange(
        "(c p) t -> p c t", p=128), in_=C.x[:]),
        wait=[("ve", C.x_ready_ve)], inc=[("dy", 16)])
    p.add("y", lambda e: e.nop(), wait=[("dy", p.val("dy"))])

    _realize(nc, p, sems)
    ctx.close()
    return nc


def _prep(inputs):
    g = {}
    x = np.asarray(inputs["L0_attn_hidden_states"], np.float32)
    g["xT"] = np.ascontiguousarray(x[0].T)
    A = np.asarray(inputs["L0_ffn_A"], np.float32)
    g["A2"] = np.ascontiguousarray(A.transpose(2, 0, 1).reshape(D, 512))
    B = np.asarray(inputs["L0_ffn_B"], np.float32)
    g["B2"] = np.ascontiguousarray(B.transpose(0, 2, 1).reshape(512, D))
    g["WrT"] = np.ascontiguousarray(np.asarray(inputs["L0_ffn_Wr"], np.float32).T)
    g["br"] = np.asarray(inputs["L0_ffn_br"], np.float32).reshape(1, 8)
    esel = np.zeros((8, 512), np.float32)
    for e in range(8):
        esel[e, e * 64:(e + 1) * 64] = 1.0
    g["Esel"] = esel
    oneh8 = np.zeros((8, 1024), np.float32)
    for e in range(8):
        oneh8[e, e * 128:(e + 1) * 128] = 1.0
    g["oneh8"] = oneh8
    g["ident"] = np.eye(128, dtype=np.float32)

    def repack(wT):
        return np.ascontiguousarray(
            wT.reshape(wT.shape[0], wT.shape[1] // 128, 128).transpose(1, 0, 2))

    for nm, key in (("qwr", "L0_attn_q_w"), ("kwr", "L0_attn_k_w"),
                    ("vwr", "L0_attn_v_w"), ("owr", "L0_attn_o_w")):
        w = np.asarray(inputs[key], np.float32)
        g[nm] = repack(np.ascontiguousarray(w.T))
    for nm, key in (("qb2", "L0_attn_q_b"), ("kb2", "L0_attn_k_b"),
                    ("vb2", "L0_attn_v_b"), ("ob2", "L0_attn_o_b")):
        b = np.asarray(inputs[key], np.float32)
        g[nm] = np.ascontiguousarray(b.reshape(NCH, 128).T)

    g["gwT"] = np.ascontiguousarray(
        np.asarray(inputs["L1_ffn_gate_w"], np.float32).T)
    g["rb"] = np.asarray(inputs["L1_ffn_router_bias"], np.float32).reshape(1, 8)
    W1 = np.asarray(inputs["L1_ffn_W1"], np.float32)
    g["W1r"] = np.ascontiguousarray(
        W1.reshape(8, D, 11, 128).transpose(0, 2, 1, 3))
    W3 = np.asarray(inputs["L1_ffn_W3"], np.float32)
    g["W3r"] = np.ascontiguousarray(
        W3.reshape(8, D, 11, 128).transpose(0, 2, 1, 3))
    W2 = np.asarray(inputs["L1_ffn_W2"], np.float32)
    g["W2r"] = np.ascontiguousarray(
        W2.reshape(8, 1408, NCH, 128).transpose(0, 2, 1, 3))
    g["sw1r"] = np.ascontiguousarray(
        np.asarray(inputs["L1_ffn_sw1"], np.float32).T.reshape(
            D, 22, 128).transpose(1, 0, 2))
    g["sw3r"] = np.ascontiguousarray(
        np.asarray(inputs["L1_ffn_sw3"], np.float32).T.reshape(
            D, 22, 128).transpose(1, 0, 2))
    g["sw2r"] = np.ascontiguousarray(
        np.asarray(inputs["L1_ffn_sw2"], np.float32).T.reshape(
            2816, NCH, 128).transpose(1, 0, 2))

    w_qa = np.asarray(inputs["L1_attn_w_qa"], np.float32)
    g["qar"] = np.ascontiguousarray(
        w_qa.reshape(D, 6, 128).transpose(1, 0, 2))
    w_kva = np.asarray(inputs["L1_attn_w_kva"], np.float32)
    kvar = np.zeros((5, D, 128), np.float32)
    kvar[:4] = w_kva[:, 0:512].reshape(D, 4, 128).transpose(1, 0, 2)
    kvar[4, :, 0:64] = w_kva[:, 512:576]
    g["kvar"] = kvar
    w_qb = np.asarray(inputs["L1_attn_w_qb"], np.float32)
    w_kvb = np.asarray(inputs["L1_attn_w_kvb"], np.float32)
    for c in range(NCORES):
        a, b = 2 * c * 192, (2 * c + 1) * 192
        qb_s = np.concatenate([w_qb[:, a:a + 128], w_qb[:, b:b + 128],
                               w_qb[:, a + 128:a + 192],
                               w_qb[:, b + 128:b + 192]], axis=1)
        g[f"qbr_{c}"] = np.ascontiguousarray(
            qb_s.reshape(768, 3, 128).transpose(1, 0, 2))
        kvb_s = w_kvb[:, 2 * c * 256:(2 * c + 2) * 256]
        g[f"kvbr_{c}"] = np.ascontiguousarray(
            kvb_s.reshape(512, 4, 128).transpose(1, 0, 2))
    w_o = np.asarray(inputs["L1_attn_w_o"], np.float32)
    g["wor"] = np.ascontiguousarray(
        w_o.reshape(D, NCH, 128).transpose(1, 0, 2))
    g["qgam"] = np.ascontiguousarray(
        np.asarray(inputs["L1_attn_q_a_norm_w"], np.float32).reshape(6, 128).T)
    g["kgam"] = np.ascontiguousarray(
        np.asarray(inputs["L1_attn_kv_a_norm_w"], np.float32).reshape(4, 128).T)
    inv_freq = 1.0 / (10000.0 ** (np.arange(0, 64, 2, dtype=np.float32) / 64))
    pos = np.arange(T, dtype=np.float32)
    emb = np.concatenate([np.outer(pos, inv_freq)] * 2, axis=1)
    cosT = np.ascontiguousarray(np.cos(emb).T.astype(np.float32))
    sinT = np.ascontiguousarray(np.sin(emb).T.astype(np.float32))
    for c in range(NCORES):
        g[f"cos1_{c}"] = np.ascontiguousarray(cosT[:, c * TOK:(c + 1) * TOK])
        g[f"sin1_{c}"] = np.ascontiguousarray(sinT[:, c * TOK:(c + 1) * TOK])
    g["cos2"] = np.ascontiguousarray(np.concatenate([cosT, cosT], axis=0))
    g["sin2"] = np.ascontiguousarray(np.concatenate([sinT, sinT], axis=0))
    P = np.zeros((64, 64), np.float32)
    for i in range(32):
        P[2 * i, 2 * i + 1] = -1.0
        P[2 * i + 1, 2 * i] = 1.0
    g["perm1"] = np.ascontiguousarray(P.T)
    P2 = np.zeros((128, 128), np.float32)
    P2[0:64, 0:64] = P.T
    P2[64:128, 64:128] = P.T
    g["perm2"] = P2
    tril = np.zeros((128, 128), np.float32)
    for q in range(128):
        tril[q, q + 1:] = -1e30
    g["trilneg"] = tril
    maskq = np.zeros((128, 4, TC), np.float32)
    for r_ in range(4):
        for k_ in range(128):
            lo = k_ + 128 * r_
            if lo < TC:
                maskq[k_, r_, lo:] = 1.0
    g["maskq"] = np.ascontiguousarray(maskq.reshape(128, 4 * TC))
    for c in range(NCORES):
        coff = np.zeros((1, 32), np.int32)
        for ch in range(NCH):
            for tc_ in range(NTC):
                coff[0, ch * 2 + tc_] = ch * 128 * T + tc_ * TC + c * TOK
        g[f"coff_{c}"] = coff
    return g


def _inputs_for(parts, g):
    m = {"ident": g["ident"]}
    if {"ffn1a", "ffn1b"} & set(parts):
        m["oneh8"] = g["oneh8"]
    if {"ffn0a", "ffn0b"} & set(parts):
        for k in ("A2", "B2", "WrT", "br", "Esel"):
            m[k] = g[k]
    if "attn0" in set(parts):
        for k in ("qwr", "kwr", "vwr", "owr", "qb2", "kb2", "vb2", "ob2"):
            m[k] = g[k]
    if {"ffn1a", "ffn1b"} & set(parts):
        for k in ("gwT", "rb", "W1r", "W3r", "W2r", "sw1r", "sw3r", "sw2r"):
            m[k] = g[k]
    return m


def _inputs_for_core(parts, g, c):
    m = {}
    if "attn1" in set(parts):
        for k in ("qar", "kvar", "wor", "qgam", "kgam", "cos2", "sin2",
                  "perm1", "perm2", "trilneg", "maskq"):
            m[k] = g[k]
        m["qbr"] = g[f"qbr_{c}"]
        m["kvbr"] = g[f"kvbr_{c}"]
        m["cos1"] = g[f"cos1_{c}"]
        m["sin1"] = g[f"sin1_{c}"]
        m["coff"] = g[f"coff_{c}"]
    return m


def kernel(**inputs):
    g = _prep(inputs)
    xT = g["xT"]
    shards = [np.ascontiguousarray(xT[:, c * TOK:(c + 1) * TOK])
              for c in range(NCORES)]

    for parts in LAUNCH_GROUPS:
        nc = build_group(parts)
        base = _inputs_for(parts, g)
        in_maps = [dict(base, x_in=shards[c], **_inputs_for_core(parts, g, c))
                   for c in range(NCORES)]
        res = run_bass_kernel_spmd(nc, in_maps, core_ids=list(range(NCORES)))
        shards = [np.ascontiguousarray(res.results[c]["x_out"])
                  for c in range(NCORES)]

    out = np.concatenate(shards, axis=1)
    return np.ascontiguousarray(out.T).reshape(1, T, D)


# version 03 (final)
# speedup vs baseline: 1.0072x; 1.0072x over previous
import numpy as np
from contextlib import ExitStack

import concourse.bass as bass
import concourse.mybir as mybir
from concourse.bass_utils import run_bass_kernel_spmd
from concourse.expressions_rust import RuntimeValue as _RV

F32 = mybir.dt.float32
BF16 = mybir.dt.bfloat16
AF = mybir.ActivationFunctionType
ALU = mybir.AluOpType
AXX = mybir.AxisListType.X

NCORES = 8
T = 8192
TOK = T // NCORES
D = 2048
NCH = D // 128
TC = 512
NTC = TOK // TC

SUBLAYERS = ("ffn0a", "attn0", "ffn0b", "ffn1a", "attn1", "ffn1b")

LAUNCH_GROUPS = [list(SUBLAYERS)]


class Plan:
    ENGS = ("t", "v", "s", "g", "y")

    def __init__(self):
        self.q = {e: [] for e in self.ENGS}
        self.cnt = {}

    def val(self, sem):
        return self.cnt.get(sem, 0)

    def add(self, eng, fn, wait=(), inc=(), nops=1):
        wait = tuple((s_, v) for (s_, v) in wait if v > 0)
        if eng == "v":
            assert not any(s_ == "ve" for s_, _ in inc)
            self.cnt["ve"] = self.cnt.get("ve", 0) + nops
        elif eng == "s":
            assert not any(s_ == "se" for s_, _ in inc)
            self.cnt["se"] = self.cnt.get("se", 0) + nops
        for s_, n in inc:
            self.cnt[s_] = self.cnt.get(s_, 0) + n
        self.q[eng].append((wait, fn, tuple(inc), nops))


class Ctx:
    pass


class _SerialEng:

    _PASS = {"wait_ge", "nop", "maybe_wait_ge", "maybe_nop_then_inc"}

    def __init__(self, eng, sem):
        self._eng = eng
        self._sem = sem
        self._n = 0

    def __getattr__(self, name):
        attr = getattr(self._eng, name)
        if name in self._PASS or not callable(attr):
            return attr

        def wrapped(*a, **k):
            if self._n > 0:
                self._eng.wait_ge(self._sem, self._n)
            inst = attr(*a, **k)
            inst.then_inc(self._sem, 1)
            self._n += 1
            return inst
        return wrapped


def _realize(nc, plan, sems):
    block_cm = nc.Block()
    block = block_cm.__enter__()

    def runner(engname):
        def run(eng):
            serial = engname in ("v", "s")
            if serial:
                eng = _SerialEng(eng, sems["ve" if engname == "v" else "se"])
            for waits, fn, incs, nops in plan.q[engname]:
                for s_, v in waits:
                    eng.wait_ge(sems[s_], v)
                before = eng._n if serial else 0
                inst = fn(eng)
                if serial:
                    assert eng._n - before == nops, (
                        f"{engname} item emitted {eng._n - before} ops, "
                        f"declared nops={nops}")
                else:
                    for s_, n in incs:
                        inst.then_inc(sems[s_], n)
        return run

    block.tensor(runner("t"))
    block.vector(runner("v"))
    block.scalar(runner("s"))
    block.gpsimd(runner("g"))
    block.sync(runner("y"))
    block_cm.__exit__(None, None, None)


def mm(C, out, lhsT, rhs, start=True, stop=True, wait=(), inc=()):
    C.plan.add("t",
               (lambda out=out, lhsT=lhsT, rhs=rhs, start=start, stop=stop:
                lambda e: e.matmul(out, lhsT, rhs, start=start, stop=stop))(),
               wait=wait, inc=inc)


def ts(i, n=128):
    return slice(i * n, (i + 1) * n)


def cstcol(C, name):
    j = C.const_idx[name]
    return C.consts[:, j:j + 1]


def cst1(C, name):
    j = C.const_idx[name]
    return C.consts[0:1, j:j + 1]


def emit_ln(C):
    p = C.plan
    for tc in range(NTC):
        stat = C.psum[7]
        xw = [("ve", C.x_ready_ve), ("dy", C.x_ready_dy)]
        for c in range(NCH):
            buf = C.sq_bf[:, 0:TC]
            w = list(xw) if (tc == 0 and c == 0) else []
            if C.ln_marks:
                w.append(("pe", C.ln_marks[-1]))
            p.add("s", (lambda buf=buf, c=c, tc=tc: lambda e: e.activation(
                buf, C.x[:, c, ts(tc, TC)], AF.Square))(),
                wait=w)
            w2 = list(xw) if c == 0 else []
            if c == 0:
                w2.append(("ve", C.ln_stats_ve))
            mm(C, stat[0:1, 0:TC], C.ones_inv2048_f32, C.x[:, c, ts(tc, TC)],
               start=(c == 0), stop=(c == NCH - 1), wait=w2)
            mm(C, stat[64:65, 0:TC], C.ones_inv2048_bf, buf,
               start=(c == 0), stop=(c == NCH - 1),
               wait=[("se", p.val("se"))], inc=[("pe", 1)])
            C.ln_marks.append(p.val("pe"))
        pe_stats = p.val("pe")

        def stats_fn(tc=tc, stat=stat):
            def f(e):
                msl = C.stat_m[0:1, ts(tc, TC)]
                e.tensor_copy(msl, stat[0:1, 0:TC])
                e.tensor_tensor(C.row_a[0:1, 0:TC], msl, msl, ALU.mult)
                return e.tensor_tensor(C.row_b[0:1, 0:TC], stat[64:65, 0:TC],
                                       C.row_a[0:1, 0:TC], ALU.subtract)
            return f
        p.add("v", stats_fn(), wait=[("pe", pe_stats)], nops=3)
        C.ln_stats_ve = p.val("ve")
        p.add("s", (lambda: lambda e: e.activation(
            C.row_a[0:1, 0:TC], C.row_b[0:1, 0:TC], AF.Sqrt,
            bias=cst1(C, "ln_eps")))(),
            wait=[("ve", p.val("ve"))])
        p.add("v", (lambda tc=tc: lambda e: e.reciprocal(
            C.stat_r[0:1, ts(tc, TC)], C.row_a[0:1, 0:TC]))(),
            wait=[("se", p.val("se"))])
        ve_rstd = p.val("ve")

        mb, rb = C.psum[5], C.psum[6]
        mm(C, mb[:, 0:TC], C.ones1_f32, C.stat_m[0:1, ts(tc, TC)],
           wait=[("ve", ve_rstd), ("se", C.b56_se)])
        mm(C, rb[:, 0:TC], C.ones1_f32, C.stat_r[0:1, ts(tc, TC)],
           inc=[("pe", 1)])
        pe_bcast = p.val("pe")

        for c in range(NCH):
            def norm_fn(c=c, tc=tc, mb=mb, rb=rb):
                def f(e):
                    e.tensor_tensor(C.f32scr[:, 0:TC], C.x[:, c, ts(tc, TC)],
                                    mb[:, 0:TC], ALU.subtract)
                    return e.tensor_tensor(C.ln[:, c, ts(tc, TC)],
                                           C.f32scr[:, 0:TC], rb[:, 0:TC],
                                           ALU.mult)
                return f
            p.add("v", norm_fn(),
                  wait=[("pe", pe_bcast)] if c == 0 else (), nops=2)
    C.ln_ready_ve = p.val("ve")


def emit_gates(C, which):
    p = C.plan
    wT = C.t_wrT if which == "l0" else C.t_gwT
    bias_t = C.t_br if which == "l0" else C.t_rb

    p.add("g", (lambda wT=wT: lambda e: e.dma_start(
        out=C.rw_sb[:], in_=wT.rearrange("(c p) e -> p c e", p=128)))(),
        wait=[("pe", p.val("pe"))], inc=[("dg", 16)])
    p.add("g", (lambda bias_t=bias_t: lambda e: e.dma_start(
        out=C.row8[0:1, :], in_=bias_t))(), inc=[("dg", 16)])
    dg0 = p.val("dg")

    for c in range(NCH):
        mm(C, C.psum[7][0:1, 0:8], C.ones_f32, C.rw_sb[:, c, :],
           start=(c == 0), stop=(c == NCH - 1),
           wait=[("dg", dg0), ("ve", C.ln_ready_ve)] if c == 0 else (),
           inc=[("pe", 1)] if c == NCH - 1 else ())
    pe_ws = p.val("pe")
    p.add("s", lambda e: e.copy(C.row_a[0:1, 0:8], C.psum[7][0:1, 0:8]),
          wait=[("pe", pe_ws)])
    se_ws = p.val("se")
    mm(C, C.psum[5][:, 0:8], C.ones1_f32, C.row_a[0:1, 0:8],
       wait=[("se", se_ws), ("ve", C.ln_ready_ve)])
    mm(C, C.psum[5][:, 8:16], C.ones1_f32, C.row8[0:1, :],
       wait=[("dg", dg0)], inc=[("pe", 1)])
    pe_bc = p.val("pe")
    p.add("s", lambda e: e.copy(C.gate_wsb[:, 0:8], C.psum[5][:, 0:8]),
          wait=[("pe", pe_bc)])
    p.add("s", lambda e: e.copy(C.gate_bb[:, 0:8], C.psum[5][:, 8:16]))
    C.b56_se = p.val("se")
    se_bc = p.val("se")

    for tt in range(TOK // 128):
        lg = C.psum[4]
        for c in range(NCH):
            w = [("se", p.val("se"))] if c == 0 else []
            mm(C, lg[:, 0:8], C.x[:, c, ts(tt)], C.rw_sb[:, c, :],
               start=(c == 0), stop=(c == NCH - 1), wait=w)
        mm(C, lg[:, 8:9], C.stat_m[0:1, ts(tt)], C.ones11_f32)
        mm(C, lg[:, 9:10], C.stat_r[0:1, ts(tt)], C.ones11_f32, inc=[("pe", 1)])
        pe_lg = p.val("pe")

        def logits_fn(which=which, lg=lg):
            def f(e):
                mcol = C.colscr[:, 0:1]
                rcol = C.colscr[:, 1:2]
                e.tensor_copy(mcol, lg[:, 8:9])
                e.tensor_copy(rcol, lg[:, 9:10])
                e.tensor_scalar_mul(C.g8, C.gate_wsb[:, 0:8], mcol)
                e.tensor_tensor(C.g8, lg[:, 0:8], C.g8, ALU.subtract)
                e.tensor_scalar_mul(C.g8, C.g8, rcol)
                if which == "l0":
                    e.tensor_tensor(C.g8, C.g8, C.gate_bb[:, 0:8], ALU.add)
                    e.reduce_max(mcol, C.g8, axis=AXX)
                    e.tensor_single_scalar(C.g8b, C.g8, mcol, ALU.is_equal)
                    return e.tensor_scalar_mul(rcol, mcol, -1.0)
                else:
                    return e.tensor_copy(C.g8d, C.g8)
            return f
        p.add("v", logits_fn(), wait=[("pe", pe_lg), ("se", se_bc)],
              nops=(9 if which == "l0" else 6))
        ve1 = p.val("ve")

        if which == "l0":
            p.add("s", lambda e: e.activation(
                C.g8c, C.g8, AF.Exp, bias=C.colscr[:, 1:2]),
                wait=[("ve", ve1)])
            se2 = p.val("se")

            def fin(e):
                scol = C.colscr[:, 2:3]
                e.tensor_reduce(scol, C.g8c, AXX, ALU.add)
                e.reciprocal(scol, scol)
                e.tensor_scalar_mul(C.g8b, C.g8b, scol)
                return e.tensor_scalar_mul(C.g8b, C.g8b, 0.5 / 64.0)
            p.add("v", fin, wait=[("se", se2)], nops=4)
        else:
            p.add("s", lambda e: e.activation(C.g8c, C.g8d, AF.Sigmoid),
                  wait=[("ve", ve1)])
            se2 = p.val("se")

            def fin(tt=tt):
                def f(e):
                    m1c = C.colscr[:, 0:1]
                    m2c = C.colscr[:, 1:2]
                    dcol = C.colscr[:, 2:3]
                    e.tensor_tensor(C.g8, C.g8c, C.gate_bb[:, 0:8], ALU.add)
                    e.reduce_max(m1c, C.g8, axis=AXX)
                    e.tensor_single_scalar(C.g8b, C.g8, m1c, ALU.is_equal)
                    e.tensor_scalar(C.g8d, C.g8b, 1e30, None, ALU.mult)
                    e.tensor_tensor(C.g8d, C.g8, C.g8d, ALU.subtract)
                    if DEBUG_DUMP and tt == 0:
                        e.tensor_copy(C.dbg_sb[:, 0:8], C.g8c)
                        e.tensor_copy(C.dbg_sb[:, 8:16], C.g8)
                        e.tensor_copy(C.dbg_sb[:, 16:24], C.g8b)
                        e.tensor_copy(C.dbg_sb[:, 24:32], C.g8d)
                    e.reduce_max(m2c, C.g8d, axis=AXX)
                    e.tensor_single_scalar(C.g8d, C.g8d, m2c, ALU.is_equal)
                    e.tensor_tensor(C.g8b, C.g8b, C.g8d, ALU.add)
                    e.tensor_tensor(C.g8b, C.g8b, C.g8c, ALU.mult)
                    if DEBUG_DUMP and tt == 0:
                        e.tensor_copy(C.dbg_sb[:, 32:40], C.g8d)
                        e.tensor_copy(C.dbg_sb[:, 40:48], C.g8b)
                    e.tensor_reduce(dcol, C.g8b, AXX, ALU.add)
                    e.tensor_scalar_max(dcol, dcol, 1e-12)
                    e.reciprocal(dcol, dcol)
                    e.tensor_scalar_mul(C.g8b, C.g8b, dcol)
                    if DEBUG_DUMP and tt == 0:
                        e.tensor_copy(C.dbg_sb[:, 48:49], dcol)
                    return e.tensor_scalar_mul(C.g8b, C.g8b, 0.5)
                return f
            p.add("v", fin(), wait=[("se", se2)],
                  nops=14 + (7 if (DEBUG_DUMP and tt == 0) else 0))
        ve_g = p.val("ve")

        p.add("t", (lambda: lambda e: e.transpose(
            C.psum[4][0:8, 16:144], C.g8b, C.ident_f32))(),
            wait=[("ve", ve_g)], inc=[("pe", 1)])
        p.add("s", (lambda tt=tt: lambda e: e.copy(
            C.gT[0:8, ts(tt)], C.psum[4][0:8, 16:144]))(),
            wait=[("pe", p.val("pe"))])
    C.gates_se = p.val("se")


def emit_ffn0(C):
    p = C.plan
    emit_gates(C, "l0")

    a2_sb = C.arena_a[:, 0:8192].rearrange("p (c r) -> p c r", c=NCH)
    b2_sb = C.arena_a[:, 8192:16384].rearrange("p (r o) -> p r o", r=4)
    gb_sb = C.arena_b[:, 0:2048].rearrange("p (r t) -> p r t", r=4)
    hp_sb = C.arena_b[:, 2048:4096].rearrange("p (r t) -> p r t", r=4)

    p.add("g", (lambda a2_sb=a2_sb: lambda e: e.dma_start(
        out=a2_sb, in_=C.t_A2.rearrange("(c p) r -> p c r", p=128)))(),
        wait=[("pe", p.val("pe"))], inc=[("dg", 16)])
    p.add("g", (lambda b2_sb=b2_sb: lambda e: e.dma_start(
        out=b2_sb, in_=C.t_B2.rearrange("(c p) o -> p c o", p=128)))(),
        inc=[("dg", 16)])
    p.add("g", lambda e: e.dma_start(out=C.esel_sb[0:8, :], in_=C.t_esel),
          inc=[("dg", 16)])
    dgA = p.val("dg")

    for tc in range(NTC):
        for rt in range(4):
            w = [("se", p.val("se"))]
            if tc == 0 and rt == 0:
                w += [("se", C.gates_se), ("dg", dgA)]
            mm(C, C.psum[4][:, 0:TC], C.esel_sb[0:8, ts(rt)],
               C.gT[0:8, ts(tc, TC)], wait=w, inc=[("pe", 1)])
            p.add("s", (lambda rt=rt, gb_sb=gb_sb: lambda e: e.copy(
                gb_sb[:, rt, 0:TC], C.psum[4][:, 0:TC]))(),
                wait=[("pe", p.val("pe"))])
        se_gb = p.val("se")

        for rt in range(4):
            hb = C.psum[rt % 2]
            for c in range(NCH):
                w = []
                if rt == 0 and c == 0:
                    w = [("ve", C.ln_ready_ve)]
                if rt >= 2 and c == 0:
                    w = [("ve", p.val("ve"))]
                mm(C, hb[:, 0:TC], a2_sb[:, c, ts(rt)],
                   C.ln[:, c, ts(tc, TC)], start=(c == 0), stop=(c == NCH - 1),
                   wait=w, inc=([("pe", 1)] if c == NCH - 1 else ()))
            p.add("v", (lambda rt=rt, hb=hb, hp_sb=hp_sb, gb_sb=gb_sb:
                        lambda e: e.tensor_tensor(
                            hp_sb[:, rt, 0:TC], hb[:, 0:TC],
                            gb_sb[:, rt, 0:TC], ALU.mult))(),
                  wait=[("pe", p.val("pe")), ("se", se_gb)])
        ve_hp = p.val("ve")

        for ot in range(NCH):
            yb = C.psum[2 + (ot % 2)]
            for rt in range(4):
                w = []
                if ot == 0 and rt == 0:
                    w = [("ve", ve_hp)]
                if ot >= 2 and rt == 0:
                    w = [("ve", p.val("ve"))]
                mm(C, yb[:, 0:TC], b2_sb[:, rt, ts(ot)],
                   hp_sb[:, rt, 0:TC], start=(rt == 0), stop=(rt == 3),
                   wait=w, inc=([("pe", 1)] if rt == 3 else ()))
            p.add("v", (lambda ot=ot, tc=tc, yb=yb: lambda e: e.tensor_add(
                C.x[:, ot, ts(tc, TC)], C.x[:, ot, ts(tc, TC)], yb[:, 0:TC]))(),
                wait=[("pe", p.val("pe"))])
    C.x_ready_ve = p.val("ve")


def emit_attn0(C):
    p = C.plan
    qp_sb = C.arena_a.rearrange("p (c t) -> p c t", c=NCH)
    kp_sb = C.arena_b.rearrange("p (c t) -> p c t", c=NCH)
    v_ch = C.sq_bf
    kv_sb = C.arena_f[:, 520:2632].rearrange("p (h f) -> p h f", h=16)
    zb_sb = C.arena_f[:, 2632:3144]
    wsl = [C.wslab[:, ts(i, 2048)].rearrange("p (c m) -> p c m", c=NCH)
           for i in range(2)]
    kp_tm = C.arena_f[:, 0:256].bitcast(BF16).rearrange(
        "p (i m) -> p i m", i=4)
    v_tm = C.arena_f[:, 256:520].bitcast(BF16).rearrange(
        "p (i m) -> p i m", i=4)

    p.add("v", (lambda kv_sb=kv_sb: lambda e: e.memset(kv_sb, 0.0))(),
          wait=[("dg", C.plan.val("dg"))])
    ve_kv0 = p.val("ve")

    bank_rel = {}

    pb4 = C.psum[4].bitcast(BF16)
    pb5 = C.psum[5].bitcast(BF16)

    def proj(kind, wtensor, bias_sb):
        for ot in range(NCH):
            buf = ot % 2
            p.add("g", (lambda ot=ot, buf=buf, wtensor=wtensor: lambda e:
                        e.dma_start(out=wsl[buf],
                                    in_=wtensor[ot].rearrange(
                                        "(c p) m -> p c m", p=128)))(),
                  wait=[("pe", C.wslab_rel.get(buf, 0))], inc=[("dg", 16)])
            dg_sl = p.val("dg")
            for tc in range(NTC):
                bi = (ot % 2) * 2 + tc
                bk = C.psum[bi]
                for c in range(NCH):
                    w = []
                    if c == 0:
                        w = [("dg", dg_sl),
                             ("ve", bank_rel.get(bi, 0)),
                             ("se", C.sbank_rel.get(bi, 0))]
                        if kind == "q" and ot == 0 and tc == 0:
                            w.append(("ve", C.ln_ready_ve))
                    mm(C, bk[:, 0:TC], wsl[buf][:, c, :],
                       C.ln[:, c, ts(tc, TC)], start=(c == 0),
                       stop=(c == NCH - 1), wait=w,
                       inc=([("pe", 1)] if c == NCH - 1 else ()))
                C.wslab_rel[buf] = p.val("pe")
                pe_pr = p.val("pe")
                if kind == "v":
                    p.add("s", (lambda ot=ot, tc=tc, bk=bk, bias_sb=bias_sb:
                                lambda e: e.activation(
                                    v_ch[:, 0:TC], bk[:, 0:TC],
                                    AF.Identity, bias=bias_sb[:, ot:ot + 1]))(),
                          wait=[("pe", pe_pr), ("pe", C.vch_pe_rel.get(0, 0))])
                    C.sbank_rel[bi] = p.val("se")
                    se_vch = p.val("se")
                    h = ot
                    for i in range(4):
                        w0 = []
                        if i == 0:
                            w0 = [("se", se_vch), ("ge", ge_ones),
                                  ("se", C.tr_rel)]
                        p.add("t", (lambda h=h, tc=tc, i=i: lambda e:
                                    e.transpose(
                                        pb4[:, ts(i)],
                                        kp_sb[:, h, tc * TC + i * 128:
                                              tc * TC + (i + 1) * 128],
                                        C.ident_bf))(),
                            wait=w0)
                        p.add("t", (lambda tc=tc, i=i: lambda e: e.transpose(
                            pb5[:, ts(i)], v_ch[:, ts(i)],
                            C.ident_bf))(),
                            inc=[("pe", 1)] if i == 3 else ())
                    pe_tr = p.val("pe")
                    C.vch_pe_rel[0] = pe_tr
                    p.add("s", (lambda: lambda e: e.copy(
                        kp_tm, pb4[:, 0:512].rearrange(
                            "p (i m) -> p i m", i=4)))(),
                        wait=[("pe", pe_tr)])
                    p.add("s", (lambda: lambda e: e.copy(
                        v_tm[:, :, 0:128],
                        pb5[:, 0:512].rearrange("p (i m) -> p i m", i=4)))())
                    C.tr_rel = p.val("se")
                    se_tm = p.val("se")
                    for i in range(4):
                        mm(C, C.psum[6][:, 0:132], kp_tm[:, i, :],
                           v_tm[:, i, 0:132],
                           start=(i == 0), stop=(i == 3),
                           wait=([("se", se_tm), ("ve", C.kvbank_rel)]
                                 if i == 0 else ()),
                           inc=([("pe", 1)] if i == 3 else ()))
                    p.add("v", (lambda h=h, kv_sb=kv_sb: lambda e:
                                e.tensor_add(kv_sb[:, h, :], kv_sb[:, h, :],
                                             C.psum[6][:, 0:132]))(),
                        wait=[("pe", p.val("pe")), ("ve", ve_kv0)])
                    C.kvbank_rel = p.val("ve")
                else:
                    dst = qp_sb if kind == "q" else kp_sb
                    p.add("s", (lambda bk=bk, bias_sb=bias_sb, ot=ot:
                                lambda e: e.activation(
                                    C.f32scr[:, 0:TC], bk[:, 0:TC], AF.Exp,
                                    bias=bias_sb[:, ot:ot + 1]))(),
                          wait=[("pe", pe_pr), ("ve", C.expscr_rel)])
                    C.sbank_rel[bi] = p.val("se")

                    def evac(bk=bk, ot=ot, tc=tc, dst=dst, bias_sb=bias_sb):
                        def f(e):
                            e.tensor_scalar(C.f32scr2[:, 0:TC], bk[:, 0:TC],
                                            bias_sb[:, ot:ot + 1], 0.0,
                                            ALU.add, ALU.max)
                            e.tensor_single_scalar(C.f32scr[:, 0:TC],
                                                   C.f32scr[:, 0:TC], 1.0,
                                                   ALU.min)
                            return e.tensor_add(dst[:, ot, ts(tc, TC)],
                                                C.f32scr[:, 0:TC],
                                                C.f32scr2[:, 0:TC])
                        return f
                    p.add("v", evac(), wait=[("se", p.val("se"))], nops=3)
                    bank_rel[bi] = p.val("ve")
                    C.expscr_rel = p.val("ve")

    C.expscr_rel = 0
    C.vch_pe_rel = {}
    C.wslab_rel = {0: p.val("pe"), 1: p.val("pe")}
    ge_ones = 0
    proj("q", C.t_qwr, C.qb_sb)
    proj("k", C.t_kwr, C.kb_sb)
    p.add("g", (lambda v_tm=v_tm: lambda e: e.memset(v_tm[:, :, 128:132], 1.0))(),
          wait=[("ve", p.val("ve"))], inc=[("ge", 1)])
    ge_ones = p.val("ge")
    proj("v", C.t_vwr, C.vb_sb)

    p.add("y", (lambda kv_sb=kv_sb: lambda e: e.dma_start(
        out=C.t_kvpart.rearrange("(h p) f -> p h f", p=128), in_=kv_sb))(),
        wait=[("ve", p.val("ve"))], inc=[("dy", 16)])
    p.add("g", lambda e: e.collective_compute(
        "AllReduce", ALU.add, replica_groups=[list(range(NCORES))],
        ins=[C.t_kvpart[:, :]], outs=[C.t_kvred[:, :]]),
        wait=[("dy", p.val("dy"))], inc=[("cc", 1)])
    p.add("g", lambda e: e.dma_start(
        out=C.kvbf_sb[:], in_=C.t_kvred.rearrange("(h p) f -> p h f", p=128)),
        wait=[("cc", p.val("cc"))], inc=[("dg", 16)])
    dg_kv = p.val("dg")

    for h in range(NCH):
        for tc in range(NTC):
            ab = C.psum[(h * NTC + tc) % 2]
            mm(C, ab[:, 0:TC], C.kvbf_sb[:, h, 0:128], qp_sb[:, h, ts(tc, TC)],
               wait=[("dg", dg_kv), ("ve", C.abank_rel.get((h * NTC + tc) % 2, 0))],
               inc=[("pe", 1)])
            mm(C, C.psum[2][0:1, 0:TC], C.kvbf_sb[:, h, 128:129],
               qp_sb[:, h, ts(tc, TC)],
               wait=[("se", C.zrow_rel)], inc=[("pe", 1)])
            pe_az = p.val("pe")
            p.add("s", (lambda: lambda e: e.activation(
                C.row_a[0:1, 0:TC], C.psum[2][0:1, 0:TC], AF.Identity,
                bias=cst1(C, "z_eps")))(),
                wait=[("pe", pe_az)])
            C.zrow_rel = p.val("se")
            p.add("v", (lambda: lambda e: e.reciprocal(
                C.row_b[0:1, 0:TC], C.row_a[0:1, 0:TC]))(),
                wait=[("se", p.val("se"))])
            mm(C, C.psum[3][:, 0:TC], C.ones1_f32, C.row_b[0:1, 0:TC],
               wait=[("ve", p.val("ve")), ("se", C.zb_rel)], inc=[("pe", 1)])
            p.add("s", (lambda zb_sb=zb_sb: lambda e: e.copy(
                zb_sb, C.psum[3][:, 0:TC]))(),
                wait=[("pe", p.val("pe"))])
            C.zb_rel = p.val("se")
            p.add("v", (lambda h=h, tc=tc, ab=ab, qp_sb=qp_sb, zb_sb=zb_sb:
                        lambda e: e.tensor_tensor(
                            qp_sb[:, h, ts(tc, TC)], ab[:, 0:TC], zb_sb,
                            ALU.mult))(),
                  wait=[("se", p.val("se"))])
            C.abank_rel[(h * NTC + tc) % 2] = p.val("ve")
    ve_attn = p.val("ve")

    for ot in range(NCH):
        buf = ot % 2
        p.add("g", (lambda ot=ot, buf=buf: lambda e: e.dma_start(
            out=wsl[buf], in_=C.t_owr[ot].rearrange("(c p) m -> p c m", p=128)))(),
            wait=[("pe", C.wslab_rel.get(buf, 0))], inc=[("dg", 16)])
        dg_sl = p.val("dg")
        for tc in range(NTC):
            bk = C.psum[(ot % 2) * 2 + tc]
            for c in range(NCH):
                w = []
                if c == 0:
                    w = [("dg", dg_sl), ("ve", ve_attn),
                         ("ve", C.obank_rel.get((ot % 2) * 2 + tc, 0))]
                mm(C, bk[:, 0:TC], wsl[buf][:, c, :], qp_sb[:, c, ts(tc, TC)],
                   start=(c == 0), stop=(c == NCH - 1), wait=w,
                   inc=([("pe", 1)] if c == NCH - 1 else ()))
            C.wslab_rel[buf] = p.val("pe")
            p.add("v", (lambda ot=ot, tc=tc, bk=bk: lambda e:
                        e.scalar_tensor_tensor(
                            C.x[:, ot, ts(tc, TC)], bk[:, 0:TC],
                            C.ob_sb[:, ot:ot + 1], C.x[:, ot, ts(tc, TC)],
                            ALU.add, ALU.add))(),
                  wait=[("pe", p.val("pe"))])
            C.obank_rel[(ot % 2) * 2 + tc] = p.val("ve")
    C.x_ready_ve = p.val("ve")


FFN1_PARTS = "both"
DEBUG_DUMP = False


def emit_ffn1(C):
    p = C.plan
    emit_gates(C, "l1")

    act_e = C.arena_a[:, 0:11264].rearrange("p (j t) -> p j t", j=11)
    gb_sb = C.arena_a[:, 11264:12288]
    act_sh = C.arena_b[:, 0:11264].rearrange("p (j t) -> p j t", j=11 * 2)
    sl = [C.wslab[:, ts(i, 2048)].rearrange("p (c m) -> p c m", c=NCH)
          for i in range(4)]
    sl2 = [C.wslab[:, ts(i, 2816)].rearrange("p (c m) -> p c m", c=22)
           for i in range(2)]
    sl3 = [C.wslab[:, i * 1408:(i + 1) * 1408].rearrange(
        "p (c m) -> p c m", c=11) for i in range(2)]

    relA = {}
    relS = {}
    C.silu2 = [C.f32scr2, C.arena_f[:, 520:1032]]
    C.acttmp = [C.f32scr, C.arena_f[:, 1032:1544]]
    C.silu_rel2 = {}
    gates_se = C.gates_se
    p.add("g", lambda e: e.dma_start(out=C.oneh8_sb, in_=C.t_oneh8),
          wait=[("pe", p.val("pe"))], inc=[("dg", 16)])
    C.oneh8_dg = p.val("dg")
    pe_now = p.val("pe")
    C.slrel = {i: pe_now for i in range(4)}
    C.sl2rel = {i: pe_now for i in range(2)}
    C.sl3rel = {i: pe_now for i in range(2)}

    def slab_dma(dst, src_ap, buf_rel):
        p.add("g", (lambda dst=dst, src_ap=src_ap: lambda e: e.dma_start(
            out=dst, in_=src_ap.rearrange("(c p) m -> p c m", p=128)))(),
            wait=[("pe", buf_rel)], inc=[("dg", 16)])
        return p.val("dg")

    C.actsh_pe_rel = 0
    C.acte_pe_rel = 0
    for tc in (range(NTC) if FFN1_PARTS in ("both", "shared") else []):
        fence_h = p.val("pe")
        actsh_fence = C.actsh_pe_rel
        for j in range(22):
            b1 = C.psum[(j % 2) * 2]
            b3 = C.psum[(j % 2) * 2 + 1]
            bufa, bufb = (j % 2) * 2, (j % 2) * 2 + 1
            dg1 = slab_dma(sl[bufa], C.t_sw1r[j],
                           max(C.slrel.get(bufa, 0), fence_h))
            dg3 = slab_dma(sl[bufb], C.t_sw3r[j],
                           max(C.slrel.get(bufb, 0), fence_h))
            for c in range(NCH):
                w = []
                if c == 0:
                    w = [("dg", dg1), ("ve", relA.get((j % 2) * 2, 0)),
                         ("se", relS.get((j % 2) * 2, 0))]
                    if tc == 0 and j == 0:
                        w.append(("ve", C.ln_ready_ve))
                mm(C, b1[:, 0:TC], sl[bufa][:, c, :], C.ln[:, c, ts(tc, TC)],
                   start=(c == 0), stop=(c == NCH - 1), wait=w,
                   inc=([("pe", 1)] if c == NCH - 1 else ()))
            C.slrel[bufa] = p.val("pe")
            pe_h1 = p.val("pe")
            for c in range(NCH):
                w = []
                if c == 0:
                    w = [("dg", dg3), ("ve", relA.get((j % 2) * 2 + 1, 0))]
                mm(C, b3[:, 0:TC], sl[bufb][:, c, :], C.ln[:, c, ts(tc, TC)],
                   start=(c == 0), stop=(c == NCH - 1), wait=w,
                   inc=([("pe", 1)] if c == NCH - 1 else ()))
            C.slrel[bufb] = p.val("pe")
            pe_h3 = p.val("pe")
            p.add("s", (lambda b1=b1: lambda e: e.activation(
                C.f32scr2, b1[:, 0:TC], AF.Silu))(),
                wait=[("pe", pe_h1), ("ve", C.silu_rel)])
            relS[(j % 2) * 2] = p.val("se")
            p.add("v", (lambda j=j, tc=tc, b3=b3: lambda e:
                        e.scalar_tensor_tensor(
                            act_sh[:, j, 0:TC],
                            b3[:, 0:TC], 0.5, C.f32scr2,
                            ALU.mult, ALU.mult))(),
                wait=[("se", p.val("se")), ("pe", pe_h3),
                      ("pe", actsh_fence)])
            relA[(j % 2) * 2] = p.val("ve")
            relA[(j % 2) * 2 + 1] = p.val("ve")
            C.silu_rel = p.val("ve")
        ve_acts = p.val("ve")
        fence_sw2 = p.val("pe")
        for ot in range(NCH):
            buf = ot % 2
            dg2 = slab_dma(sl2[buf], C.t_sw2r[ot],
                           max(C.sl2rel.get(buf, 0), fence_sw2))
            bk = C.psum[5 + (ot % 2)]
            for jj in range(22):
                w = []
                if jj == 0:
                    w = [("dg", dg2), ("ve", ve_acts),
                         ("ve", relA.get(5 + (ot % 2), 0))]
                mm(C, bk[:, 0:TC], sl2[buf][:, jj, :], act_sh[:, jj, 0:TC],
                   start=(jj == 0), stop=(jj == 21), wait=w,
                   inc=([("pe", 1)] if jj == 21 else ()))
            C.sl2rel[buf] = p.val("pe")
            p.add("v", (lambda ot=ot, tc=tc, bk=bk: lambda e: e.tensor_add(
                C.x[:, ot, ts(tc, TC)], C.x[:, ot, ts(tc, TC)], bk[:, 0:TC]))(),
                wait=[("pe", p.val("pe"))])
            relA[5 + (ot % 2)] = p.val("ve")
        C.actsh_pe_rel = p.val("pe")

    if DEBUG_DUMP:
        C.t_gtdump = C.nc.dram_tensor("gT_dump", [8, TOK], BF16,
                                      kind="ExternalOutput").ap()
        p.add("y", lambda e: e.dma_start(out=C.t_gtdump, in_=C.gT[0:8, :]),
              wait=[("se", C.gates_se)], inc=[("dy", 16)])
        C.t_dbgdump = C.nc.dram_tensor("dbg_dump", [128, 64], F32,
                                       kind="ExternalOutput").ap()
        p.add("y", lambda e: e.dma_start(out=C.t_dbgdump, in_=C.dbg_sb),
              wait=[("se", C.gates_se)], inc=[("dy", 16)])

    for ex in (range(8) if FFN1_PARTS in ("both", "experts") else []):
        for tc in range(NTC):
            mm(C, C.psum[4][:, 0:TC], C.oneh8_sb[0:8, ts(ex)],
               C.gT[0:8, ts(tc, TC)],
               wait=[("se", p.val("se")), ("se", gates_se),
                     ("ve", relA.get(4, 0)), ("ve", relA.get(6, 0)),
                     ("dg", C.oneh8_dg)], inc=[("pe", 1)])
            p.add("s", (lambda ex=ex, tc=tc: lambda e: e.copy(
                gb_sb[:, ts(tc, TC)], C.psum[4][:, 0:TC]))(),
                wait=[("pe", p.val("pe")), ("ve", C.gb_rel)])
        se_gb = p.val("se")
        fence_h = p.val("pe")
        acte_fence = C.acte_pe_rel
        for j in range(11):
            bufa, bufb = (j % 2) * 2, (j % 2) * 2 + 1
            dg1 = slab_dma(sl[bufa], C.t_w1r[ex, j],
                           max(C.slrel.get(bufa, 0), fence_h))
            dg3 = slab_dma(sl[bufb], C.t_w3r[ex, j],
                           max(C.slrel.get(bufb, 0), fence_h))
            for c in range(NCH):
                for tc in range(NTC):
                    w = []
                    if c == 0:
                        w = [("dg", dg1), ("ve", relA.get(tc, 0)),
                             ("se", relS.get(tc, 0))]
                    mm(C, C.psum[tc][:, 0:TC], sl[bufa][:, c, :],
                       C.ln[:, c, ts(tc, TC)], start=(c == 0),
                       stop=(c == NCH - 1), wait=w,
                       inc=([("pe", 1)] if (c == NCH - 1 and tc == 1) else ()))
            C.slrel[bufa] = p.val("pe")
            pe_eh1 = p.val("pe")
            for c in range(NCH):
                for tc in range(NTC):
                    w = []
                    if c == 0:
                        w = [("dg", dg3), ("ve", relA.get(2 + tc, 0))]
                    mm(C, C.psum[2 + tc][:, 0:TC], sl[bufb][:, c, :],
                       C.ln[:, c, ts(tc, TC)], start=(c == 0),
                       stop=(c == NCH - 1), wait=w,
                       inc=([("pe", 1)] if (c == NCH - 1 and tc == 1) else ()))
            C.slrel[bufb] = p.val("pe")
            pe_eh3 = p.val("pe")
            for tc in range(NTC):
                sil = C.silu2[tc]
                tmp = C.acttmp[tc]
                p.add("s", (lambda tc=tc, sil=sil: lambda e: e.activation(
                    sil, C.psum[tc][:, 0:TC], AF.Silu))(),
                    wait=[("pe", pe_eh1), ("ve", C.silu_rel2.get(tc, 0))])
                relS[tc] = p.val("se")

                def actfn(j=j, tc=tc, sil=sil, tmp=tmp):
                    def f(e):
                        e.tensor_tensor(tmp, sil,
                                        C.psum[2 + tc][:, 0:TC], ALU.mult)
                        return e.tensor_tensor(
                            act_e[:, j, ts(tc, TC)], tmp,
                            gb_sb[:, ts(tc, TC)], ALU.mult)
                    return f
                p.add("v", actfn(), wait=[("se", p.val("se")), ("se", se_gb),
                                          ("pe", pe_eh3),
                                          ("pe", acte_fence)], nops=2)
                relA[tc] = p.val("ve")
                relA[2 + tc] = p.val("ve")
                C.silu_rel2[tc] = p.val("ve")
        ve_acts = p.val("ve")
        C.gb_rel = ve_acts
        fence_w2 = p.val("pe")
        for ot in range(NCH):
            buf = ot % 2
            dg2 = slab_dma(sl3[buf], C.t_w2r[ex, ot],
                           max(C.sl3rel.get(buf, 0), fence_w2))
            bks = [C.psum[4 + (ot % 2) * 2 + tc] for tc in range(NTC)]
            for jj in range(11):
                for tc in range(NTC):
                    w = []
                    if jj == 0:
                        w = [("dg", dg2), ("ve", ve_acts),
                             ("ve", relA.get(4 + (ot % 2) * 2 + tc, 0)),
                             ("se", relS.get(4 + (ot % 2) * 2 + tc, 0))]
                    mm(C, bks[tc][:, 0:TC], sl3[buf][:, jj, :],
                       act_e[:, jj, ts(tc, TC)],
                       start=(jj == 0), stop=(jj == 10), wait=w,
                       inc=([("pe", 1)] if (jj == 10 and tc == 1) else ()))
            pe_w2 = p.val("pe")
            for tc in range(NTC):
                p.add("v", (lambda ot=ot, tc=tc, bk=bks[tc]: lambda e:
                            e.tensor_add(C.x[:, ot, ts(tc, TC)],
                                         C.x[:, ot, ts(tc, TC)],
                                         bk[:, 0:TC]))(),
                    wait=[("pe", pe_w2)])
                relA[4 + (ot % 2) * 2 + tc] = p.val("ve")
            C.sl3rel[buf] = p.val("pe")
        C.acte_pe_rel = p.val("pe")
    C.x_ready_ve = p.val("ve")



def emit_attn1(C):
    p = C.plan
    ln_flat = C.ln.rearrange("p c t -> p (c t)")
    K_sb = C.arena_a.rearrange("p (h t) -> p h t", h=2)
    V_sb = C.arena_b.rearrange("p (h t) -> p h t", h=2)
    qa_bf = C.arena_a[:, 0:6144].rearrange("p (o t) -> p o t", o=6)
    ckv_raw = C.arena_a[:, 6144:8192].rearrange("p (o t) -> p o t", o=4)
    cos1_sb = C.arena_b[0:64, 0:2048].bitcast(F32)
    sin1_sb = C.arena_b[0:64, 2048:4096].bitcast(F32)
    wsl = [C.wslab[:, ts(i, 2048)].rearrange("p (c m) -> p c m", c=NCH)
           for i in range(2)]
    cos2_sb = C.wslab[:, 4096:5120].bitcast(F32)
    sin2_sb = C.wslab[:, 5120:6144].bitcast(F32)
    qan_sb = ln_flat[:, 0:6144].rearrange("p (o t) -> p o t", o=6)
    ckv_sb = ln_flat[:, 0:4096].rearrange("p (o t) -> p o t", o=4)
    att_sb = ln_flat[:, 0:16384].rearrange("p (c t) -> p c t", c=NCH)
    nrm = [C.nrm_bf[:, ts(i, TC)] for i in range(2)]
    kr_bf = C.kr_bf
    qro_bf = C.nrm_bf[:, 0:TC]
    pe0 = p.val("pe")
    C.wslab_rel = {0: pe0, 1: pe0}
    C.cos2_rel = pe0
    C.attn1_sb_rel = {}
    p.add("y", lambda e: e.dma_start(out=cos1_sb, in_=C.t_cos1),
          wait=[("pe", pe0), ("ve", C.x_ready_ve)], inc=[("dy", 16)])
    p.add("y", lambda e: e.dma_start(out=sin1_sb, in_=C.t_sin1),
          inc=[("dy", 16)])
    C.cs1_dy = p.val("dy")

    def rms_block(n_tiles, wdram, gam, inv_name, dst_row0, raw_sb, tc,
                  stat_slice):
        for o in range(n_tiles):
            buf = o % 2
            p.add("g", (lambda o=o, buf=buf, wdram=wdram: lambda e:
                        e.dma_start(out=wsl[buf], in_=wdram[o].rearrange(
                            "(c p) m -> p c m", p=128)))(),
                  wait=[("pe", C.wslab_rel.get(buf, 0))], inc=[("dg", 16)])
            dg_sl = p.val("dg")
            bk = C.psum[o % 2]
            for c in range(NCH):
                w = []
                if c == 0:
                    w = [("dg", dg_sl), ("se", C.attn1_sb_rel.get(o % 2, 0))]
                    if o < 2:
                        w.append(("ve", C.ln_ready_ve))
                mm(C, bk[:, 0:TC], wsl[buf][:, c, :], C.ln[:, c, ts(tc, TC)],
                   start=(c == 0), stop=(c == NCH - 1), wait=w,
                   inc=([("pe", 1)] if c == NCH - 1 else ()))
            C.wslab_rel[buf] = p.val("pe")
            pe_pr = p.val("pe")
            p.add("s", (lambda o=o, bk=bk, raw_sb=raw_sb: lambda e: e.copy(
                raw_sb[:, o, 0:TC], bk[:, 0:TC]))(), wait=[("pe", pe_pr)])
            p.add("s", (lambda o=o, raw_sb=raw_sb: lambda e: e.activation(
                C.sq_bf[:, 0:TC], raw_sb[:, o, 0:TC], AF.Square))(),
                wait=[("pe", C.sqmm_rel)])
            se_sq = p.val("se")
            C.attn1_sb_rel[o % 2] = se_sq
            mm(C, C.psum[7][stat_slice, 0:TC], C.ones_bf, C.sq_bf[:, 0:TC],
               start=(o == 0), stop=(o == n_tiles - 1),
               wait=[("se", se_sq)] + ([("ve", C.stat7_rel)] if o == 0 else []),
               inc=[("pe", 1)])
            C.sqmm_rel = p.val("pe")
        pe_stats = p.val("pe")
        p.add("s", (lambda inv_name=inv_name, stat_slice=stat_slice:
                    lambda e: e.activation(
                        C.row_a[0:1, 0:TC], C.psum[7][stat_slice, 0:TC],
                        AF.Sqrt, bias=cst1(C, "rms_eps"),
                        scale=cst1(C, inv_name)))(),
            wait=[("pe", pe_stats)])
        p.add("v", (lambda: lambda e: e.reciprocal(
            C.row_b[0:1, 0:TC], C.row_a[0:1, 0:TC]))(),
            wait=[("se", p.val("se"))])
        C.stat7_rel = p.val("ve")
        mm(C, C.psum[5][:, 0:TC], C.ones1_f32, C.row_b[0:1, 0:TC],
           wait=[("ve", p.val("ve")), ("se", C.b5_rel)], inc=[("pe", 1)])
        p.add("s", (lambda: lambda e: e.copy(C.f32scr[:, 0:TC],
                                             C.psum[5][:, 0:TC]))(),
            wait=[("pe", p.val("pe"))])
        C.b5_rel = p.val("se")
        se_rb = p.val("se")
        for o in range(n_tiles):
            nb = nrm[o % 2]

            def normf(o=o, nb=nb, raw_sb=raw_sb, gam=gam):
                def f(e):
                    e.tensor_tensor(C.f32scr2[:, 0:TC], raw_sb[:, o, 0:TC],
                                    C.f32scr[:, 0:TC], ALU.mult)
                    return e.tensor_scalar_mul(nb, C.f32scr2[:, 0:TC],
                                               gam[:, o:o + 1])
                return f
            p.add("v", normf(), wait=[("se", se_rb),
                                      ("dy", C.nrm_dy_rel.get(o % 2, 0))],
                  nops=2)
            ve_n = p.val("ve")
            p.add("y", (lambda o=o, nb=nb, dst_row0=dst_row0, tc=tc:
                        lambda e: e.dma_start(
                            out=C.t_contrib[dst_row0 + o * 128:
                                            dst_row0 + (o + 1) * 128,
                                            ts(tc, TC)],
                            in_=nb))(),
                  wait=[("ve", ve_n)], inc=[("dy", 16)])
            C.nrm_dy_rel[o % 2] = p.val("dy")

    C.sqmm_rel = 0
    C.stat7_rel = 0
    C.b5_rel = 0
    C.nrm_dy_rel = {}
    for tc in range(NTC):
        rms_block(6, C.t_qar, C.qgam_sb, "inv768", 0, qa_bf, tc,
                  slice(0, 1))
        rms_block(4, C.t_kvar, C.kgam_sb, "inv512", 768, ckv_raw, tc,
                  slice(64, 65))
        p.add("g", (lambda: lambda e: e.dma_start(
            out=wsl[0], in_=C.t_kvar[4].rearrange("(c p) m -> p c m", p=128)))(),
            wait=[("pe", C.wslab_rel.get(0, 0))], inc=[("dg", 16)])
        dg_sl = p.val("dg")
        bk = C.psum[0]
        for c in range(NCH):
            w = [("dg", dg_sl), ("se", C.attn1_sb_rel.get(0, 0))] if c == 0 else []
            mm(C, bk[:, 0:TC], wsl[0][:, c, :], C.ln[:, c, ts(tc, TC)],
               start=(c == 0), stop=(c == NCH - 1), wait=w,
               inc=([("pe", 1)] if c == NCH - 1 else ()))
        C.wslab_rel[0] = p.val("pe")
        pe_kr = p.val("pe")
        p.add("s", (lambda bk=bk: lambda e: e.copy(
            kr_bf[0:64, 0:TC], bk[0:64, 0:TC]))(),
            wait=[("pe", pe_kr), ("dy", C.kro_dy_rel)])
        C.attn1_sb_rel[0] = p.val("se")
        se_kr = p.val("se")
        mm(C, C.psum[6][0:64, 0:TC], C.perm1_sb, kr_bf[0:64, 0:TC],
           wait=[("se", se_kr), ("ve", C.rot_rel)], inc=[("pe", 1)])
        pe_rot = p.val("pe")

        def ropef(tc=tc):
            def f(e):
                e.tensor_tensor(C.f32scr[0:64, 0:TC], kr_bf[0:64, 0:TC],
                                cos1_sb[0:64, ts(tc, TC)], ALU.mult)
                e.tensor_tensor(C.f32scr2[0:64, 0:TC], C.psum[6][0:64, 0:TC],
                                sin1_sb[0:64, ts(tc, TC)], ALU.mult)
                return e.tensor_add(kr_bf[0:64, 0:TC], C.f32scr[0:64, 0:TC],
                                    C.f32scr2[0:64, 0:TC])
            return f
        p.add("v", ropef(), wait=[("pe", pe_rot), ("dy", C.kro_dy_rel),
                                  ("dy", C.cs1_dy)], nops=3)
        C.rot_rel = p.val("ve")
        p.add("y", (lambda tc=tc: lambda e: e.dma_start(
            out=C.t_contrib[1280:1344, ts(tc, TC)], in_=kr_bf[0:64, 0:TC]))(),
            wait=[("ve", p.val("ve"))], inc=[("dy", 16)])
        C.kro_dy_rel = p.val("dy")

    p.add("g", lambda e: e.collective_compute(
        "AllGather", ALU.bypass, replica_groups=[list(range(NCORES))],
        ins=[C.t_contrib[:, :]], outs=[C.t_gath[:, :]]),
        wait=[("dy", p.val("dy"))], inc=[("cc", 1)])
    cc_gath = p.val("cc")

    wslab_free = max(C.wslab_rel.get(0, 0), C.wslab_rel.get(1, 0))
    p.add("g", (lambda: lambda e: e.dma_start(
        out=C.kvb_sb, in_=C.t_kvbr.rearrange("o (c p) m -> p o c m", p=128)))(),
        wait=[("pe", wslab_free)], inc=[("dg", 16)])
    p.add("g", (lambda: lambda e: e.dma_start(
        out=C.qbr_sb, in_=C.t_qbr.rearrange("o (c p) m -> p o c m", p=128)))(),
        inc=[("dg", 16)])
    p.add("g", lambda e: e.dma_start(out=C.maskq_sb, in_=C.t_maskq),
          wait=[("cc", cc_gath)], inc=[("dg", 16)])
    dg_w = p.val("dg")
    p.add("v", lambda e: e.memset(C.kra_sb[64:65, :], 1.0),
          wait=[("cc", cc_gath)])
    ve_kra1 = p.val("ve")

    pb4 = C.psum[4].bitcast(BF16)
    C.ckv_pe_rel = 0
    for r in range(NCORES):
        p.add("y", (lambda r=r: lambda e: e.dma_start(
            out=ckv_sb, in_=C.t_gath[r * 1344 + 768:r * 1344 + 1280, :]
            .rearrange("(c p) t -> p c t", p=128)))(),
            wait=[("cc", cc_gath), ("pe", C.ckv_pe_rel), ("ve", ve_kra1)],
            inc=[("dy", 16)])
        p.add("y", (lambda r=r: lambda e: e.dma_start(
            out=C.kra_sb[0:64, r * TOK:(r + 1) * TOK],
            in_=C.t_gath[r * 1344 + 1280:r * 1344 + 1344, :]))(),
            wait=[("cc", cc_gath)], inc=[("dy", 16)])
        dy_ck = p.val("dy")
        for o4 in range(4):
            h, is_v = o4 // 2, o4 % 2
            for tc2 in range(NTC):
                bk = C.psum[o4 % 2]
                for c4 in range(4):
                    w = []
                    if c4 == 0:
                        w = [("dy", dy_ck), ("dg", dg_w),
                             ("se", C.attn1_sb_rel.get(o4 % 2, 0))]
                    mm(C, bk[:, 0:TC], C.kvb_sb[:, o4, c4, :],
                       ckv_sb[:, c4, ts(tc2, TC)],
                       start=(c4 == 0), stop=(c4 == 3), wait=w,
                       inc=([("pe", 1)] if c4 == 3 else ()))
                pe_up = p.val("pe")
                col0 = r * TOK + tc2 * TC
                if not is_v:
                    p.add("s", (lambda h=h, col0=col0, bk=bk: lambda e:
                                e.copy(K_sb[:, h, col0:col0 + TC],
                                       bk[:, 0:TC]))(),
                          wait=[("pe", pe_up)])
                    C.attn1_sb_rel[o4 % 2] = p.val("se")
                else:
                    p.add("s", (lambda bk=bk: lambda e: e.copy(
                        C.sq_bf[:, 0:TC], bk[:, 0:TC]))(),
                          wait=[("pe", pe_up), ("pe", C.vtr_rel)])
                    C.attn1_sb_rel[o4 % 2] = p.val("se")
                    se_v = p.val("se")
                    for i in range(4):
                        p.add("t", (lambda i=i: lambda e: e.transpose(
                            pb4[:, ts(i)], C.sq_bf[:, ts(i)], C.ident_bf))(),
                            wait=([("se", se_v), ("se", C.vtr_se_rel)]
                                  if i == 0 else ()),
                            inc=([("pe", 1)] if i == 3 else ()))
                    C.vtr_rel = p.val("pe")
                    p.add("s", (lambda h=h, col0=col0: lambda e: e.copy(
                        V_sb[:, h, col0:col0 + TC],
                        pb4[:, 0:512]))(), wait=[("pe", C.vtr_rel)])
                    C.vtr_se_rel = p.val("se")
        C.ckv_pe_rel = p.val("pe")

    maskq = C.maskq_sb.rearrange("p (r t) -> p r t", r=4)
    PTv = [C.pt_bf[:, ts(i, TC)] for i in range(2)]
    last_flash_pe = p.val("pe")
    for qt in range(16):
        r, tc2 = qt // 2, qt % 2
        if tc2 == 0:
            p.add("y", (lambda r=r: lambda e: e.dma_start(
                out=qan_sb, in_=C.t_gath[r * 1344:r * 1344 + 768, :]
                .rearrange("(c p) t -> p c t", p=128)))(),
                wait=[("cc", cc_gath), ("pe", p.val("pe"))], inc=[("dy", 16)])
        dy_qan = p.val("dy")
        p.add("y", (lambda qt=qt: lambda e: e.dma_start(
            out=cos2_sb, in_=C.t_cos2[:, ts(qt, TC)]))(),
            wait=[("pe", C.cos2_rel)], inc=[("dy", 16)])
        p.add("y", (lambda qt=qt: lambda e: e.dma_start(
            out=sin2_sb, in_=C.t_sin2[:, ts(qt, TC)]))(),
            inc=[("dy", 16)])
        dy_cs = p.val("dy")
        for o3 in range(3):
            bk = C.psum[6]
            for c6 in range(6):
                w = []
                if c6 == 0:
                    w = [("dy", dy_qan), ("dg", dg_w),
                         ("se", C.qup_se_rel)]
                mm(C, bk[:, 0:TC], C.qbr_sb[:, o3, c6, :],
                   qan_sb[:, c6, ts(tc2, TC)],
                   start=(c6 == 0), stop=(c6 == 5), wait=w,
                   inc=([("pe", 1)] if c6 == 5 else ()))
            pe_q = p.val("pe")
            if o3 < 2:
                p.add("s", (lambda o3=o3, bk=bk: lambda e: e.activation(
                    C.qtn_sb[:, o3, :], bk[:, 0:TC], AF.Identity,
                    scale=cstcol(C, "qscale")))(),
                    wait=[("pe", pe_q), ("pe", last_flash_pe)])
                C.qup_se_rel = p.val("se")
            else:
                p.add("s", (lambda bk=bk: lambda e: e.activation(
                    C.qr_bf[:, 0:TC], bk[:, 0:TC], AF.Identity,
                    scale=cstcol(C, "qscale")))(),
                    wait=[("pe", pe_q), ("ve", C.qro_ve_rel)])
                C.qup_se_rel = p.val("se")
                se_qr = p.val("se")
                mm(C, C.psum[7][:, 0:TC], C.perm2_sb, C.qr_bf[:, 0:TC],
                   wait=[("se", se_qr), ("ve", C.qro_ve_rel)],
                   inc=[("pe", 1)])
                pe_rot = p.val("pe")

                def qropef():
                    def f(e):
                        e.tensor_tensor(C.f32scr[:, 0:TC], C.qr_bf[:, 0:TC],
                                        cos2_sb, ALU.mult)
                        e.tensor_tensor(C.f32scr2[:, 0:TC],
                                        C.psum[7][:, 0:TC], sin2_sb, ALU.mult)
                        return e.tensor_add(qro_bf, C.f32scr[:, 0:TC],
                                            C.f32scr2[:, 0:TC])
                    return f
                p.add("v", qropef(), wait=[("pe", pe_rot), ("dy", dy_cs),
                                           ("se", C.qra_se_rel)], nops=3)
                C.qro_ve_rel = p.val("ve")
                p.add("s", (lambda: lambda e: e.copy(
                    C.qra0[0:64, 0:TC], qro_bf[0:64, 0:TC]))(),
                    wait=[("ve", p.val("ve")), ("pe", last_flash_pe)])
                p.add("s", (lambda: lambda e: e.copy(
                    C.qra1[0:64, 0:TC], qro_bf[64:128, 0:TC]))())
                C.qra_se_rel = p.val("se")
        se_q = p.val("se")

        for h in range(2):
            qra = C.qra0 if h == 0 else C.qra1
            for js in range(4):
                kcd = qt * 4 + js
                mm(C, C.psum[4][:, 0:128],
                   C.qtn_sb[:, h, ts(js)], K_sb[:, h, ts(kcd)],
                   start=True, stop=False,
                   wait=[("se", se_q), ("ve", C.m0_ve_rel)])
                mm(C, C.psum[4][:, 0:128], qra[0:64, ts(js)],
                   C.kra_sb[0:64, ts(kcd)], start=False, stop=True,
                   inc=[("pe", 1)])
                pe_qk = p.val("pe")

                def m0f(js=js):
                    def f(e):
                        e.tensor_add(C.psum[4][:, 0:128], C.psum[4][:, 0:128],
                                     C.trilneg_sb)
                        return e.reduce_max(C.colscr2[:, 0:1],
                                            C.psum[4][:, 0:128], axis=AXX)
                    return f
                p.add("v", m0f(), wait=[("pe", pe_qk)], nops=2)
                C.m0_ve_rel = p.val("ve")
                mm(C, C.psum[5][0:1, ts(js)], C.colscr2[:, 0:1], C.ident_f32,
                   wait=[("ve", C.m0_ve_rel), ("se", C.m0row_se_rel)],
                   inc=[("pe", 1)])
            pe_m0 = p.val("pe")
            p.add("s", (lambda qra=qra: lambda e: e.mul(
                qra[64:65, 0:TC], C.psum[5][0:1, 0:TC], -1.0))(),
                wait=[("pe", pe_m0)])
            C.m0row_se_rel = p.val("se")
            se_aug = p.val("se")

            nkc = qt * 4 + 4
            for kc in range(nkc):
                sb_ = C.psum[kc % 2]
                mm(C, sb_[:, 0:TC], K_sb[:, h, ts(kc)], C.qtn_sb[:, h, :],
                   start=True, stop=False,
                   wait=[("se", se_aug), ("se", C.pt_se_rel.get(kc % 2, 0))])
                mm(C, sb_[:, 0:TC], C.kra_sb[0:65, ts(kc)], qra[0:65, 0:TC],
                   start=False, stop=True, inc=[("pe", 1)])
                pe_sc = p.val("pe")
                p.add("s", (lambda kc=kc, sb_=sb_: lambda e: e.activation(
                    PTv[kc % 2], sb_[:, 0:TC], AF.Exp))(),
                    wait=[("pe", pe_sc), ("ve", C.pt_ve_rel.get(kc % 2, 0)),
                          ("pe", C.pt_pe_rel.get(kc % 2, 0))])
                C.pt_se_rel[kc % 2] = p.val("se")
                se_pt = p.val("se")
                wpv = [("se", se_pt)]
                if kc >= qt * 4:
                    p.add("v", (lambda kc=kc, qt=qt: lambda e: e.tensor_mul(
                        PTv[kc % 2], PTv[kc % 2], maskq[:, kc - qt * 4, :]))(),
                        wait=[("se", se_pt)])
                    C.pt_ve_rel[kc % 2] = p.val("ve")
                    wpv = [("ve", p.val("ve"))]
                if kc == 0:
                    wpv = wpv + [("ve", C.acc_ve_rel)]
                mm(C, C.psum[2][:, 0:TC], V_sb[:, h, ts(kc)], PTv[kc % 2],
                   start=(kc == 0), stop=(kc == nkc - 1), wait=wpv)
                mm(C, C.psum[3][0:1, 0:TC], C.ones_bf, PTv[kc % 2],
                   start=(kc == 0), stop=(kc == nkc - 1),
                   wait=([("se", C.sum_se_rel)] if kc == 0 else ()),
                   inc=[("pe", 1)])
                C.pt_pe_rel[kc % 2] = p.val("pe")
            pe_fl = p.val("pe")
            last_flash_pe = pe_fl
            p.add("s", (lambda: lambda e: e.copy(
                C.row_a[0:1, 0:TC], C.psum[3][0:1, 0:TC]))(),
                wait=[("pe", pe_fl)])
            p.add("v", (lambda: lambda e: e.reciprocal(
                C.row_b[0:1, 0:TC], C.row_a[0:1, 0:TC]))(),
                wait=[("se", p.val("se"))])
            mm(C, C.psum[3][:, 0:TC], C.ones1_f32, C.row_b[0:1, 0:TC],
               wait=[("ve", p.val("ve"))], inc=[("pe", 1)])
            p.add("s", (lambda: lambda e: e.copy(
                C.f32scr[:, 0:TC], C.psum[3][:, 0:TC]))(),
                wait=[("pe", p.val("pe"))])
            se_rb2 = p.val("se")
            C.sum_se_rel = se_rb2
            p.add("v", (lambda: lambda e: e.tensor_tensor(
                C.attno_bf[:, 0:TC], C.psum[2][:, 0:TC], C.f32scr[:, 0:TC],
                ALU.mult))(),
                wait=[("se", se_rb2), ("dy", C.atto_dy_rel)])
            ve_at = p.val("ve")
            C.acc_ve_rel = ve_at
            p.add("y", (lambda h=h, qt=qt: lambda e: e.dma_start(
                out=C.t_attc[h * 128:(h + 1) * 128, ts(qt, TC)],
                in_=C.attno_bf[:, 0:TC]))(),
                wait=[("ve", ve_at)], inc=[("dy", 16)])
            C.atto_dy_rel = p.val("dy")

    p.add("g", lambda e: e.collective_compute(
        "AllGather", ALU.bypass, replica_groups=[list(range(NCORES))],
        ins=[C.t_attc[:, :]], outs=[C.t_gath2[:, :]]),
        wait=[("dy", p.val("dy"))], inc=[("cc", 1)])
    cc_g2 = p.val("cc")
    p.add("y", lambda e: e.dma_start(out=C.coff_sb[:], in_=C.t_coff),
          inc=[("dy", 16)])
    dy_off = p.val("dy")

    def att_load(idx, dst):
        def f(e):
            with e.register(f"roff{idx}") as r:
                e.reg_load(r, C.coff_sb[0:1, idx:idx + 1])
                src = bass.AP(C.t_gath2_h, _RV(r), [[T, 128], [1, TC]])
                return e.dma_start(out=dst, in_=src)
        return f
    for tc in range(NTC):
        for c in range(NCH):
            p.add("y", att_load(c * 2 + tc, att_sb[:, c, ts(tc, TC)]),
                  wait=[("cc", cc_g2), ("dy", dy_off),
                        ("pe", p.val("pe"))] if (tc == 0 and c == 0) else
                       [("cc", cc_g2)],
                  inc=[("dy", 16)])
    dy_att = p.val("dy")

    C.wslab_rel = {0: p.val("pe"), 1: p.val("pe")}
    for ot in range(NCH):
        buf = ot % 2
        p.add("g", (lambda ot=ot, buf=buf: lambda e: e.dma_start(
            out=wsl[buf], in_=C.t_wor[ot].rearrange("(c p) m -> p c m",
                                                    p=128)))(),
            wait=[("pe", C.wslab_rel.get(buf, 0))], inc=[("dg", 16)])
        dg_sl = p.val("dg")
        for tc in range(NTC):
            bk = C.psum[(ot % 2) * 2 + tc]
            for c in range(NCH):
                w = []
                if c == 0:
                    w = [("dg", dg_sl), ("dy", dy_att),
                         ("ve", C.oproj_rel.get((ot % 2) * 2 + tc, 0))]
                mm(C, bk[:, 0:TC], wsl[buf][:, c, :], att_sb[:, c, ts(tc, TC)],
                   start=(c == 0), stop=(c == NCH - 1), wait=w,
                   inc=([("pe", 1)] if c == NCH - 1 else ()))
            C.wslab_rel[buf] = p.val("pe")
            p.add("v", (lambda ot=ot, tc=tc, bk=bk: lambda e: e.tensor_add(
                C.x[:, ot, ts(tc, TC)], C.x[:, ot, ts(tc, TC)], bk[:, 0:TC]))(),
                wait=[("pe", p.val("pe"))])
            C.oproj_rel[(ot % 2) * 2 + tc] = p.val("ve")
    C.x_ready_ve = p.val("ve")


def build_group(parts):
    nc = bass.Bass("TRN2", target_bir_lowering=False,
                   detect_race_conditions=False)
    C = Ctx()
    C.nc = nc
    C.plan = Plan()
    C.ln_marks = []
    C.wslab_rel = {}
    C.sbank_rel = {}
    C.abank_rel = {}
    C.obank_rel = {}
    C.tr_rel = 0
    C.kvbank_rel = 0
    C.zrow_rel = 0
    C.zb_rel = 0
    C.b56_se = 0
    C.gb_rel = 0
    C.silu_rel = 0
    C.pt_se_rel = {}
    C.pt_ve_rel = {}
    C.pt_pe_rel = {}
    C.sum_se_rel = 0
    C.m0_ve_rel = 0
    C.m0row_se_rel = 0
    C.qup_se_rel = 0
    C.qro_ve_rel = 0
    C.qra_se_rel = 0
    C.atto_dy_rel = 0
    C.acc_ve_rel = 0
    C.oproj_rel = {}
    C.vtr_rel = 0
    C.vtr_se_rel = 0
    C.rot_rel = 0
    C.kro_dy_rel = 0
    p = C.plan

    need = set(parts)
    C.t_xin = nc.dram_tensor("x_in", [D, TOK], F32, kind="ExternalInput").ap()
    C.t_xout = nc.dram_tensor("x_out", [D, TOK], F32, kind="ExternalOutput").ap()
    C.t_ident = nc.dram_tensor("ident", [128, 128], F32, kind="ExternalInput").ap()

    if {"ffn0a", "ffn0b"} & need:
        C.t_A2 = nc.dram_tensor("A2", [D, 512], F32, kind="ExternalInput").ap()
        C.t_B2 = nc.dram_tensor("B2", [512, D], F32, kind="ExternalInput").ap()
        C.t_wrT = nc.dram_tensor("WrT", [D, 8], F32, kind="ExternalInput").ap()
        C.t_br = nc.dram_tensor("br", [1, 8], F32, kind="ExternalInput").ap()
        C.t_esel = nc.dram_tensor("Esel", [8, 512], F32, kind="ExternalInput").ap()
    if {"ffn1a", "ffn1b"} & need:
        C.t_oneh8 = nc.dram_tensor("oneh8", [8, 1024], F32, kind="ExternalInput").ap()
        C.t_gwT = nc.dram_tensor("gwT", [D, 8], F32, kind="ExternalInput").ap()
        C.t_rb = nc.dram_tensor("rb", [1, 8], F32, kind="ExternalInput").ap()
        C.t_w1r = nc.dram_tensor("W1r", [8, 11, D, 128], F32, kind="ExternalInput").ap()
        C.t_w3r = nc.dram_tensor("W3r", [8, 11, D, 128], F32, kind="ExternalInput").ap()
        C.t_w2r = nc.dram_tensor("W2r", [8, NCH, 1408, 128], F32, kind="ExternalInput").ap()
        C.t_sw1r = nc.dram_tensor("sw1r", [22, D, 128], F32, kind="ExternalInput").ap()
        C.t_sw3r = nc.dram_tensor("sw3r", [22, D, 128], F32, kind="ExternalInput").ap()
        C.t_sw2r = nc.dram_tensor("sw2r", [NCH, 2816, 128], F32, kind="ExternalInput").ap()
    if "attn1" in need:
        C.t_qar = nc.dram_tensor("qar", [6, D, 128], F32, kind="ExternalInput").ap()
        C.t_kvar = nc.dram_tensor("kvar", [5, D, 128], F32, kind="ExternalInput").ap()
        C.t_qbr = nc.dram_tensor("qbr", [3, 768, 128], F32, kind="ExternalInput").ap()
        C.t_kvbr = nc.dram_tensor("kvbr", [4, 512, 128], F32, kind="ExternalInput").ap()
        C.t_wor = nc.dram_tensor("wor", [NCH, D, 128], F32, kind="ExternalInput").ap()
        C.t_qgam = nc.dram_tensor("qgam", [128, 6], F32, kind="ExternalInput").ap()
        C.t_kgam = nc.dram_tensor("kgam", [128, 4], F32, kind="ExternalInput").ap()
        C.t_cos1 = nc.dram_tensor("cos1", [64, TOK], F32, kind="ExternalInput").ap()
        C.t_sin1 = nc.dram_tensor("sin1", [64, TOK], F32, kind="ExternalInput").ap()
        C.t_cos2 = nc.dram_tensor("cos2", [128, T], F32, kind="ExternalInput").ap()
        C.t_sin2 = nc.dram_tensor("sin2", [128, T], F32, kind="ExternalInput").ap()
        C.t_perm1 = nc.dram_tensor("perm1", [64, 64], F32, kind="ExternalInput").ap()
        C.t_perm2 = nc.dram_tensor("perm2", [128, 128], F32, kind="ExternalInput").ap()
        C.t_trilneg = nc.dram_tensor("trilneg", [128, 128], F32, kind="ExternalInput").ap()
        C.t_maskq = nc.dram_tensor("maskq", [128, 4 * TC], F32, kind="ExternalInput").ap()
        C.t_coff = nc.dram_tensor("coff", [1, 32], mybir.dt.int32, kind="ExternalInput").ap()
        C.t_contrib = nc.dram_tensor("contrib", [1344, TOK], BF16).ap()
        C.t_gath = nc.dram_tensor("gath", [NCORES * 1344, TOK], BF16).ap()
        C.t_attc = nc.dram_tensor("attc", [256, T], BF16).ap()
        C.t_gath2 = nc.dram_tensor("gath2", [NCORES * 256, T], BF16).ap()
        C.t_gath2_h = C.t_gath2.tensor
    if "attn0" in need:
        C.t_qwr = nc.dram_tensor("qwr", [NCH, D, 128], F32, kind="ExternalInput").ap()
        C.t_kwr = nc.dram_tensor("kwr", [NCH, D, 128], F32, kind="ExternalInput").ap()
        C.t_vwr = nc.dram_tensor("vwr", [NCH, D, 128], F32, kind="ExternalInput").ap()
        C.t_owr = nc.dram_tensor("owr", [NCH, D, 128], F32, kind="ExternalInput").ap()
        C.t_qb = nc.dram_tensor("qb2", [128, NCH], F32, kind="ExternalInput").ap()
        C.t_kb = nc.dram_tensor("kb2", [128, NCH], F32, kind="ExternalInput").ap()
        C.t_vb = nc.dram_tensor("vb2", [128, NCH], F32, kind="ExternalInput").ap()
        C.t_ob = nc.dram_tensor("ob2", [128, NCH], F32, kind="ExternalInput").ap()
        C.t_kvpart = nc.dram_tensor("kvpart", [D, 132], F32).ap()
        C.t_kvred = nc.dram_tensor("kvred", [D, 132], F32).ap()

    ctx = ExitStack()

    def sb(name, shape, dt):
        h = ctx.enter_context(nc.sbuf_tensor(name, shape, dt))
        return h[(slice(None),) * len(shape)]

    C.x = sb("x_sb", [128, NCH, TOK], F32)
    C.ln = sb("ln_sb", [128, NCH, TOK], BF16)
    C.arena_a = sb("arena_a", [128, 16384], BF16)
    C.arena_b = sb("arena_b", [128, 16384], BF16)
    C.arena_f = sb("arena_f", [128, 3144], F32)
    C.wslab = sb("wslab", [128, 8192], BF16)
    C.sq_bf = sb("sq_bf", [128, TC], BF16)
    C.f32scr = C.arena_f[:, 0:TC]
    C.f32scr2 = C.arena_f[:, 2632:3144]
    C.consts = sb("consts", [128, 16], F32)
    C.row_a = sb("row_a", [1, TC], F32)
    C.row_b = sb("row_b", [1, TC], F32)
    C.stat_m = C.arena_f[0:1, 520:1544]
    C.stat_r = C.arena_f[0:1, 1544:2568]
    C.row8 = sb("row8", [1, 8], F32)
    C.ones_f32 = sb("ones_f32", [128, 1], F32)
    C.ones_inv2048_f32 = sb("ones_i2048f", [128, 1], F32)
    C.ones_inv2048_bf = sb("ones_i2048b", [128, 1], BF16)
    C.ones1_f32 = sb("ones1_f32", [1, 128], F32)
    C.ones1_bf = sb("ones1_bf", [1, 128], BF16)
    C.ones11_f32 = sb("ones11_f32", [1, 1], F32)
    C.ident_f32 = sb("ident_f32", [128, 128], F32)
    C.ident_bf = sb("ident_bf", [128, 128], BF16)

    if {"ffn0a", "ffn0b", "ffn1a", "ffn1b"} & need:
        C.rw_sb = sb("rw_sb", [128, NCH, 8], F32)
        C.gate_wsb = sb("gate_wsb", [128, 8], F32)
        C.gate_bb = sb("gate_bb", [128, 8], F32)
        C.g8 = sb("g8", [128, 8], F32)
        C.g8b = sb("g8b", [128, 8], F32)
        C.g8c = sb("g8c", [128, 8], F32)
        C.g8d = sb("g8d", [128, 8], F32)
        C.colscr = sb("colscr", [128, 4], F32)
        if DEBUG_DUMP:
            C.dbg_sb = C.arena_b[:, 12288:12416].bitcast(F32)
        C.gT = sb("gT", [8, TOK], BF16)
        sel8 = sb("sel8", [8, 1024], BF16)
        C.esel_sb = sel8[0:8, 0:512]
        C.oneh8_sb = sel8[0:8, 0:1024]
    if "attn0" in need:
        C.qb_sb = sb("qb_sb", [128, NCH], F32)
        C.kb_sb = sb("kb_sb", [128, NCH], F32)
        C.vb_sb = sb("vb_sb", [128, NCH], F32)
        C.ob_sb = sb("ob_sb", [128, NCH], F32)
        C.kvbf_sb = C.arena_f[:, 520:1576].bitcast(BF16).rearrange(
            "p (h f) -> p h f", h=NCH)

    if "attn1" in need:
        C.kra_sb = C.ln.rearrange("p c t -> p (c t)")[0:65, 6144:14336]
        C.qtn_sb = sb("qtn_sb", [128, 2, TC], BF16)
        C.qra0 = sb("qra0", [65, TC], BF16)
        C.qra1 = sb("qra1", [65, TC], BF16)
        C.nrm_bf = C.arena_f[:, 520:1032].bitcast(BF16)
        C.pt_bf = C.arena_f[:, 1032:1544].bitcast(BF16)
        C.qr_bf = C.arena_f[:, 1544:1800].bitcast(BF16)
        C.attno_bf = C.arena_f[:, 1800:2056].bitcast(BF16)
        C.kr_bf = C.arena_f[:, 2056:2312].bitcast(BF16)
        C.kvb_sb = C.wslab[:, 6144:8192].rearrange(
            "p (o c m) -> p o c m", o=4, c=4)
        C.qbr_sb = C.wslab[:, 0:2304].rearrange(
            "p (o c m) -> p o c m", o=3, c=6)
        C.perm1_sb = sb("perm1_sb", [64, 64], BF16)
        C.perm2_sb = sb("perm2_sb", [128, 128], BF16)
        C.trilneg_sb = sb("trilneg_sb", [128, 128], F32)
        C.maskq_sb = C.ln.rearrange("p c t -> p (c t)")[:, 14336:16384]
        C.qgam_sb = sb("qgam_sb", [128, 6], F32)
        C.kgam_sb = sb("kgam_sb", [128, 4], F32)
        C.coff_sb = sb("coff_sb", [1, 32], mybir.dt.int32)
    C.colscr2 = sb("colscr2", [128, 2], F32)
    C.ones_bf = sb("ones_bf", [128, 1], BF16)

    C.psum = [ctx.enter_context(nc.psum_tensor(f"bank{i}", [128, 512], F32))[:, :]
              for i in range(8)]

    sems = {}
    for name in ("pe", "ve", "se", "ge", "dg", "dy", "cc"):
        sems[name] = ctx.enter_context(nc.semaphore(name))

    const_vals = {
        "ln_eps": 1e-5, "rms_eps": 1e-6, "z_eps": 1e-6,
        "inv768": 1.0 / 768.0, "inv512": 1.0 / 512.0,
        "qscale": float(1.0 / np.sqrt(192.0)),
    }
    C.const_idx = {k: i for i, k in enumerate(const_vals)}

    def init_fn(e):
        for k, v in const_vals.items():
            e.memset(C.consts[:, C.const_idx[k]:C.const_idx[k] + 1], v)
        e.memset(C.ones_f32[:], 1.0)
        e.memset(C.ones_inv2048_f32[:], 1.0 / 2048.0)
        e.memset(C.ones_inv2048_bf[:], 1.0 / 2048.0)
        e.memset(C.ones1_f32[:], 1.0)
        e.memset(C.ones1_bf[:], 1.0)
        e.memset(C.ones_bf[:], 1.0)
        return e.memset(C.ones11_f32[:], 1.0)
    p.add("g", init_fn, inc=[("ge", 1)])
    p.add("g", lambda e: e.dma_start(out=C.ident_f32[:], in_=C.t_ident),
          inc=[("dg", 16)])
    p.add("g", lambda e: e.dma_start(out=C.ident_bf[:], in_=C.t_ident),
          inc=[("dg", 16)])
    if "attn1" in need:
        p.add("g", lambda e: e.dma_start(out=C.perm1_sb, in_=C.t_perm1), inc=[("dg", 16)])
        p.add("g", lambda e: e.dma_start(out=C.perm2_sb, in_=C.t_perm2), inc=[("dg", 16)])
        p.add("g", lambda e: e.dma_start(out=C.trilneg_sb, in_=C.t_trilneg), inc=[("dg", 16)])
        p.add("g", lambda e: e.dma_start(out=C.qgam_sb, in_=C.t_qgam), inc=[("dg", 16)])
        p.add("g", lambda e: e.dma_start(out=C.kgam_sb, in_=C.t_kgam), inc=[("dg", 16)])
    if "attn0" in need:
        p.add("g", lambda e: e.dma_start(out=C.qb_sb, in_=C.t_qb), inc=[("dg", 16)])
        p.add("g", lambda e: e.dma_start(out=C.kb_sb, in_=C.t_kb), inc=[("dg", 16)])
        p.add("g", lambda e: e.dma_start(out=C.vb_sb, in_=C.t_vb), inc=[("dg", 16)])
        p.add("g", lambda e: e.dma_start(out=C.ob_sb, in_=C.t_ob), inc=[("dg", 16)])
    fence = [("ge", p.val("ge")), ("dg", p.val("dg"))]
    p.add("t", lambda e: e.nop(), wait=fence)
    p.add("v", lambda e: e.nop(), wait=fence, nops=0)
    p.add("s", lambda e: e.nop(), wait=fence, nops=0)
    p.add("y", lambda e: e.dma_start(out=C.x[:], in_=C.t_xin.rearrange(
        "(c p) t -> p c t", p=128)), inc=[("dy", 16)])
    C.x_ready_dy = p.val("dy")
    C.x_ready_ve = 0
    C.ln_ready_ve = 0
    C.ln_stats_ve = 0

    for part in parts:
        emit_ln(C)
        if part in ("ffn0a", "ffn0b"):
            emit_ffn0(C)
        elif part == "attn0":
            emit_attn0(C)
        elif part in ("ffn1a", "ffn1b"):
            emit_ffn1(C)
        elif part == "attn1":
            emit_attn1(C)
        else:
            raise NotImplementedError(part)

    p.add("y", lambda e: e.dma_start(out=C.t_xout.rearrange(
        "(c p) t -> p c t", p=128), in_=C.x[:]),
        wait=[("ve", C.x_ready_ve)], inc=[("dy", 16)])
    p.add("y", lambda e: e.nop(), wait=[("dy", p.val("dy"))])

    _realize(nc, p, sems)
    ctx.close()
    return nc


def _prep(inputs):
    g = {}
    x = np.asarray(inputs["L0_attn_hidden_states"], np.float32)
    g["xT"] = np.ascontiguousarray(x[0].T)
    A = np.asarray(inputs["L0_ffn_A"], np.float32)
    g["A2"] = np.ascontiguousarray(A.transpose(2, 0, 1).reshape(D, 512))
    B = np.asarray(inputs["L0_ffn_B"], np.float32)
    g["B2"] = np.ascontiguousarray(B.transpose(0, 2, 1).reshape(512, D))
    g["WrT"] = np.ascontiguousarray(np.asarray(inputs["L0_ffn_Wr"], np.float32).T)
    g["br"] = np.asarray(inputs["L0_ffn_br"], np.float32).reshape(1, 8)
    esel = np.zeros((8, 512), np.float32)
    for e in range(8):
        esel[e, e * 64:(e + 1) * 64] = 1.0
    g["Esel"] = esel
    oneh8 = np.zeros((8, 1024), np.float32)
    for e in range(8):
        oneh8[e, e * 128:(e + 1) * 128] = 1.0
    g["oneh8"] = oneh8
    g["ident"] = np.eye(128, dtype=np.float32)

    def repack(wT):
        return np.ascontiguousarray(
            wT.reshape(wT.shape[0], wT.shape[1] // 128, 128).transpose(1, 0, 2))

    for nm, key in (("qwr", "L0_attn_q_w"), ("kwr", "L0_attn_k_w"),
                    ("vwr", "L0_attn_v_w"), ("owr", "L0_attn_o_w")):
        w = np.asarray(inputs[key], np.float32)
        g[nm] = repack(np.ascontiguousarray(w.T))
    for nm, key in (("qb2", "L0_attn_q_b"), ("kb2", "L0_attn_k_b"),
                    ("vb2", "L0_attn_v_b"), ("ob2", "L0_attn_o_b")):
        b = np.asarray(inputs[key], np.float32)
        g[nm] = np.ascontiguousarray(b.reshape(NCH, 128).T)

    g["gwT"] = np.ascontiguousarray(
        np.asarray(inputs["L1_ffn_gate_w"], np.float32).T)
    g["rb"] = np.asarray(inputs["L1_ffn_router_bias"], np.float32).reshape(1, 8)
    W1 = np.asarray(inputs["L1_ffn_W1"], np.float32)
    g["W1r"] = np.ascontiguousarray(
        W1.reshape(8, D, 11, 128).transpose(0, 2, 1, 3))
    W3 = np.asarray(inputs["L1_ffn_W3"], np.float32)
    g["W3r"] = np.ascontiguousarray(
        W3.reshape(8, D, 11, 128).transpose(0, 2, 1, 3))
    W2 = np.asarray(inputs["L1_ffn_W2"], np.float32)
    g["W2r"] = np.ascontiguousarray(
        W2.reshape(8, 1408, NCH, 128).transpose(0, 2, 1, 3))
    g["sw1r"] = np.ascontiguousarray(
        np.asarray(inputs["L1_ffn_sw1"], np.float32).T.reshape(
            D, 22, 128).transpose(1, 0, 2))
    g["sw3r"] = np.ascontiguousarray(
        np.asarray(inputs["L1_ffn_sw3"], np.float32).T.reshape(
            D, 22, 128).transpose(1, 0, 2))
    g["sw2r"] = np.ascontiguousarray(
        np.asarray(inputs["L1_ffn_sw2"], np.float32).T.reshape(
            2816, NCH, 128).transpose(1, 0, 2))

    w_qa = np.asarray(inputs["L1_attn_w_qa"], np.float32)
    g["qar"] = np.ascontiguousarray(
        w_qa.reshape(D, 6, 128).transpose(1, 0, 2))
    w_kva = np.asarray(inputs["L1_attn_w_kva"], np.float32)
    kvar = np.zeros((5, D, 128), np.float32)
    kvar[:4] = w_kva[:, 0:512].reshape(D, 4, 128).transpose(1, 0, 2)
    kvar[4, :, 0:64] = w_kva[:, 512:576]
    g["kvar"] = kvar
    w_qb = np.asarray(inputs["L1_attn_w_qb"], np.float32)
    w_kvb = np.asarray(inputs["L1_attn_w_kvb"], np.float32)
    for c in range(NCORES):
        a, b = 2 * c * 192, (2 * c + 1) * 192
        qb_s = np.concatenate([w_qb[:, a:a + 128], w_qb[:, b:b + 128],
                               w_qb[:, a + 128:a + 192],
                               w_qb[:, b + 128:b + 192]], axis=1)
        g[f"qbr_{c}"] = np.ascontiguousarray(
            qb_s.reshape(768, 3, 128).transpose(1, 0, 2))
        kvb_s = w_kvb[:, 2 * c * 256:(2 * c + 2) * 256]
        g[f"kvbr_{c}"] = np.ascontiguousarray(
            kvb_s.reshape(512, 4, 128).transpose(1, 0, 2))
    w_o = np.asarray(inputs["L1_attn_w_o"], np.float32)
    g["wor"] = np.ascontiguousarray(
        w_o.reshape(D, NCH, 128).transpose(1, 0, 2))
    g["qgam"] = np.ascontiguousarray(
        np.asarray(inputs["L1_attn_q_a_norm_w"], np.float32).reshape(6, 128).T)
    g["kgam"] = np.ascontiguousarray(
        np.asarray(inputs["L1_attn_kv_a_norm_w"], np.float32).reshape(4, 128).T)
    inv_freq = 1.0 / (10000.0 ** (np.arange(0, 64, 2, dtype=np.float32) / 64))
    pos = np.arange(T, dtype=np.float32)
    emb = np.concatenate([np.outer(pos, inv_freq)] * 2, axis=1)
    cosT = np.ascontiguousarray(np.cos(emb).T.astype(np.float32))
    sinT = np.ascontiguousarray(np.sin(emb).T.astype(np.float32))
    for c in range(NCORES):
        g[f"cos1_{c}"] = np.ascontiguousarray(cosT[:, c * TOK:(c + 1) * TOK])
        g[f"sin1_{c}"] = np.ascontiguousarray(sinT[:, c * TOK:(c + 1) * TOK])
    g["cos2"] = np.ascontiguousarray(np.concatenate([cosT, cosT], axis=0))
    g["sin2"] = np.ascontiguousarray(np.concatenate([sinT, sinT], axis=0))
    P = np.zeros((64, 64), np.float32)
    for i in range(32):
        P[2 * i, 2 * i + 1] = -1.0
        P[2 * i + 1, 2 * i] = 1.0
    g["perm1"] = np.ascontiguousarray(P.T)
    P2 = np.zeros((128, 128), np.float32)
    P2[0:64, 0:64] = P.T
    P2[64:128, 64:128] = P.T
    g["perm2"] = P2
    tril = np.zeros((128, 128), np.float32)
    for q in range(128):
        tril[q, q + 1:] = -1e30
    g["trilneg"] = tril
    maskq = np.zeros((128, 4, TC), np.float32)
    for r_ in range(4):
        for k_ in range(128):
            lo = k_ + 128 * r_
            if lo < TC:
                maskq[k_, r_, lo:] = 1.0
    g["maskq"] = np.ascontiguousarray(maskq.reshape(128, 4 * TC))
    for c in range(NCORES):
        coff = np.zeros((1, 32), np.int32)
        for ch in range(NCH):
            for tc_ in range(NTC):
                coff[0, ch * 2 + tc_] = ch * 128 * T + tc_ * TC + c * TOK
        g[f"coff_{c}"] = coff
    return g


def _inputs_for(parts, g):
    m = {"ident": g["ident"]}
    if {"ffn1a", "ffn1b"} & set(parts):
        m["oneh8"] = g["oneh8"]
    if {"ffn0a", "ffn0b"} & set(parts):
        for k in ("A2", "B2", "WrT", "br", "Esel"):
            m[k] = g[k]
    if "attn0" in set(parts):
        for k in ("qwr", "kwr", "vwr", "owr", "qb2", "kb2", "vb2", "ob2"):
            m[k] = g[k]
    if {"ffn1a", "ffn1b"} & set(parts):
        for k in ("gwT", "rb", "W1r", "W3r", "W2r", "sw1r", "sw3r", "sw2r"):
            m[k] = g[k]
    return m


def _inputs_for_core(parts, g, c):
    m = {}
    if "attn1" in set(parts):
        for k in ("qar", "kvar", "wor", "qgam", "kgam", "cos2", "sin2",
                  "perm1", "perm2", "trilneg", "maskq"):
            m[k] = g[k]
        m["qbr"] = g[f"qbr_{c}"]
        m["kvbr"] = g[f"kvbr_{c}"]
        m["cos1"] = g[f"cos1_{c}"]
        m["sin1"] = g[f"sin1_{c}"]
        m["coff"] = g[f"coff_{c}"]
    return m


def kernel(**inputs):
    g = _prep(inputs)
    xT = g["xT"]
    shards = [np.ascontiguousarray(xT[:, c * TOK:(c + 1) * TOK])
              for c in range(NCORES)]

    for parts in LAUNCH_GROUPS:
        nc = build_group(parts)
        base = _inputs_for(parts, g)
        in_maps = [dict(base, x_in=shards[c], **_inputs_for_core(parts, g, c))
                   for c in range(NCORES)]
        res = run_bass_kernel_spmd(nc, in_maps, core_ids=list(range(NCORES)))
        shards = [np.ascontiguousarray(res.results[c]["x_out"])
                  for c in range(NCORES)]

    out = np.concatenate(shards, axis=1)
    return np.ascontiguousarray(out.T).reshape(1, T, D)


# version 04
# speedup vs baseline: 1.0604x; 1.0528x over previous
import numpy as np
from contextlib import ExitStack

import concourse.bass as bass
import concourse.mybir as mybir
from concourse.bass_utils import run_bass_kernel_spmd
from concourse.expressions_rust import RuntimeValue as _RV

F32 = mybir.dt.float32
BF16 = mybir.dt.bfloat16
AF = mybir.ActivationFunctionType
ALU = mybir.AluOpType
AXX = mybir.AxisListType.X

NCORES = 8
T = 8192
TOK = T // NCORES
D = 2048
NCH = D // 128
TC = 512
NTC = TOK // TC

SUBLAYERS = ("ffn0a", "attn0", "ffn0b", "ffn1a", "attn1", "ffn1b")

LAUNCH_GROUPS = [list(SUBLAYERS)]


class Plan:
    ENGS = ("t", "v", "s", "g", "y")

    def __init__(self):
        self.q = {e: [] for e in self.ENGS}
        self.cnt = {}

    def val(self, sem):
        return self.cnt.get(sem, 0)

    def add(self, eng, fn, wait=(), inc=(), nops=1):
        wait = tuple((s_, v) for (s_, v) in wait if v > 0)
        if eng == "v":
            assert not any(s_ == "ve" for s_, _ in inc)
            self.cnt["ve"] = self.cnt.get("ve", 0) + nops
        elif eng == "s":
            assert not any(s_ == "se" for s_, _ in inc)
            self.cnt["se"] = self.cnt.get("se", 0) + nops
        for s_, n in inc:
            self.cnt[s_] = self.cnt.get(s_, 0) + n
        self.q[eng].append((wait, fn, tuple(inc), nops))


class Ctx:
    pass


class _SerialEng:

    _PASS = {"wait_ge", "nop", "maybe_wait_ge", "maybe_nop_then_inc"}

    def __init__(self, eng, sem):
        self._eng = eng
        self._sem = sem
        self._n = 0

    def __getattr__(self, name):
        attr = getattr(self._eng, name)
        if name in self._PASS or not callable(attr):
            return attr

        def wrapped(*a, **k):
            if self._n > 0:
                self._eng.wait_ge(self._sem, self._n)
            inst = attr(*a, **k)
            inst.then_inc(self._sem, 1)
            self._n += 1
            return inst
        return wrapped


def _realize(nc, plan, sems):
    block_cm = nc.Block()
    block = block_cm.__enter__()

    def runner(engname):
        def run(eng):
            serial = engname in ("v", "s")
            if serial:
                eng = _SerialEng(eng, sems["ve" if engname == "v" else "se"])
            for waits, fn, incs, nops in plan.q[engname]:
                for s_, v in waits:
                    eng.wait_ge(sems[s_], v)
                before = eng._n if serial else 0
                inst = fn(eng)
                if serial:
                    assert eng._n - before == nops, (
                        f"{engname} item emitted {eng._n - before} ops, "
                        f"declared nops={nops}")
                else:
                    for s_, n in incs:
                        inst.then_inc(sems[s_], n)
        return run

    block.tensor(runner("t"))
    block.vector(runner("v"))
    block.scalar(runner("s"))
    block.gpsimd(runner("g"))
    block.sync(runner("y"))
    block_cm.__exit__(None, None, None)


def mm(C, out, lhsT, rhs, start=True, stop=True, wait=(), inc=()):
    C.plan.add("t",
               (lambda out=out, lhsT=lhsT, rhs=rhs, start=start, stop=stop:
                lambda e: e.matmul(out, lhsT, rhs, start=start, stop=stop))(),
               wait=wait, inc=inc)


def ts(i, n=128):
    return slice(i * n, (i + 1) * n)


def cstcol(C, name):
    j = C.const_idx[name]
    return C.consts[:, j:j + 1]


def cst1(C, name):
    j = C.const_idx[name]
    return C.consts[0:1, j:j + 1]


def emit_ln(C):
    p = C.plan
    for tc in range(NTC):
        stat = C.psum[7]
        xw = [("ve", C.x_ready_ve), ("dy", C.x_ready_dy)]
        for c in range(NCH):
            buf = C.sq_bf[:, 0:TC]
            w = list(xw) if (tc == 0 and c == 0) else []
            if C.ln_marks:
                w.append(("pe", C.ln_marks[-1]))
            p.add("s", (lambda buf=buf, c=c, tc=tc: lambda e: e.activation(
                buf, C.x[:, c, ts(tc, TC)], AF.Square))(),
                wait=w)
            w2 = list(xw) if c == 0 else []
            if c == 0:
                w2.append(("ve", C.ln_stats_ve))
            mm(C, stat[0:1, 0:TC], C.ones_inv2048_f32, C.x[:, c, ts(tc, TC)],
               start=(c == 0), stop=(c == NCH - 1), wait=w2)
            mm(C, stat[64:65, 0:TC], C.ones_inv2048_bf, buf,
               start=(c == 0), stop=(c == NCH - 1),
               wait=[("se", p.val("se"))], inc=[("pe", 1)])
            C.ln_marks.append(p.val("pe"))
        pe_stats = p.val("pe")

        def stats_fn(tc=tc, stat=stat):
            def f(e):
                msl = C.stat_m[0:1, ts(tc, TC)]
                e.tensor_copy(msl, stat[0:1, 0:TC])
                e.tensor_tensor(C.row_a[0:1, 0:TC], msl, msl, ALU.mult)
                return e.tensor_tensor(C.row_b[0:1, 0:TC], stat[64:65, 0:TC],
                                       C.row_a[0:1, 0:TC], ALU.subtract)
            return f
        p.add("v", stats_fn(), wait=[("pe", pe_stats)], nops=3)
        C.ln_stats_ve = p.val("ve")
        p.add("s", (lambda: lambda e: e.activation(
            C.row_a[0:1, 0:TC], C.row_b[0:1, 0:TC], AF.Sqrt,
            bias=cst1(C, "ln_eps")))(),
            wait=[("ve", p.val("ve"))])
        p.add("v", (lambda tc=tc: lambda e: e.reciprocal(
            C.stat_r[0:1, ts(tc, TC)], C.row_a[0:1, 0:TC]))(),
            wait=[("se", p.val("se"))])
        ve_rstd = p.val("ve")

        mb, rb = C.psum[5], C.psum[6]
        mm(C, mb[:, 0:TC], C.ones1_f32, C.stat_m[0:1, ts(tc, TC)],
           wait=[("ve", ve_rstd), ("se", C.b56_se)])
        mm(C, rb[:, 0:TC], C.ones1_f32, C.stat_r[0:1, ts(tc, TC)],
           inc=[("pe", 1)])
        pe_bcast = p.val("pe")

        for c in range(NCH):
            def norm_fn(c=c, tc=tc, mb=mb, rb=rb):
                def f(e):
                    e.tensor_tensor(C.f32scr[:, 0:TC], C.x[:, c, ts(tc, TC)],
                                    mb[:, 0:TC], ALU.subtract)
                    return e.tensor_tensor(C.ln[:, c, ts(tc, TC)],
                                           C.f32scr[:, 0:TC], rb[:, 0:TC],
                                           ALU.mult)
                return f
            p.add("v", norm_fn(),
                  wait=[("pe", pe_bcast)] if c == 0 else (), nops=2)
    C.ln_ready_ve = p.val("ve")


def emit_gates(C, which):
    p = C.plan
    wT = C.t_wrT if which == "l0" else C.t_gwT
    bias_t = C.t_br if which == "l0" else C.t_rb

    p.add("g", (lambda wT=wT: lambda e: e.dma_start(
        out=C.rw_sb[:], in_=wT.rearrange("(c p) e -> p c e", p=128)))(),
        wait=[("pe", p.val("pe"))], inc=[("dg", 16)])
    p.add("g", (lambda bias_t=bias_t: lambda e: e.dma_start(
        out=C.row8[0:1, :], in_=bias_t))(), inc=[("dg", 16)])
    dg0 = p.val("dg")

    for c in range(NCH):
        mm(C, C.psum[7][0:1, 0:8], C.ones_f32, C.rw_sb[:, c, :],
           start=(c == 0), stop=(c == NCH - 1),
           wait=[("dg", dg0), ("ve", C.ln_ready_ve)] if c == 0 else (),
           inc=[("pe", 1)] if c == NCH - 1 else ())
    pe_ws = p.val("pe")
    p.add("s", lambda e: e.copy(C.row_a[0:1, 0:8], C.psum[7][0:1, 0:8]),
          wait=[("pe", pe_ws)])
    se_ws = p.val("se")
    mm(C, C.psum[5][:, 0:8], C.ones1_f32, C.row_a[0:1, 0:8],
       wait=[("se", se_ws), ("ve", C.ln_ready_ve)])
    mm(C, C.psum[5][:, 8:16], C.ones1_f32, C.row8[0:1, :],
       wait=[("dg", dg0)], inc=[("pe", 1)])
    pe_bc = p.val("pe")
    p.add("s", lambda e: e.copy(C.gate_wsb[:, 0:8], C.psum[5][:, 0:8]),
          wait=[("pe", pe_bc)])
    p.add("s", lambda e: e.copy(C.gate_bb[:, 0:8], C.psum[5][:, 8:16]))
    C.b56_se = p.val("se")
    se_bc = p.val("se")

    for tt in range(TOK // 128):
        lg = C.psum[4]
        for c in range(NCH):
            w = [("se", p.val("se"))] if c == 0 else []
            mm(C, lg[:, 0:8], C.x[:, c, ts(tt)], C.rw_sb[:, c, :],
               start=(c == 0), stop=(c == NCH - 1), wait=w)
        mm(C, lg[:, 8:9], C.stat_m[0:1, ts(tt)], C.ones11_f32)
        mm(C, lg[:, 9:10], C.stat_r[0:1, ts(tt)], C.ones11_f32, inc=[("pe", 1)])
        pe_lg = p.val("pe")

        def logits_fn(which=which, lg=lg):
            def f(e):
                mcol = C.colscr[:, 0:1]
                rcol = C.colscr[:, 1:2]
                e.tensor_copy(mcol, lg[:, 8:9])
                e.tensor_copy(rcol, lg[:, 9:10])
                e.tensor_scalar_mul(C.g8, C.gate_wsb[:, 0:8], mcol)
                e.tensor_tensor(C.g8, lg[:, 0:8], C.g8, ALU.subtract)
                e.tensor_scalar_mul(C.g8, C.g8, rcol)
                if which == "l0":
                    e.tensor_tensor(C.g8, C.g8, C.gate_bb[:, 0:8], ALU.add)
                    e.reduce_max(mcol, C.g8, axis=AXX)
                    e.tensor_single_scalar(C.g8b, C.g8, mcol, ALU.is_equal)
                    return e.tensor_scalar_mul(rcol, mcol, -1.0)
                else:
                    return e.tensor_copy(C.g8d, C.g8)
            return f
        p.add("v", logits_fn(), wait=[("pe", pe_lg), ("se", se_bc)],
              nops=(9 if which == "l0" else 6))
        ve1 = p.val("ve")

        if which == "l0":
            p.add("s", lambda e: e.activation(
                C.g8c, C.g8, AF.Exp, bias=C.colscr[:, 1:2]),
                wait=[("ve", ve1)])
            se2 = p.val("se")

            def fin(e):
                scol = C.colscr[:, 2:3]
                e.tensor_reduce(scol, C.g8c, AXX, ALU.add)
                e.reciprocal(scol, scol)
                e.tensor_scalar_mul(C.g8b, C.g8b, scol)
                return e.tensor_scalar_mul(C.g8b, C.g8b, 0.5 / 64.0)
            p.add("v", fin, wait=[("se", se2)], nops=4)
        else:
            p.add("s", lambda e: e.activation(C.g8c, C.g8d, AF.Sigmoid),
                  wait=[("ve", ve1)])
            se2 = p.val("se")

            def fin(tt=tt):
                def f(e):
                    m1c = C.colscr[:, 0:1]
                    m2c = C.colscr[:, 1:2]
                    dcol = C.colscr[:, 2:3]
                    e.tensor_tensor(C.g8, C.g8c, C.gate_bb[:, 0:8], ALU.add)
                    e.reduce_max(m1c, C.g8, axis=AXX)
                    e.tensor_single_scalar(C.g8b, C.g8, m1c, ALU.is_equal)
                    e.tensor_scalar(C.g8d, C.g8b, 1e30, None, ALU.mult)
                    e.tensor_tensor(C.g8d, C.g8, C.g8d, ALU.subtract)
                    if DEBUG_DUMP and tt == 0:
                        e.tensor_copy(C.dbg_sb[:, 0:8], C.g8c)
                        e.tensor_copy(C.dbg_sb[:, 8:16], C.g8)
                        e.tensor_copy(C.dbg_sb[:, 16:24], C.g8b)
                        e.tensor_copy(C.dbg_sb[:, 24:32], C.g8d)
                    e.reduce_max(m2c, C.g8d, axis=AXX)
                    e.tensor_single_scalar(C.g8d, C.g8d, m2c, ALU.is_equal)
                    e.tensor_tensor(C.g8b, C.g8b, C.g8d, ALU.add)
                    e.tensor_tensor(C.g8b, C.g8b, C.g8c, ALU.mult)
                    if DEBUG_DUMP and tt == 0:
                        e.tensor_copy(C.dbg_sb[:, 32:40], C.g8d)
                        e.tensor_copy(C.dbg_sb[:, 40:48], C.g8b)
                    e.tensor_reduce(dcol, C.g8b, AXX, ALU.add)
                    e.tensor_scalar_max(dcol, dcol, 1e-12)
                    e.reciprocal(dcol, dcol)
                    e.tensor_scalar_mul(C.g8b, C.g8b, dcol)
                    if DEBUG_DUMP and tt == 0:
                        e.tensor_copy(C.dbg_sb[:, 48:49], dcol)
                    return e.tensor_scalar_mul(C.g8b, C.g8b, 0.5)
                return f
            p.add("v", fin(), wait=[("se", se2)],
                  nops=14 + (7 if (DEBUG_DUMP and tt == 0) else 0))
        ve_g = p.val("ve")

        p.add("t", (lambda: lambda e: e.transpose(
            C.psum[4][0:8, 16:144], C.g8b, C.ident_f32))(),
            wait=[("ve", ve_g)], inc=[("pe", 1)])
        p.add("s", (lambda tt=tt: lambda e: e.copy(
            C.gT[0:8, ts(tt)], C.psum[4][0:8, 16:144]))(),
            wait=[("pe", p.val("pe"))])
    C.gates_se = p.val("se")


def emit_ffn0(C):
    p = C.plan
    emit_gates(C, "l0")

    a2_sb = C.arena_a[:, 0:8192].rearrange("p (c r) -> p c r", c=NCH)
    b2_sb = C.arena_a[:, 8192:16384].rearrange("p (r o) -> p r o", r=4)
    gb_sb = C.arena_b[:, 0:2048].rearrange("p (r t) -> p r t", r=4)
    hp_sb = C.arena_b[:, 2048:4096].rearrange("p (r t) -> p r t", r=4)

    p.add("g", (lambda a2_sb=a2_sb: lambda e: e.dma_start(
        out=a2_sb, in_=C.t_A2.rearrange("(c p) r -> p c r", p=128)))(),
        wait=[("pe", p.val("pe"))], inc=[("dg", 16)])
    p.add("g", (lambda b2_sb=b2_sb: lambda e: e.dma_start(
        out=b2_sb, in_=C.t_B2.rearrange("(c p) o -> p c o", p=128)))(),
        inc=[("dg", 16)])
    p.add("g", lambda e: e.dma_start(out=C.esel_sb[0:8, :], in_=C.t_esel),
          inc=[("dg", 16)])
    dgA = p.val("dg")

    for tc in range(NTC):
        for rt in range(4):
            w = [("se", p.val("se"))]
            if tc == 0 and rt == 0:
                w += [("se", C.gates_se), ("dg", dgA)]
            mm(C, C.psum[4][:, 0:TC], C.esel_sb[0:8, ts(rt)],
               C.gT[0:8, ts(tc, TC)], wait=w, inc=[("pe", 1)])
            p.add("s", (lambda rt=rt, gb_sb=gb_sb: lambda e: e.copy(
                gb_sb[:, rt, 0:TC], C.psum[4][:, 0:TC]))(),
                wait=[("pe", p.val("pe"))])
        se_gb = p.val("se")

        for rt in range(4):
            hb = C.psum[rt % 2]
            for c in range(NCH):
                w = []
                if rt == 0 and c == 0:
                    w = [("ve", C.ln_ready_ve)]
                if rt >= 2 and c == 0:
                    w = [("ve", p.val("ve"))]
                mm(C, hb[:, 0:TC], a2_sb[:, c, ts(rt)],
                   C.ln[:, c, ts(tc, TC)], start=(c == 0), stop=(c == NCH - 1),
                   wait=w, inc=([("pe", 1)] if c == NCH - 1 else ()))
            p.add("v", (lambda rt=rt, hb=hb, hp_sb=hp_sb, gb_sb=gb_sb:
                        lambda e: e.tensor_tensor(
                            hp_sb[:, rt, 0:TC], hb[:, 0:TC],
                            gb_sb[:, rt, 0:TC], ALU.mult))(),
                  wait=[("pe", p.val("pe")), ("se", se_gb)])
        ve_hp = p.val("ve")

        for ot in range(NCH):
            yb = C.psum[2 + (ot % 2)]
            for rt in range(4):
                w = []
                if ot == 0 and rt == 0:
                    w = [("ve", ve_hp)]
                if ot >= 2 and rt == 0:
                    w = [("ve", p.val("ve"))]
                mm(C, yb[:, 0:TC], b2_sb[:, rt, ts(ot)],
                   hp_sb[:, rt, 0:TC], start=(rt == 0), stop=(rt == 3),
                   wait=w, inc=([("pe", 1)] if rt == 3 else ()))
            p.add("v", (lambda ot=ot, tc=tc, yb=yb: lambda e: e.tensor_add(
                C.x[:, ot, ts(tc, TC)], C.x[:, ot, ts(tc, TC)], yb[:, 0:TC]))(),
                wait=[("pe", p.val("pe"))])
    C.x_ready_ve = p.val("ve")


def emit_attn0(C):
    p = C.plan
    qp_sb = C.arena_a.rearrange("p (c t) -> p c t", c=NCH)
    kp_sb = C.arena_b.rearrange("p (c t) -> p c t", c=NCH)
    v_ch = C.sq_bf
    kv_sb = C.arena_f[:, 520:2632].rearrange("p (h f) -> p h f", h=16)
    zb_sb = C.arena_f[:, 2632:3144]
    wsl = [C.wslab[:, ts(i, 2048)].rearrange("p (c m) -> p c m", c=NCH)
           for i in range(2)]
    kp_tm = C.arena_f[:, 0:256].bitcast(BF16).rearrange(
        "p (i m) -> p i m", i=4)
    v_tm = C.arena_f[:, 256:520].bitcast(BF16).rearrange(
        "p (i m) -> p i m", i=4)

    p.add("v", (lambda kv_sb=kv_sb: lambda e: e.memset(kv_sb, 0.0))(),
          wait=[("dg", C.plan.val("dg"))])
    ve_kv0 = p.val("ve")

    bank_rel = {}

    pb4 = C.psum[4].bitcast(BF16)
    pb5 = C.psum[5].bitcast(BF16)

    def proj(kind, wtensor, bias_sb):
        for ot in range(NCH):
            buf = ot % 2
            p.add("g", (lambda ot=ot, buf=buf, wtensor=wtensor: lambda e:
                        e.dma_start(out=wsl[buf],
                                    in_=wtensor[ot].rearrange(
                                        "(c p) m -> p c m", p=128)))(),
                  wait=[("pe", C.wslab_rel.get(buf, 0))], inc=[("dg", 16)])
            dg_sl = p.val("dg")
            for tc in range(NTC):
                bi = (ot % 2) * 2 + tc
                bk = C.psum[bi]
                for c in range(NCH):
                    w = []
                    if c == 0:
                        w = [("dg", dg_sl),
                             ("ve", bank_rel.get(bi, 0)),
                             ("se", C.sbank_rel.get(bi, 0))]
                        if kind == "q" and ot == 0 and tc == 0:
                            w.append(("ve", C.ln_ready_ve))
                    mm(C, bk[:, 0:TC], wsl[buf][:, c, :],
                       C.ln[:, c, ts(tc, TC)], start=(c == 0),
                       stop=(c == NCH - 1), wait=w,
                       inc=([("pe", 1)] if c == NCH - 1 else ()))
                C.wslab_rel[buf] = p.val("pe")
                pe_pr = p.val("pe")
                if kind == "v":
                    p.add("s", (lambda ot=ot, tc=tc, bk=bk, bias_sb=bias_sb:
                                lambda e: e.activation(
                                    v_ch[:, 0:TC], bk[:, 0:TC],
                                    AF.Identity, bias=bias_sb[:, ot:ot + 1]))(),
                          wait=[("pe", pe_pr), ("pe", C.vch_pe_rel.get(0, 0))])
                    C.sbank_rel[bi] = p.val("se")
                    se_vch = p.val("se")
                    h = ot
                    for i in range(4):
                        w0 = []
                        if i == 0:
                            w0 = [("se", se_vch), ("ge", ge_ones),
                                  ("se", C.tr_rel)]
                        p.add("t", (lambda h=h, tc=tc, i=i: lambda e:
                                    e.transpose(
                                        pb4[:, ts(i)],
                                        kp_sb[:, h, tc * TC + i * 128:
                                              tc * TC + (i + 1) * 128],
                                        C.ident_bf))(),
                            wait=w0)
                        p.add("t", (lambda tc=tc, i=i: lambda e: e.transpose(
                            pb5[:, ts(i)], v_ch[:, ts(i)],
                            C.ident_bf))(),
                            inc=[("pe", 1)] if i == 3 else ())
                    pe_tr = p.val("pe")
                    C.vch_pe_rel[0] = pe_tr
                    p.add("s", (lambda: lambda e: e.copy(
                        kp_tm, pb4[:, 0:512].rearrange(
                            "p (i m) -> p i m", i=4)))(),
                        wait=[("pe", pe_tr)])
                    p.add("s", (lambda: lambda e: e.copy(
                        v_tm[:, :, 0:128],
                        pb5[:, 0:512].rearrange("p (i m) -> p i m", i=4)))())
                    C.tr_rel = p.val("se")
                    se_tm = p.val("se")
                    for i in range(4):
                        mm(C, C.psum[6][:, 0:132], kp_tm[:, i, :],
                           v_tm[:, i, 0:132],
                           start=(i == 0), stop=(i == 3),
                           wait=([("se", se_tm), ("ve", C.kvbank_rel)]
                                 if i == 0 else ()),
                           inc=([("pe", 1)] if i == 3 else ()))
                    p.add("v", (lambda h=h, kv_sb=kv_sb: lambda e:
                                e.tensor_add(kv_sb[:, h, :], kv_sb[:, h, :],
                                             C.psum[6][:, 0:132]))(),
                        wait=[("pe", p.val("pe")), ("ve", ve_kv0)])
                    C.kvbank_rel = p.val("ve")
                else:
                    dst = qp_sb if kind == "q" else kp_sb
                    p.add("s", (lambda bk=bk, bias_sb=bias_sb, ot=ot:
                                lambda e: e.activation(
                                    C.f32scr[:, 0:TC], bk[:, 0:TC], AF.Exp,
                                    bias=bias_sb[:, ot:ot + 1]))(),
                          wait=[("pe", pe_pr), ("ve", C.expscr_rel)])
                    C.sbank_rel[bi] = p.val("se")

                    def evac(bk=bk, ot=ot, tc=tc, dst=dst, bias_sb=bias_sb):
                        def f(e):
                            e.tensor_scalar(C.f32scr2[:, 0:TC], bk[:, 0:TC],
                                            bias_sb[:, ot:ot + 1], 0.0,
                                            ALU.add, ALU.max)
                            e.tensor_single_scalar(C.f32scr[:, 0:TC],
                                                   C.f32scr[:, 0:TC], 1.0,
                                                   ALU.min)
                            return e.tensor_add(dst[:, ot, ts(tc, TC)],
                                                C.f32scr[:, 0:TC],
                                                C.f32scr2[:, 0:TC])
                        return f
                    p.add("v", evac(), wait=[("se", p.val("se"))], nops=3)
                    bank_rel[bi] = p.val("ve")
                    C.expscr_rel = p.val("ve")

    C.expscr_rel = 0
    C.vch_pe_rel = {}
    C.wslab_rel = {0: p.val("pe"), 1: p.val("pe")}
    ge_ones = 0
    proj("q", C.t_qwr, C.qb_sb)
    proj("k", C.t_kwr, C.kb_sb)
    p.add("g", (lambda v_tm=v_tm: lambda e: e.memset(v_tm[:, :, 128:132], 1.0))(),
          wait=[("ve", p.val("ve"))], inc=[("ge", 1)])
    ge_ones = p.val("ge")
    proj("v", C.t_vwr, C.vb_sb)

    p.add("y", (lambda kv_sb=kv_sb: lambda e: e.dma_start(
        out=C.t_kvpart.rearrange("(h p) f -> p h f", p=128), in_=kv_sb))(),
        wait=[("ve", p.val("ve"))], inc=[("dy", 16)])
    p.add("g", lambda e: e.collective_compute(
        "AllReduce", ALU.add, replica_groups=[list(range(NCORES))],
        ins=[C.t_kvpart[:, :]], outs=[C.t_kvred[:, :]]),
        wait=[("dy", p.val("dy"))], inc=[("cc", 1)])
    p.add("g", lambda e: e.dma_start(
        out=C.kvbf_sb[:], in_=C.t_kvred.rearrange("(h p) f -> p h f", p=128)),
        wait=[("cc", p.val("cc"))], inc=[("dg", 16)])
    dg_kv = p.val("dg")

    for h in range(NCH):
        for tc in range(NTC):
            ab = C.psum[(h * NTC + tc) % 2]
            mm(C, ab[:, 0:TC], C.kvbf_sb[:, h, 0:128], qp_sb[:, h, ts(tc, TC)],
               wait=[("dg", dg_kv), ("ve", C.abank_rel.get((h * NTC + tc) % 2, 0))],
               inc=[("pe", 1)])
            mm(C, C.psum[2][0:1, 0:TC], C.kvbf_sb[:, h, 128:129],
               qp_sb[:, h, ts(tc, TC)],
               wait=[("se", C.zrow_rel)], inc=[("pe", 1)])
            pe_az = p.val("pe")
            p.add("s", (lambda: lambda e: e.activation(
                C.row_a[0:1, 0:TC], C.psum[2][0:1, 0:TC], AF.Identity,
                bias=cst1(C, "z_eps")))(),
                wait=[("pe", pe_az)])
            C.zrow_rel = p.val("se")
            p.add("v", (lambda: lambda e: e.reciprocal(
                C.row_b[0:1, 0:TC], C.row_a[0:1, 0:TC]))(),
                wait=[("se", p.val("se"))])
            mm(C, C.psum[3][:, 0:TC], C.ones1_f32, C.row_b[0:1, 0:TC],
               wait=[("ve", p.val("ve")), ("se", C.zb_rel)], inc=[("pe", 1)])
            p.add("s", (lambda zb_sb=zb_sb: lambda e: e.copy(
                zb_sb, C.psum[3][:, 0:TC]))(),
                wait=[("pe", p.val("pe"))])
            C.zb_rel = p.val("se")
            p.add("v", (lambda h=h, tc=tc, ab=ab, qp_sb=qp_sb, zb_sb=zb_sb:
                        lambda e: e.tensor_tensor(
                            qp_sb[:, h, ts(tc, TC)], ab[:, 0:TC], zb_sb,
                            ALU.mult))(),
                  wait=[("se", p.val("se"))])
            C.abank_rel[(h * NTC + tc) % 2] = p.val("ve")
    ve_attn = p.val("ve")

    for ot in range(NCH):
        buf = ot % 2
        p.add("g", (lambda ot=ot, buf=buf: lambda e: e.dma_start(
            out=wsl[buf], in_=C.t_owr[ot].rearrange("(c p) m -> p c m", p=128)))(),
            wait=[("pe", C.wslab_rel.get(buf, 0))], inc=[("dg", 16)])
        dg_sl = p.val("dg")
        for tc in range(NTC):
            bk = C.psum[(ot % 2) * 2 + tc]
            for c in range(NCH):
                w = []
                if c == 0:
                    w = [("dg", dg_sl), ("ve", ve_attn),
                         ("ve", C.obank_rel.get((ot % 2) * 2 + tc, 0))]
                mm(C, bk[:, 0:TC], wsl[buf][:, c, :], qp_sb[:, c, ts(tc, TC)],
                   start=(c == 0), stop=(c == NCH - 1), wait=w,
                   inc=([("pe", 1)] if c == NCH - 1 else ()))
            C.wslab_rel[buf] = p.val("pe")
            p.add("v", (lambda ot=ot, tc=tc, bk=bk: lambda e:
                        e.scalar_tensor_tensor(
                            C.x[:, ot, ts(tc, TC)], bk[:, 0:TC],
                            C.ob_sb[:, ot:ot + 1], C.x[:, ot, ts(tc, TC)],
                            ALU.add, ALU.add))(),
                  wait=[("pe", p.val("pe"))])
            C.obank_rel[(ot % 2) * 2 + tc] = p.val("ve")
    C.x_ready_ve = p.val("ve")


FFN1_PARTS = "both"
DEBUG_DUMP = False


def emit_ffn1(C):
    p = C.plan
    emit_gates(C, "l1")

    act_e = C.arena_a[:, 0:11264].rearrange("p (j t) -> p j t", j=11)
    gb_sb = C.arena_a[:, 11264:12288]
    act_sh = C.arena_b[:, 0:11264].rearrange("p (j t) -> p j t", j=11 * 2)
    sl = [C.wslab[:, ts(i, 2048)].rearrange("p (c m) -> p c m", c=NCH)
          for i in range(4)]
    sl2 = [C.wslab[:, ts(i, 2816)].rearrange("p (c m) -> p c m", c=22)
           for i in range(2)]
    sl3 = [C.wslab[:, i * 1408:(i + 1) * 1408].rearrange(
        "p (c m) -> p c m", c=11) for i in range(2)]

    relA = {}
    relS = {}
    C.silu2 = [C.f32scr2, C.arena_f[:, 520:1032]]
    C.acttmp = [C.f32scr, C.arena_f[:, 1032:1544]]
    C.silu_rel2 = {}
    gates_se = C.gates_se
    p.add("g", lambda e: e.dma_start(out=C.oneh8_sb, in_=C.t_oneh8),
          wait=[("pe", p.val("pe"))], inc=[("dg", 16)])
    C.oneh8_dg = p.val("dg")
    pe_now = p.val("pe")
    C.slrel = {i: pe_now for i in range(4)}
    C.sl2rel = {i: pe_now for i in range(2)}
    C.sl3rel = {i: pe_now for i in range(2)}

    def slab_dma(dst, src_ap, buf_rel):
        p.add("g", (lambda dst=dst, src_ap=src_ap: lambda e: e.dma_start(
            out=dst, in_=src_ap.rearrange("(c p) m -> p c m", p=128)))(),
            wait=[("pe", buf_rel)], inc=[("dg", 16)])
        return p.val("dg")

    C.actsh_pe_rel = 0
    C.acte_pe_rel = 0
    for tc in (range(NTC) if FFN1_PARTS in ("both", "shared") else []):
        fence_h = p.val("pe")
        actsh_fence = C.actsh_pe_rel
        for j in range(22):
            b1 = C.psum[(j % 2) * 2]
            b3 = C.psum[(j % 2) * 2 + 1]
            bufa, bufb = (j % 2) * 2, (j % 2) * 2 + 1
            dg1 = slab_dma(sl[bufa], C.t_sw1r[j],
                           max(C.slrel.get(bufa, 0), fence_h))
            dg3 = slab_dma(sl[bufb], C.t_sw3r[j],
                           max(C.slrel.get(bufb, 0), fence_h))
            for c in range(NCH):
                w = []
                if c == 0:
                    w = [("dg", dg1), ("ve", relA.get((j % 2) * 2, 0)),
                         ("se", relS.get((j % 2) * 2, 0))]
                    if tc == 0 and j == 0:
                        w.append(("ve", C.ln_ready_ve))
                mm(C, b1[:, 0:TC], sl[bufa][:, c, :], C.ln[:, c, ts(tc, TC)],
                   start=(c == 0), stop=(c == NCH - 1), wait=w,
                   inc=([("pe", 1)] if c == NCH - 1 else ()))
            C.slrel[bufa] = p.val("pe")
            pe_h1 = p.val("pe")
            for c in range(NCH):
                w = []
                if c == 0:
                    w = [("dg", dg3), ("ve", relA.get((j % 2) * 2 + 1, 0))]
                mm(C, b3[:, 0:TC], sl[bufb][:, c, :], C.ln[:, c, ts(tc, TC)],
                   start=(c == 0), stop=(c == NCH - 1), wait=w,
                   inc=([("pe", 1)] if c == NCH - 1 else ()))
            C.slrel[bufb] = p.val("pe")
            pe_h3 = p.val("pe")
            p.add("s", (lambda b1=b1: lambda e: e.activation(
                C.f32scr2, b1[:, 0:TC], AF.Silu))(),
                wait=[("pe", pe_h1), ("ve", C.silu_rel)])
            relS[(j % 2) * 2] = p.val("se")
            p.add("v", (lambda j=j, tc=tc, b3=b3: lambda e:
                        e.scalar_tensor_tensor(
                            act_sh[:, j, 0:TC],
                            b3[:, 0:TC], 0.5, C.f32scr2,
                            ALU.mult, ALU.mult))(),
                wait=[("se", p.val("se")), ("pe", pe_h3),
                      ("pe", actsh_fence)])
            relA[(j % 2) * 2] = p.val("ve")
            relA[(j % 2) * 2 + 1] = p.val("ve")
            C.silu_rel = p.val("ve")
        ve_acts = p.val("ve")
        fence_sw2 = p.val("pe")
        for ot in range(NCH):
            buf = ot % 2
            dg2 = slab_dma(sl2[buf], C.t_sw2r[ot],
                           max(C.sl2rel.get(buf, 0), fence_sw2))
            bk = C.psum[5 + (ot % 2)]
            for jj in range(22):
                w = []
                if jj == 0:
                    w = [("dg", dg2), ("ve", ve_acts),
                         ("ve", relA.get(5 + (ot % 2), 0))]
                mm(C, bk[:, 0:TC], sl2[buf][:, jj, :], act_sh[:, jj, 0:TC],
                   start=(jj == 0), stop=(jj == 21), wait=w,
                   inc=([("pe", 1)] if jj == 21 else ()))
            C.sl2rel[buf] = p.val("pe")
            p.add("v", (lambda ot=ot, tc=tc, bk=bk: lambda e: e.tensor_add(
                C.x[:, ot, ts(tc, TC)], C.x[:, ot, ts(tc, TC)], bk[:, 0:TC]))(),
                wait=[("pe", p.val("pe"))])
            relA[5 + (ot % 2)] = p.val("ve")
        C.actsh_pe_rel = p.val("pe")

    if DEBUG_DUMP:
        C.t_gtdump = C.nc.dram_tensor("gT_dump", [8, TOK], BF16,
                                      kind="ExternalOutput").ap()
        p.add("y", lambda e: e.dma_start(out=C.t_gtdump, in_=C.gT[0:8, :]),
              wait=[("se", C.gates_se)], inc=[("dy", 16)])
        C.t_dbgdump = C.nc.dram_tensor("dbg_dump", [128, 64], F32,
                                       kind="ExternalOutput").ap()
        p.add("y", lambda e: e.dma_start(out=C.t_dbgdump, in_=C.dbg_sb),
              wait=[("se", C.gates_se)], inc=[("dy", 16)])

    for ex in (range(8) if FFN1_PARTS in ("both", "experts") else []):
        for tc in range(NTC):
            mm(C, C.psum[4][:, 0:TC], C.oneh8_sb[0:8, ts(ex)],
               C.gT[0:8, ts(tc, TC)],
               wait=[("se", p.val("se")), ("se", gates_se),
                     ("ve", relA.get(4, 0)), ("ve", relA.get(5, 0)),
                     ("ve", relA.get(6, 0)), ("ve", relA.get(7, 0)),
                     ("se", relS.get(4, 0)), ("se", relS.get(5, 0)),
                     ("dg", C.oneh8_dg)], inc=[("pe", 1)])
            p.add("s", (lambda ex=ex, tc=tc: lambda e: e.copy(
                gb_sb[:, ts(tc, TC)], C.psum[4][:, 0:TC]))(),
                wait=[("pe", p.val("pe")), ("ve", C.gb_rel)])
            relS[4] = p.val("se")
        se_gb = p.val("se")
        fence_h = p.val("pe")
        acte_fence = C.acte_pe_rel
        for j in range(11):
            bufa, bufb = (j % 2) * 2, (j % 2) * 2 + 1
            dg1 = slab_dma(sl[bufa], C.t_w1r[ex, j],
                           max(C.slrel.get(bufa, 0), fence_h))
            dg3 = slab_dma(sl[bufb], C.t_w3r[ex, j],
                           max(C.slrel.get(bufb, 0), fence_h))
            jb = (j % 2) * 4
            for c in range(NCH):
                for tc in range(NTC):
                    w = []
                    if c == 0:
                        w = [("dg", dg1), ("ve", relA.get(jb + tc, 0)),
                             ("se", relS.get(jb + tc, 0))]
                    mm(C, C.psum[jb + tc][:, 0:TC], sl[bufa][:, c, :],
                       C.ln[:, c, ts(tc, TC)], start=(c == 0),
                       stop=(c == NCH - 1), wait=w,
                       inc=([("pe", 1)] if (c == NCH - 1 and tc == 1) else ()))
            C.slrel[bufa] = p.val("pe")
            pe_eh1 = p.val("pe")
            for c in range(NCH):
                for tc in range(NTC):
                    w = []
                    if c == 0:
                        w = [("dg", dg3), ("ve", relA.get(jb + 2 + tc, 0)),
                             ("se", relS.get(jb + 2 + tc, 0))]
                    mm(C, C.psum[jb + 2 + tc][:, 0:TC], sl[bufb][:, c, :],
                       C.ln[:, c, ts(tc, TC)], start=(c == 0),
                       stop=(c == NCH - 1), wait=w,
                       inc=([("pe", 1)] if (c == NCH - 1 and tc == 1) else ()))
            C.slrel[bufb] = p.val("pe")
            pe_eh3 = p.val("pe")
            for tc in range(NTC):
                sil = C.silu2[tc]
                tmp = C.acttmp[tc]
                p.add("s", (lambda tc=tc, sil=sil, jb=jb: lambda e:
                            e.activation(sil, C.psum[jb + tc][:, 0:TC],
                                         AF.Silu))(),
                    wait=[("pe", pe_eh1), ("ve", C.silu_rel2.get(tc, 0))])
                relS[jb + tc] = p.val("se")

                def actfn(j=j, tc=tc, sil=sil, tmp=tmp, jb=jb):
                    def f(e):
                        e.tensor_tensor(tmp, sil,
                                        C.psum[jb + 2 + tc][:, 0:TC], ALU.mult)
                        return e.tensor_tensor(
                            act_e[:, j, ts(tc, TC)], tmp,
                            gb_sb[:, ts(tc, TC)], ALU.mult)
                    return f
                p.add("v", actfn(), wait=[("se", p.val("se")), ("se", se_gb),
                                          ("pe", pe_eh3),
                                          ("pe", acte_fence)], nops=2)
                relA[jb + tc] = p.val("ve")
                relA[jb + 2 + tc] = p.val("ve")
                C.silu_rel2[tc] = p.val("ve")
        ve_acts = p.val("ve")
        C.gb_rel = ve_acts
        fence_w2 = p.val("pe")
        for ot in range(NCH):
            buf = ot % 2
            dg2 = slab_dma(sl3[buf], C.t_w2r[ex, ot],
                           max(C.sl3rel.get(buf, 0), fence_w2))
            bks = [C.psum[4 + (ot % 2) * 2 + tc] for tc in range(NTC)]
            for jj in range(11):
                for tc in range(NTC):
                    w = []
                    if jj == 0:
                        w = [("dg", dg2), ("ve", ve_acts),
                             ("ve", relA.get(4 + (ot % 2) * 2 + tc, 0)),
                             ("se", relS.get(4 + (ot % 2) * 2 + tc, 0))]
                    mm(C, bks[tc][:, 0:TC], sl3[buf][:, jj, :],
                       act_e[:, jj, ts(tc, TC)],
                       start=(jj == 0), stop=(jj == 10), wait=w,
                       inc=([("pe", 1)] if (jj == 10 and tc == 1) else ()))
            pe_w2 = p.val("pe")
            for tc in range(NTC):
                p.add("v", (lambda ot=ot, tc=tc, bk=bks[tc]: lambda e:
                            e.tensor_add(C.x[:, ot, ts(tc, TC)],
                                         C.x[:, ot, ts(tc, TC)],
                                         bk[:, 0:TC]))(),
                    wait=[("pe", pe_w2)])
                relA[4 + (ot % 2) * 2 + tc] = p.val("ve")
            C.sl3rel[buf] = p.val("pe")
        C.acte_pe_rel = p.val("pe")
    C.x_ready_ve = p.val("ve")



def emit_attn1(C):
    p = C.plan
    ln_flat = C.ln.rearrange("p c t -> p (c t)")
    K_sb = C.arena_a.rearrange("p (h t) -> p h t", h=2)
    V_sb = C.arena_b.rearrange("p (h t) -> p h t", h=2)
    qa_bf = C.arena_a[:, 0:6144].rearrange("p (o t) -> p o t", o=6)
    ckv_raw = C.arena_a[:, 6144:8192].rearrange("p (o t) -> p o t", o=4)
    cos1_sb = C.arena_b[0:64, 0:2048].bitcast(F32)
    sin1_sb = C.arena_b[0:64, 2048:4096].bitcast(F32)
    wsl = [C.wslab[:, ts(i, 2048)].rearrange("p (c m) -> p c m", c=NCH)
           for i in range(2)]
    cos2_sb = C.wslab[:, 4096:5120].bitcast(F32)
    sin2_sb = C.wslab[:, 5120:6144].bitcast(F32)
    qan_sb = ln_flat[:, 0:6144].rearrange("p (o t) -> p o t", o=6)
    ckv_sb = ln_flat[:, 0:4096].rearrange("p (o t) -> p o t", o=4)
    att_sb = ln_flat[:, 0:16384].rearrange("p (c t) -> p c t", c=NCH)
    nrm = [C.nrm_bf[:, ts(i, TC)] for i in range(2)]
    kr_bf = C.kr_bf
    qro_bf = C.nrm_bf[:, 0:TC]
    pe0 = p.val("pe")
    C.wslab_rel = {0: pe0, 1: pe0}
    C.cos2_rel = pe0
    C.attn1_sb_rel = {}
    p.add("y", lambda e: e.dma_start(out=cos1_sb, in_=C.t_cos1),
          wait=[("pe", pe0), ("ve", C.x_ready_ve)], inc=[("dy", 16)])
    p.add("y", lambda e: e.dma_start(out=sin1_sb, in_=C.t_sin1),
          inc=[("dy", 16)])
    C.cs1_dy = p.val("dy")

    def rms_block(n_tiles, wdram, gam, inv_name, dst_row0, raw_sb, tc,
                  stat_slice):
        for o in range(n_tiles):
            buf = o % 2
            p.add("g", (lambda o=o, buf=buf, wdram=wdram: lambda e:
                        e.dma_start(out=wsl[buf], in_=wdram[o].rearrange(
                            "(c p) m -> p c m", p=128)))(),
                  wait=[("pe", C.wslab_rel.get(buf, 0))], inc=[("dg", 16)])
            dg_sl = p.val("dg")
            bk = C.psum[o % 2]
            for c in range(NCH):
                w = []
                if c == 0:
                    w = [("dg", dg_sl), ("se", C.attn1_sb_rel.get(o % 2, 0))]
                    if o < 2:
                        w.append(("ve", C.ln_ready_ve))
                mm(C, bk[:, 0:TC], wsl[buf][:, c, :], C.ln[:, c, ts(tc, TC)],
                   start=(c == 0), stop=(c == NCH - 1), wait=w,
                   inc=([("pe", 1)] if c == NCH - 1 else ()))
            C.wslab_rel[buf] = p.val("pe")
            pe_pr = p.val("pe")
            p.add("s", (lambda o=o, bk=bk, raw_sb=raw_sb: lambda e: e.copy(
                raw_sb[:, o, 0:TC], bk[:, 0:TC]))(), wait=[("pe", pe_pr)])
            p.add("s", (lambda o=o, raw_sb=raw_sb: lambda e: e.activation(
                C.sq_bf[:, 0:TC], raw_sb[:, o, 0:TC], AF.Square))(),
                wait=[("pe", C.sqmm_rel)])
            se_sq = p.val("se")
            C.attn1_sb_rel[o % 2] = se_sq
            mm(C, C.psum[7][stat_slice, 0:TC], C.ones_bf, C.sq_bf[:, 0:TC],
               start=(o == 0), stop=(o == n_tiles - 1),
               wait=[("se", se_sq)] + ([("ve", C.stat7_rel)] if o == 0 else []),
               inc=[("pe", 1)])
            C.sqmm_rel = p.val("pe")
        pe_stats = p.val("pe")
        p.add("s", (lambda inv_name=inv_name, stat_slice=stat_slice:
                    lambda e: e.activation(
                        C.row_a[0:1, 0:TC], C.psum[7][stat_slice, 0:TC],
                        AF.Sqrt, bias=cst1(C, "rms_eps"),
                        scale=cst1(C, inv_name)))(),
            wait=[("pe", pe_stats)])
        p.add("v", (lambda: lambda e: e.reciprocal(
            C.row_b[0:1, 0:TC], C.row_a[0:1, 0:TC]))(),
            wait=[("se", p.val("se"))])
        C.stat7_rel = p.val("ve")
        mm(C, C.psum[5][:, 0:TC], C.ones1_f32, C.row_b[0:1, 0:TC],
           wait=[("ve", p.val("ve")), ("se", C.b5_rel)], inc=[("pe", 1)])
        p.add("s", (lambda: lambda e: e.copy(C.f32scr[:, 0:TC],
                                             C.psum[5][:, 0:TC]))(),
            wait=[("pe", p.val("pe"))])
        C.b5_rel = p.val("se")
        se_rb = p.val("se")
        for o in range(n_tiles):
            nb = nrm[o % 2]

            def normf(o=o, nb=nb, raw_sb=raw_sb, gam=gam):
                def f(e):
                    e.tensor_tensor(C.f32scr2[:, 0:TC], raw_sb[:, o, 0:TC],
                                    C.f32scr[:, 0:TC], ALU.mult)
                    return e.tensor_scalar_mul(nb, C.f32scr2[:, 0:TC],
                                               gam[:, o:o + 1])
                return f
            p.add("v", normf(), wait=[("se", se_rb),
                                      ("dy", C.nrm_dy_rel.get(o % 2, 0))],
                  nops=2)
            ve_n = p.val("ve")
            p.add("y", (lambda o=o, nb=nb, dst_row0=dst_row0, tc=tc:
                        lambda e: e.dma_start(
                            out=C.t_contrib[dst_row0 + o * 128:
                                            dst_row0 + (o + 1) * 128,
                                            ts(tc, TC)],
                            in_=nb))(),
                  wait=[("ve", ve_n)], inc=[("dy", 16)])
            C.nrm_dy_rel[o % 2] = p.val("dy")

    C.sqmm_rel = 0
    C.stat7_rel = 0
    C.b5_rel = 0
    C.nrm_dy_rel = {}
    for tc in range(NTC):
        rms_block(6, C.t_qar, C.qgam_sb, "inv768", 0, qa_bf, tc,
                  slice(0, 1))
        rms_block(4, C.t_kvar, C.kgam_sb, "inv512", 768, ckv_raw, tc,
                  slice(64, 65))
        p.add("g", (lambda: lambda e: e.dma_start(
            out=wsl[0], in_=C.t_kvar[4].rearrange("(c p) m -> p c m", p=128)))(),
            wait=[("pe", C.wslab_rel.get(0, 0))], inc=[("dg", 16)])
        dg_sl = p.val("dg")
        bk = C.psum[0]
        for c in range(NCH):
            w = [("dg", dg_sl), ("se", C.attn1_sb_rel.get(0, 0))] if c == 0 else []
            mm(C, bk[:, 0:TC], wsl[0][:, c, :], C.ln[:, c, ts(tc, TC)],
               start=(c == 0), stop=(c == NCH - 1), wait=w,
               inc=([("pe", 1)] if c == NCH - 1 else ()))
        C.wslab_rel[0] = p.val("pe")
        pe_kr = p.val("pe")
        p.add("s", (lambda bk=bk: lambda e: e.copy(
            kr_bf[0:64, 0:TC], bk[0:64, 0:TC]))(),
            wait=[("pe", pe_kr), ("dy", C.kro_dy_rel)])
        C.attn1_sb_rel[0] = p.val("se")
        se_kr = p.val("se")
        mm(C, C.psum[6][0:64, 0:TC], C.perm1_sb, kr_bf[0:64, 0:TC],
           wait=[("se", se_kr), ("ve", C.rot_rel)], inc=[("pe", 1)])
        pe_rot = p.val("pe")

        def ropef(tc=tc):
            def f(e):
                e.tensor_tensor(C.f32scr[0:64, 0:TC], kr_bf[0:64, 0:TC],
                                cos1_sb[0:64, ts(tc, TC)], ALU.mult)
                e.tensor_tensor(C.f32scr2[0:64, 0:TC], C.psum[6][0:64, 0:TC],
                                sin1_sb[0:64, ts(tc, TC)], ALU.mult)
                return e.tensor_add(kr_bf[0:64, 0:TC], C.f32scr[0:64, 0:TC],
                                    C.f32scr2[0:64, 0:TC])
            return f
        p.add("v", ropef(), wait=[("pe", pe_rot), ("dy", C.kro_dy_rel),
                                  ("dy", C.cs1_dy)], nops=3)
        C.rot_rel = p.val("ve")
        p.add("y", (lambda tc=tc: lambda e: e.dma_start(
            out=C.t_contrib[1280:1344, ts(tc, TC)], in_=kr_bf[0:64, 0:TC]))(),
            wait=[("ve", p.val("ve"))], inc=[("dy", 16)])
        C.kro_dy_rel = p.val("dy")

    p.add("g", lambda e: e.collective_compute(
        "AllGather", ALU.bypass, replica_groups=[list(range(NCORES))],
        ins=[C.t_contrib[:, :]], outs=[C.t_gath[:, :]]),
        wait=[("dy", p.val("dy"))], inc=[("cc", 1)])
    cc_gath = p.val("cc")

    wslab_free = max(C.wslab_rel.get(0, 0), C.wslab_rel.get(1, 0))
    p.add("g", (lambda: lambda e: e.dma_start(
        out=C.kvb_sb, in_=C.t_kvbr.rearrange("o (c p) m -> p o c m", p=128)))(),
        wait=[("pe", wslab_free)], inc=[("dg", 16)])
    p.add("g", (lambda: lambda e: e.dma_start(
        out=C.qbr_sb, in_=C.t_qbr.rearrange("o (c p) m -> p o c m", p=128)))(),
        inc=[("dg", 16)])
    p.add("g", lambda e: e.dma_start(out=C.maskq_sb, in_=C.t_maskq),
          wait=[("cc", cc_gath)], inc=[("dg", 16)])
    dg_w = p.val("dg")
    p.add("v", lambda e: e.memset(C.kra_sb[64:65, :], 1.0),
          wait=[("cc", cc_gath)])
    ve_kra1 = p.val("ve")

    pb4 = C.psum[4].bitcast(BF16)
    C.ckv_pe_rel = 0
    for r in range(NCORES):
        p.add("y", (lambda r=r: lambda e: e.dma_start(
            out=ckv_sb, in_=C.t_gath[r * 1344 + 768:r * 1344 + 1280, :]
            .rearrange("(c p) t -> p c t", p=128)))(),
            wait=[("cc", cc_gath), ("pe", C.ckv_pe_rel), ("ve", ve_kra1)],
            inc=[("dy", 16)])
        p.add("y", (lambda r=r: lambda e: e.dma_start(
            out=C.kra_sb[0:64, r * TOK:(r + 1) * TOK],
            in_=C.t_gath[r * 1344 + 1280:r * 1344 + 1344, :]))(),
            wait=[("cc", cc_gath)], inc=[("dy", 16)])
        dy_ck = p.val("dy")
        for o4 in range(4):
            h, is_v = o4 // 2, o4 % 2
            for tc2 in range(NTC):
                bk = C.psum[o4 % 2]
                for c4 in range(4):
                    w = []
                    if c4 == 0:
                        w = [("dy", dy_ck), ("dg", dg_w),
                             ("se", C.attn1_sb_rel.get(o4 % 2, 0))]
                    mm(C, bk[:, 0:TC], C.kvb_sb[:, o4, c4, :],
                       ckv_sb[:, c4, ts(tc2, TC)],
                       start=(c4 == 0), stop=(c4 == 3), wait=w,
                       inc=([("pe", 1)] if c4 == 3 else ()))
                pe_up = p.val("pe")
                col0 = r * TOK + tc2 * TC
                if not is_v:
                    p.add("s", (lambda h=h, col0=col0, bk=bk: lambda e:
                                e.copy(K_sb[:, h, col0:col0 + TC],
                                       bk[:, 0:TC]))(),
                          wait=[("pe", pe_up)])
                    C.attn1_sb_rel[o4 % 2] = p.val("se")
                else:
                    p.add("s", (lambda bk=bk: lambda e: e.copy(
                        C.sq_bf[:, 0:TC], bk[:, 0:TC]))(),
                          wait=[("pe", pe_up), ("pe", C.vtr_rel)])
                    C.attn1_sb_rel[o4 % 2] = p.val("se")
                    se_v = p.val("se")
                    for i in range(4):
                        p.add("t", (lambda i=i: lambda e: e.transpose(
                            pb4[:, ts(i)], C.sq_bf[:, ts(i)], C.ident_bf))(),
                            wait=([("se", se_v), ("se", C.vtr_se_rel)]
                                  if i == 0 else ()),
                            inc=([("pe", 1)] if i == 3 else ()))
                    C.vtr_rel = p.val("pe")
                    p.add("s", (lambda h=h, col0=col0: lambda e: e.copy(
                        V_sb[:, h, col0:col0 + TC],
                        pb4[:, 0:512]))(), wait=[("pe", C.vtr_rel)])
                    C.vtr_se_rel = p.val("se")
        C.ckv_pe_rel = p.val("pe")

    maskq = C.maskq_sb.rearrange("p (r t) -> p r t", r=4)
    PTv = [C.pt_bf[:, ts(i, TC)] for i in range(2)]
    last_flash_pe = p.val("pe")
    for qt in range(16):
        r, tc2 = qt // 2, qt % 2
        if tc2 == 0:
            p.add("y", (lambda r=r: lambda e: e.dma_start(
                out=qan_sb, in_=C.t_gath[r * 1344:r * 1344 + 768, :]
                .rearrange("(c p) t -> p c t", p=128)))(),
                wait=[("cc", cc_gath), ("pe", p.val("pe"))], inc=[("dy", 16)])
        dy_qan = p.val("dy")
        p.add("y", (lambda qt=qt: lambda e: e.dma_start(
            out=cos2_sb, in_=C.t_cos2[:, ts(qt, TC)]))(),
            wait=[("pe", C.cos2_rel)], inc=[("dy", 16)])
        p.add("y", (lambda qt=qt: lambda e: e.dma_start(
            out=sin2_sb, in_=C.t_sin2[:, ts(qt, TC)]))(),
            inc=[("dy", 16)])
        dy_cs = p.val("dy")
        for o3 in range(3):
            bk = C.psum[6]
            for c6 in range(6):
                w = []
                if c6 == 0:
                    w = [("dy", dy_qan), ("dg", dg_w),
                         ("se", C.qup_se_rel)]
                mm(C, bk[:, 0:TC], C.qbr_sb[:, o3, c6, :],
                   qan_sb[:, c6, ts(tc2, TC)],
                   start=(c6 == 0), stop=(c6 == 5), wait=w,
                   inc=([("pe", 1)] if c6 == 5 else ()))
            pe_q = p.val("pe")
            if o3 < 2:
                p.add("s", (lambda o3=o3, bk=bk: lambda e: e.activation(
                    C.qtn_sb[:, o3, :], bk[:, 0:TC], AF.Identity,
                    scale=cstcol(C, "qscale")))(),
                    wait=[("pe", pe_q), ("pe", last_flash_pe)])
                C.qup_se_rel = p.val("se")
            else:
                p.add("s", (lambda bk=bk: lambda e: e.activation(
                    C.qr_bf[:, 0:TC], bk[:, 0:TC], AF.Identity,
                    scale=cstcol(C, "qscale")))(),
                    wait=[("pe", pe_q), ("ve", C.qro_ve_rel)])
                C.qup_se_rel = p.val("se")
                se_qr = p.val("se")
                mm(C, C.psum[7][:, 0:TC], C.perm2_sb, C.qr_bf[:, 0:TC],
                   wait=[("se", se_qr), ("ve", C.qro_ve_rel)],
                   inc=[("pe", 1)])
                pe_rot = p.val("pe")

                def qropef():
                    def f(e):
                        e.tensor_tensor(C.f32scr[:, 0:TC], C.qr_bf[:, 0:TC],
                                        cos2_sb, ALU.mult)
                        e.tensor_tensor(C.f32scr2[:, 0:TC],
                                        C.psum[7][:, 0:TC], sin2_sb, ALU.mult)
                        return e.tensor_add(qro_bf, C.f32scr[:, 0:TC],
                                            C.f32scr2[:, 0:TC])
                    return f
                p.add("v", qropef(), wait=[("pe", pe_rot), ("dy", dy_cs),
                                           ("se", C.qra_se_rel)], nops=3)
                C.qro_ve_rel = p.val("ve")
                p.add("s", (lambda: lambda e: e.copy(
                    C.qra0[0:64, 0:TC], qro_bf[0:64, 0:TC]))(),
                    wait=[("ve", p.val("ve")), ("pe", last_flash_pe)])
                p.add("s", (lambda: lambda e: e.copy(
                    C.qra1[0:64, 0:TC], qro_bf[64:128, 0:TC]))())
                C.qra_se_rel = p.val("se")
        se_q = p.val("se")

        for h in range(2):
            qra = C.qra0 if h == 0 else C.qra1
            for js in range(4):
                kcd = qt * 4 + js
                mm(C, C.psum[4][:, 0:128],
                   C.qtn_sb[:, h, ts(js)], K_sb[:, h, ts(kcd)],
                   start=True, stop=False,
                   wait=[("se", se_q), ("ve", C.m0_ve_rel)])
                mm(C, C.psum[4][:, 0:128], qra[0:64, ts(js)],
                   C.kra_sb[0:64, ts(kcd)], start=False, stop=True,
                   inc=[("pe", 1)])
                pe_qk = p.val("pe")

                def m0f(js=js):
                    def f(e):
                        e.tensor_add(C.psum[4][:, 0:128], C.psum[4][:, 0:128],
                                     C.trilneg_sb)
                        return e.reduce_max(C.colscr2[:, 0:1],
                                            C.psum[4][:, 0:128], axis=AXX)
                    return f
                p.add("v", m0f(), wait=[("pe", pe_qk)], nops=2)
                C.m0_ve_rel = p.val("ve")
                mm(C, C.psum[5][0:1, ts(js)], C.colscr2[:, 0:1], C.ident_f32,
                   wait=[("ve", C.m0_ve_rel), ("se", C.m0row_se_rel)],
                   inc=[("pe", 1)])
            pe_m0 = p.val("pe")
            p.add("s", (lambda qra=qra: lambda e: e.mul(
                qra[64:65, 0:TC], C.psum[5][0:1, 0:TC], -1.0))(),
                wait=[("pe", pe_m0)])
            C.m0row_se_rel = p.val("se")
            se_aug = p.val("se")

            nkc = qt * 4 + 4
            for kc in range(nkc):
                sb_ = C.psum[kc % 2]
                mm(C, sb_[:, 0:TC], K_sb[:, h, ts(kc)], C.qtn_sb[:, h, :],
                   start=True, stop=False,
                   wait=[("se", se_aug), ("se", C.pt_se_rel.get(kc % 2, 0))])
                mm(C, sb_[:, 0:TC], C.kra_sb[0:65, ts(kc)], qra[0:65, 0:TC],
                   start=False, stop=True, inc=[("pe", 1)])
                pe_sc = p.val("pe")
                p.add("s", (lambda kc=kc, sb_=sb_: lambda e: e.activation(
                    PTv[kc % 2], sb_[:, 0:TC], AF.Exp))(),
                    wait=[("pe", pe_sc), ("ve", C.pt_ve_rel.get(kc % 2, 0)),
                          ("pe", C.pt_pe_rel.get(kc % 2, 0))])
                C.pt_se_rel[kc % 2] = p.val("se")
                se_pt = p.val("se")
                wpv = [("se", se_pt)]
                if kc >= qt * 4:
                    p.add("v", (lambda kc=kc, qt=qt: lambda e: e.tensor_mul(
                        PTv[kc % 2], PTv[kc % 2], maskq[:, kc - qt * 4, :]))(),
                        wait=[("se", se_pt)])
                    C.pt_ve_rel[kc % 2] = p.val("ve")
                    wpv = [("ve", p.val("ve"))]
                if kc == 0:
                    wpv = wpv + [("ve", C.acc_ve_rel)]
                mm(C, C.psum[2][:, 0:TC], V_sb[:, h, ts(kc)], PTv[kc % 2],
                   start=(kc == 0), stop=(kc == nkc - 1), wait=wpv)
                mm(C, C.psum[3][0:1, 0:TC], C.ones_bf, PTv[kc % 2],
                   start=(kc == 0), stop=(kc == nkc - 1),
                   wait=([("se", C.sum_se_rel)] if kc == 0 else ()),
                   inc=[("pe", 1)])
                C.pt_pe_rel[kc % 2] = p.val("pe")
            pe_fl = p.val("pe")
            last_flash_pe = pe_fl
            p.add("s", (lambda: lambda e: e.copy(
                C.row_a[0:1, 0:TC], C.psum[3][0:1, 0:TC]))(),
                wait=[("pe", pe_fl)])
            p.add("v", (lambda: lambda e: e.reciprocal(
                C.row_b[0:1, 0:TC], C.row_a[0:1, 0:TC]))(),
                wait=[("se", p.val("se"))])
            mm(C, C.psum[3][:, 0:TC], C.ones1_f32, C.row_b[0:1, 0:TC],
               wait=[("ve", p.val("ve"))], inc=[("pe", 1)])
            p.add("s", (lambda: lambda e: e.copy(
                C.f32scr[:, 0:TC], C.psum[3][:, 0:TC]))(),
                wait=[("pe", p.val("pe"))])
            se_rb2 = p.val("se")
            C.sum_se_rel = se_rb2
            p.add("v", (lambda: lambda e: e.tensor_tensor(
                C.attno_bf[:, 0:TC], C.psum[2][:, 0:TC], C.f32scr[:, 0:TC],
                ALU.mult))(),
                wait=[("se", se_rb2), ("dy", C.atto_dy_rel)])
            ve_at = p.val("ve")
            C.acc_ve_rel = ve_at
            p.add("y", (lambda h=h, qt=qt: lambda e: e.dma_start(
                out=C.t_attc[h * 128:(h + 1) * 128, ts(qt, TC)],
                in_=C.attno_bf[:, 0:TC]))(),
                wait=[("ve", ve_at)], inc=[("dy", 16)])
            C.atto_dy_rel = p.val("dy")

    p.add("g", lambda e: e.collective_compute(
        "AllGather", ALU.bypass, replica_groups=[list(range(NCORES))],
        ins=[C.t_attc[:, :]], outs=[C.t_gath2[:, :]]),
        wait=[("dy", p.val("dy"))], inc=[("cc", 1)])
    cc_g2 = p.val("cc")
    p.add("y", lambda e: e.dma_start(out=C.coff_sb[:], in_=C.t_coff),
          inc=[("dy", 16)])
    dy_off = p.val("dy")

    def att_load(idx, dst):
        def f(e):
            with e.register(f"roff{idx}") as r:
                e.reg_load(r, C.coff_sb[0:1, idx:idx + 1])
                src = bass.AP(C.t_gath2_h, _RV(r), [[T, 128], [1, TC]])
                return e.dma_start(out=dst, in_=src)
        return f
    for tc in range(NTC):
        for c in range(NCH):
            p.add("y", att_load(c * 2 + tc, att_sb[:, c, ts(tc, TC)]),
                  wait=[("cc", cc_g2), ("dy", dy_off),
                        ("pe", p.val("pe"))] if (tc == 0 and c == 0) else
                       [("cc", cc_g2)],
                  inc=[("dy", 16)])
    dy_att = p.val("dy")

    C.wslab_rel = {0: p.val("pe"), 1: p.val("pe")}
    for ot in range(NCH):
        buf = ot % 2
        p.add("g", (lambda ot=ot, buf=buf: lambda e: e.dma_start(
            out=wsl[buf], in_=C.t_wor[ot].rearrange("(c p) m -> p c m",
                                                    p=128)))(),
            wait=[("pe", C.wslab_rel.get(buf, 0))], inc=[("dg", 16)])
        dg_sl = p.val("dg")
        for tc in range(NTC):
            bk = C.psum[(ot % 2) * 2 + tc]
            for c in range(NCH):
                w = []
                if c == 0:
                    w = [("dg", dg_sl), ("dy", dy_att),
                         ("ve", C.oproj_rel.get((ot % 2) * 2 + tc, 0))]
                mm(C, bk[:, 0:TC], wsl[buf][:, c, :], att_sb[:, c, ts(tc, TC)],
                   start=(c == 0), stop=(c == NCH - 1), wait=w,
                   inc=([("pe", 1)] if c == NCH - 1 else ()))
            C.wslab_rel[buf] = p.val("pe")
            p.add("v", (lambda ot=ot, tc=tc, bk=bk: lambda e: e.tensor_add(
                C.x[:, ot, ts(tc, TC)], C.x[:, ot, ts(tc, TC)], bk[:, 0:TC]))(),
                wait=[("pe", p.val("pe"))])
            C.oproj_rel[(ot % 2) * 2 + tc] = p.val("ve")
    C.x_ready_ve = p.val("ve")


def build_group(parts):
    nc = bass.Bass("TRN2", target_bir_lowering=False,
                   detect_race_conditions=False)
    C = Ctx()
    C.nc = nc
    C.plan = Plan()
    C.ln_marks = []
    C.wslab_rel = {}
    C.sbank_rel = {}
    C.abank_rel = {}
    C.obank_rel = {}
    C.tr_rel = 0
    C.kvbank_rel = 0
    C.zrow_rel = 0
    C.zb_rel = 0
    C.b56_se = 0
    C.gb_rel = 0
    C.silu_rel = 0
    C.pt_se_rel = {}
    C.pt_ve_rel = {}
    C.pt_pe_rel = {}
    C.sum_se_rel = 0
    C.m0_ve_rel = 0
    C.m0row_se_rel = 0
    C.qup_se_rel = 0
    C.qro_ve_rel = 0
    C.qra_se_rel = 0
    C.atto_dy_rel = 0
    C.acc_ve_rel = 0
    C.oproj_rel = {}
    C.vtr_rel = 0
    C.vtr_se_rel = 0
    C.rot_rel = 0
    C.kro_dy_rel = 0
    p = C.plan

    need = set(parts)
    C.t_xin = nc.dram_tensor("x_in", [D, TOK], F32, kind="ExternalInput").ap()
    C.t_xout = nc.dram_tensor("x_out", [D, TOK], F32, kind="ExternalOutput").ap()
    C.t_ident = nc.dram_tensor("ident", [128, 128], F32, kind="ExternalInput").ap()

    if {"ffn0a", "ffn0b"} & need:
        C.t_A2 = nc.dram_tensor("A2", [D, 512], F32, kind="ExternalInput").ap()
        C.t_B2 = nc.dram_tensor("B2", [512, D], F32, kind="ExternalInput").ap()
        C.t_wrT = nc.dram_tensor("WrT", [D, 8], F32, kind="ExternalInput").ap()
        C.t_br = nc.dram_tensor("br", [1, 8], F32, kind="ExternalInput").ap()
        C.t_esel = nc.dram_tensor("Esel", [8, 512], F32, kind="ExternalInput").ap()
    if {"ffn1a", "ffn1b"} & need:
        C.t_oneh8 = nc.dram_tensor("oneh8", [8, 1024], F32, kind="ExternalInput").ap()
        C.t_gwT = nc.dram_tensor("gwT", [D, 8], F32, kind="ExternalInput").ap()
        C.t_rb = nc.dram_tensor("rb", [1, 8], F32, kind="ExternalInput").ap()
        C.t_w1r = nc.dram_tensor("W1r", [8, 11, D, 128], F32, kind="ExternalInput").ap()
        C.t_w3r = nc.dram_tensor("W3r", [8, 11, D, 128], F32, kind="ExternalInput").ap()
        C.t_w2r = nc.dram_tensor("W2r", [8, NCH, 1408, 128], F32, kind="ExternalInput").ap()
        C.t_sw1r = nc.dram_tensor("sw1r", [22, D, 128], F32, kind="ExternalInput").ap()
        C.t_sw3r = nc.dram_tensor("sw3r", [22, D, 128], F32, kind="ExternalInput").ap()
        C.t_sw2r = nc.dram_tensor("sw2r", [NCH, 2816, 128], F32, kind="ExternalInput").ap()
    if "attn1" in need:
        C.t_qar = nc.dram_tensor("qar", [6, D, 128], F32, kind="ExternalInput").ap()
        C.t_kvar = nc.dram_tensor("kvar", [5, D, 128], F32, kind="ExternalInput").ap()
        C.t_qbr = nc.dram_tensor("qbr", [3, 768, 128], F32, kind="ExternalInput").ap()
        C.t_kvbr = nc.dram_tensor("kvbr", [4, 512, 128], F32, kind="ExternalInput").ap()
        C.t_wor = nc.dram_tensor("wor", [NCH, D, 128], F32, kind="ExternalInput").ap()
        C.t_qgam = nc.dram_tensor("qgam", [128, 6], F32, kind="ExternalInput").ap()
        C.t_kgam = nc.dram_tensor("kgam", [128, 4], F32, kind="ExternalInput").ap()
        C.t_cos1 = nc.dram_tensor("cos1", [64, TOK], F32, kind="ExternalInput").ap()
        C.t_sin1 = nc.dram_tensor("sin1", [64, TOK], F32, kind="ExternalInput").ap()
        C.t_cos2 = nc.dram_tensor("cos2", [128, T], F32, kind="ExternalInput").ap()
        C.t_sin2 = nc.dram_tensor("sin2", [128, T], F32, kind="ExternalInput").ap()
        C.t_perm1 = nc.dram_tensor("perm1", [64, 64], F32, kind="ExternalInput").ap()
        C.t_perm2 = nc.dram_tensor("perm2", [128, 128], F32, kind="ExternalInput").ap()
        C.t_trilneg = nc.dram_tensor("trilneg", [128, 128], F32, kind="ExternalInput").ap()
        C.t_maskq = nc.dram_tensor("maskq", [128, 4 * TC], F32, kind="ExternalInput").ap()
        C.t_coff = nc.dram_tensor("coff", [1, 32], mybir.dt.int32, kind="ExternalInput").ap()
        C.t_contrib = nc.dram_tensor("contrib", [1344, TOK], BF16).ap()
        C.t_gath = nc.dram_tensor("gath", [NCORES * 1344, TOK], BF16).ap()
        C.t_attc = nc.dram_tensor("attc", [256, T], BF16).ap()
        C.t_gath2 = nc.dram_tensor("gath2", [NCORES * 256, T], BF16).ap()
        C.t_gath2_h = C.t_gath2.tensor
    if "attn0" in need:
        C.t_qwr = nc.dram_tensor("qwr", [NCH, D, 128], F32, kind="ExternalInput").ap()
        C.t_kwr = nc.dram_tensor("kwr", [NCH, D, 128], F32, kind="ExternalInput").ap()
        C.t_vwr = nc.dram_tensor("vwr", [NCH, D, 128], F32, kind="ExternalInput").ap()
        C.t_owr = nc.dram_tensor("owr", [NCH, D, 128], F32, kind="ExternalInput").ap()
        C.t_qb = nc.dram_tensor("qb2", [128, NCH], F32, kind="ExternalInput").ap()
        C.t_kb = nc.dram_tensor("kb2", [128, NCH], F32, kind="ExternalInput").ap()
        C.t_vb = nc.dram_tensor("vb2", [128, NCH], F32, kind="ExternalInput").ap()
        C.t_ob = nc.dram_tensor("ob2", [128, NCH], F32, kind="ExternalInput").ap()
        C.t_kvpart = nc.dram_tensor("kvpart", [D, 132], F32).ap()
        C.t_kvred = nc.dram_tensor("kvred", [D, 132], F32).ap()

    ctx = ExitStack()

    def sb(name, shape, dt):
        h = ctx.enter_context(nc.sbuf_tensor(name, shape, dt))
        return h[(slice(None),) * len(shape)]

    C.x = sb("x_sb", [128, NCH, TOK], F32)
    C.ln = sb("ln_sb", [128, NCH, TOK], BF16)
    C.arena_a = sb("arena_a", [128, 16384], BF16)
    C.arena_b = sb("arena_b", [128, 16384], BF16)
    C.arena_f = sb("arena_f", [128, 3144], F32)
    C.wslab = sb("wslab", [128, 8192], BF16)
    C.sq_bf = sb("sq_bf", [128, TC], BF16)
    C.f32scr = C.arena_f[:, 0:TC]
    C.f32scr2 = C.arena_f[:, 2632:3144]
    C.consts = sb("consts", [128, 16], F32)
    C.row_a = sb("row_a", [1, TC], F32)
    C.row_b = sb("row_b", [1, TC], F32)
    C.stat_m = C.arena_f[0:1, 520:1544]
    C.stat_r = C.arena_f[0:1, 1544:2568]
    C.row8 = sb("row8", [1, 8], F32)
    C.ones_f32 = sb("ones_f32", [128, 1], F32)
    C.ones_inv2048_f32 = sb("ones_i2048f", [128, 1], F32)
    C.ones_inv2048_bf = sb("ones_i2048b", [128, 1], BF16)
    C.ones1_f32 = sb("ones1_f32", [1, 128], F32)
    C.ones1_bf = sb("ones1_bf", [1, 128], BF16)
    C.ones11_f32 = sb("ones11_f32", [1, 1], F32)
    C.ident_f32 = sb("ident_f32", [128, 128], F32)
    C.ident_bf = sb("ident_bf", [128, 128], BF16)

    if {"ffn0a", "ffn0b", "ffn1a", "ffn1b"} & need:
        C.rw_sb = sb("rw_sb", [128, NCH, 8], F32)
        C.gate_wsb = sb("gate_wsb", [128, 8], F32)
        C.gate_bb = sb("gate_bb", [128, 8], F32)
        C.g8 = sb("g8", [128, 8], F32)
        C.g8b = sb("g8b", [128, 8], F32)
        C.g8c = sb("g8c", [128, 8], F32)
        C.g8d = sb("g8d", [128, 8], F32)
        C.colscr = sb("colscr", [128, 4], F32)
        if DEBUG_DUMP:
            C.dbg_sb = C.arena_b[:, 12288:12416].bitcast(F32)
        C.gT = sb("gT", [8, TOK], BF16)
        sel8 = sb("sel8", [8, 1024], BF16)
        C.esel_sb = sel8[0:8, 0:512]
        C.oneh8_sb = sel8[0:8, 0:1024]
    if "attn0" in need:
        C.qb_sb = sb("qb_sb", [128, NCH], F32)
        C.kb_sb = sb("kb_sb", [128, NCH], F32)
        C.vb_sb = sb("vb_sb", [128, NCH], F32)
        C.ob_sb = sb("ob_sb", [128, NCH], F32)
        C.kvbf_sb = C.arena_f[:, 520:1576].bitcast(BF16).rearrange(
            "p (h f) -> p h f", h=NCH)

    if "attn1" in need:
        C.kra_sb = C.ln.rearrange("p c t -> p (c t)")[0:65, 6144:14336]
        C.qtn_sb = sb("qtn_sb", [128, 2, TC], BF16)
        C.qra0 = sb("qra0", [65, TC], BF16)
        C.qra1 = sb("qra1", [65, TC], BF16)
        C.nrm_bf = C.arena_f[:, 520:1032].bitcast(BF16)
        C.pt_bf = C.arena_f[:, 1032:1544].bitcast(BF16)
        C.qr_bf = C.arena_f[:, 1544:1800].bitcast(BF16)
        C.attno_bf = C.arena_f[:, 1800:2056].bitcast(BF16)
        C.kr_bf = C.arena_f[:, 2056:2312].bitcast(BF16)
        C.kvb_sb = C.wslab[:, 6144:8192].rearrange(
            "p (o c m) -> p o c m", o=4, c=4)
        C.qbr_sb = C.wslab[:, 0:2304].rearrange(
            "p (o c m) -> p o c m", o=3, c=6)
        C.perm1_sb = sb("perm1_sb", [64, 64], BF16)
        C.perm2_sb = sb("perm2_sb", [128, 128], BF16)
        C.trilneg_sb = sb("trilneg_sb", [128, 128], F32)
        C.maskq_sb = C.ln.rearrange("p c t -> p (c t)")[:, 14336:16384]
        C.qgam_sb = sb("qgam_sb", [128, 6], F32)
        C.kgam_sb = sb("kgam_sb", [128, 4], F32)
        C.coff_sb = sb("coff_sb", [1, 32], mybir.dt.int32)
    C.colscr2 = sb("colscr2", [128, 2], F32)
    C.ones_bf = sb("ones_bf", [128, 1], BF16)

    C.psum = [ctx.enter_context(nc.psum_tensor(f"bank{i}", [128, 512], F32))[:, :]
              for i in range(8)]

    sems = {}
    for name in ("pe", "ve", "se", "ge", "dg", "dy", "cc"):
        sems[name] = ctx.enter_context(nc.semaphore(name))

    const_vals = {
        "ln_eps": 1e-5, "rms_eps": 1e-6, "z_eps": 1e-6,
        "inv768": 1.0 / 768.0, "inv512": 1.0 / 512.0,
        "qscale": float(1.0 / np.sqrt(192.0)),
    }
    C.const_idx = {k: i for i, k in enumerate(const_vals)}

    def init_fn(e):
        for k, v in const_vals.items():
            e.memset(C.consts[:, C.const_idx[k]:C.const_idx[k] + 1], v)
        e.memset(C.ones_f32[:], 1.0)
        e.memset(C.ones_inv2048_f32[:], 1.0 / 2048.0)
        e.memset(C.ones_inv2048_bf[:], 1.0 / 2048.0)
        e.memset(C.ones1_f32[:], 1.0)
        e.memset(C.ones1_bf[:], 1.0)
        e.memset(C.ones_bf[:], 1.0)
        return e.memset(C.ones11_f32[:], 1.0)
    p.add("g", init_fn, inc=[("ge", 1)])
    p.add("g", lambda e: e.dma_start(out=C.ident_f32[:], in_=C.t_ident),
          inc=[("dg", 16)])
    p.add("g", lambda e: e.dma_start(out=C.ident_bf[:], in_=C.t_ident),
          inc=[("dg", 16)])
    if "attn1" in need:
        p.add("g", lambda e: e.dma_start(out=C.perm1_sb, in_=C.t_perm1), inc=[("dg", 16)])
        p.add("g", lambda e: e.dma_start(out=C.perm2_sb, in_=C.t_perm2), inc=[("dg", 16)])
        p.add("g", lambda e: e.dma_start(out=C.trilneg_sb, in_=C.t_trilneg), inc=[("dg", 16)])
        p.add("g", lambda e: e.dma_start(out=C.qgam_sb, in_=C.t_qgam), inc=[("dg", 16)])
        p.add("g", lambda e: e.dma_start(out=C.kgam_sb, in_=C.t_kgam), inc=[("dg", 16)])
    if "attn0" in need:
        p.add("g", lambda e: e.dma_start(out=C.qb_sb, in_=C.t_qb), inc=[("dg", 16)])
        p.add("g", lambda e: e.dma_start(out=C.kb_sb, in_=C.t_kb), inc=[("dg", 16)])
        p.add("g", lambda e: e.dma_start(out=C.vb_sb, in_=C.t_vb), inc=[("dg", 16)])
        p.add("g", lambda e: e.dma_start(out=C.ob_sb, in_=C.t_ob), inc=[("dg", 16)])
    fence = [("ge", p.val("ge")), ("dg", p.val("dg"))]
    p.add("t", lambda e: e.nop(), wait=fence)
    p.add("v", lambda e: e.nop(), wait=fence, nops=0)
    p.add("s", lambda e: e.nop(), wait=fence, nops=0)
    p.add("y", lambda e: e.dma_start(out=C.x[:], in_=C.t_xin.rearrange(
        "(c p) t -> p c t", p=128)), inc=[("dy", 16)])
    C.x_ready_dy = p.val("dy")
    C.x_ready_ve = 0
    C.ln_ready_ve = 0
    C.ln_stats_ve = 0

    for part in parts:
        emit_ln(C)
        if part in ("ffn0a", "ffn0b"):
            emit_ffn0(C)
        elif part == "attn0":
            emit_attn0(C)
        elif part in ("ffn1a", "ffn1b"):
            emit_ffn1(C)
        elif part == "attn1":
            emit_attn1(C)
        else:
            raise NotImplementedError(part)

    p.add("y", lambda e: e.dma_start(out=C.t_xout.rearrange(
        "(c p) t -> p c t", p=128), in_=C.x[:]),
        wait=[("ve", C.x_ready_ve)], inc=[("dy", 16)])
    p.add("y", lambda e: e.nop(), wait=[("dy", p.val("dy"))])

    _realize(nc, p, sems)
    ctx.close()
    return nc


def _prep(inputs):
    g = {}
    x = np.asarray(inputs["L0_attn_hidden_states"], np.float32)
    g["xT"] = np.ascontiguousarray(x[0].T)
    A = np.asarray(inputs["L0_ffn_A"], np.float32)
    g["A2"] = np.ascontiguousarray(A.transpose(2, 0, 1).reshape(D, 512))
    B = np.asarray(inputs["L0_ffn_B"], np.float32)
    g["B2"] = np.ascontiguousarray(B.transpose(0, 2, 1).reshape(512, D))
    g["WrT"] = np.ascontiguousarray(np.asarray(inputs["L0_ffn_Wr"], np.float32).T)
    g["br"] = np.asarray(inputs["L0_ffn_br"], np.float32).reshape(1, 8)
    esel = np.zeros((8, 512), np.float32)
    for e in range(8):
        esel[e, e * 64:(e + 1) * 64] = 1.0
    g["Esel"] = esel
    oneh8 = np.zeros((8, 1024), np.float32)
    for e in range(8):
        oneh8[e, e * 128:(e + 1) * 128] = 1.0
    g["oneh8"] = oneh8
    g["ident"] = np.eye(128, dtype=np.float32)

    def repack(wT):
        return np.ascontiguousarray(
            wT.reshape(wT.shape[0], wT.shape[1] // 128, 128).transpose(1, 0, 2))

    for nm, key in (("qwr", "L0_attn_q_w"), ("kwr", "L0_attn_k_w"),
                    ("vwr", "L0_attn_v_w"), ("owr", "L0_attn_o_w")):
        w = np.asarray(inputs[key], np.float32)
        g[nm] = repack(np.ascontiguousarray(w.T))
    for nm, key in (("qb2", "L0_attn_q_b"), ("kb2", "L0_attn_k_b"),
                    ("vb2", "L0_attn_v_b"), ("ob2", "L0_attn_o_b")):
        b = np.asarray(inputs[key], np.float32)
        g[nm] = np.ascontiguousarray(b.reshape(NCH, 128).T)

    g["gwT"] = np.ascontiguousarray(
        np.asarray(inputs["L1_ffn_gate_w"], np.float32).T)
    g["rb"] = np.asarray(inputs["L1_ffn_router_bias"], np.float32).reshape(1, 8)
    W1 = np.asarray(inputs["L1_ffn_W1"], np.float32)
    g["W1r"] = np.ascontiguousarray(
        W1.reshape(8, D, 11, 128).transpose(0, 2, 1, 3))
    W3 = np.asarray(inputs["L1_ffn_W3"], np.float32)
    g["W3r"] = np.ascontiguousarray(
        W3.reshape(8, D, 11, 128).transpose(0, 2, 1, 3))
    W2 = np.asarray(inputs["L1_ffn_W2"], np.float32)
    g["W2r"] = np.ascontiguousarray(
        W2.reshape(8, 1408, NCH, 128).transpose(0, 2, 1, 3))
    g["sw1r"] = np.ascontiguousarray(
        np.asarray(inputs["L1_ffn_sw1"], np.float32).T.reshape(
            D, 22, 128).transpose(1, 0, 2))
    g["sw3r"] = np.ascontiguousarray(
        np.asarray(inputs["L1_ffn_sw3"], np.float32).T.reshape(
            D, 22, 128).transpose(1, 0, 2))
    g["sw2r"] = np.ascontiguousarray(
        np.asarray(inputs["L1_ffn_sw2"], np.float32).T.reshape(
            2816, NCH, 128).transpose(1, 0, 2))

    w_qa = np.asarray(inputs["L1_attn_w_qa"], np.float32)
    g["qar"] = np.ascontiguousarray(
        w_qa.reshape(D, 6, 128).transpose(1, 0, 2))
    w_kva = np.asarray(inputs["L1_attn_w_kva"], np.float32)
    kvar = np.zeros((5, D, 128), np.float32)
    kvar[:4] = w_kva[:, 0:512].reshape(D, 4, 128).transpose(1, 0, 2)
    kvar[4, :, 0:64] = w_kva[:, 512:576]
    g["kvar"] = kvar
    w_qb = np.asarray(inputs["L1_attn_w_qb"], np.float32)
    w_kvb = np.asarray(inputs["L1_attn_w_kvb"], np.float32)
    for c in range(NCORES):
        a, b = 2 * c * 192, (2 * c + 1) * 192
        qb_s = np.concatenate([w_qb[:, a:a + 128], w_qb[:, b:b + 128],
                               w_qb[:, a + 128:a + 192],
                               w_qb[:, b + 128:b + 192]], axis=1)
        g[f"qbr_{c}"] = np.ascontiguousarray(
            qb_s.reshape(768, 3, 128).transpose(1, 0, 2))
        kvb_s = w_kvb[:, 2 * c * 256:(2 * c + 2) * 256]
        g[f"kvbr_{c}"] = np.ascontiguousarray(
            kvb_s.reshape(512, 4, 128).transpose(1, 0, 2))
    w_o = np.asarray(inputs["L1_attn_w_o"], np.float32)
    g["wor"] = np.ascontiguousarray(
        w_o.reshape(D, NCH, 128).transpose(1, 0, 2))
    g["qgam"] = np.ascontiguousarray(
        np.asarray(inputs["L1_attn_q_a_norm_w"], np.float32).reshape(6, 128).T)
    g["kgam"] = np.ascontiguousarray(
        np.asarray(inputs["L1_attn_kv_a_norm_w"], np.float32).reshape(4, 128).T)
    inv_freq = 1.0 / (10000.0 ** (np.arange(0, 64, 2, dtype=np.float32) / 64))
    pos = np.arange(T, dtype=np.float32)
    emb = np.concatenate([np.outer(pos, inv_freq)] * 2, axis=1)
    cosT = np.ascontiguousarray(np.cos(emb).T.astype(np.float32))
    sinT = np.ascontiguousarray(np.sin(emb).T.astype(np.float32))
    for c in range(NCORES):
        g[f"cos1_{c}"] = np.ascontiguousarray(cosT[:, c * TOK:(c + 1) * TOK])
        g[f"sin1_{c}"] = np.ascontiguousarray(sinT[:, c * TOK:(c + 1) * TOK])
    g["cos2"] = np.ascontiguousarray(np.concatenate([cosT, cosT], axis=0))
    g["sin2"] = np.ascontiguousarray(np.concatenate([sinT, sinT], axis=0))
    P = np.zeros((64, 64), np.float32)
    for i in range(32):
        P[2 * i, 2 * i + 1] = -1.0
        P[2 * i + 1, 2 * i] = 1.0
    g["perm1"] = np.ascontiguousarray(P.T)
    P2 = np.zeros((128, 128), np.float32)
    P2[0:64, 0:64] = P.T
    P2[64:128, 64:128] = P.T
    g["perm2"] = P2
    tril = np.zeros((128, 128), np.float32)
    for q in range(128):
        tril[q, q + 1:] = -1e30
    g["trilneg"] = tril
    maskq = np.zeros((128, 4, TC), np.float32)
    for r_ in range(4):
        for k_ in range(128):
            lo = k_ + 128 * r_
            if lo < TC:
                maskq[k_, r_, lo:] = 1.0
    g["maskq"] = np.ascontiguousarray(maskq.reshape(128, 4 * TC))
    for c in range(NCORES):
        coff = np.zeros((1, 32), np.int32)
        for ch in range(NCH):
            for tc_ in range(NTC):
                coff[0, ch * 2 + tc_] = ch * 128 * T + tc_ * TC + c * TOK
        g[f"coff_{c}"] = coff
    return g


def _inputs_for(parts, g):
    m = {"ident": g["ident"]}
    if {"ffn1a", "ffn1b"} & set(parts):
        m["oneh8"] = g["oneh8"]
    if {"ffn0a", "ffn0b"} & set(parts):
        for k in ("A2", "B2", "WrT", "br", "Esel"):
            m[k] = g[k]
    if "attn0" in set(parts):
        for k in ("qwr", "kwr", "vwr", "owr", "qb2", "kb2", "vb2", "ob2"):
            m[k] = g[k]
    if {"ffn1a", "ffn1b"} & set(parts):
        for k in ("gwT", "rb", "W1r", "W3r", "W2r", "sw1r", "sw3r", "sw2r"):
            m[k] = g[k]
    return m


def _inputs_for_core(parts, g, c):
    m = {}
    if "attn1" in set(parts):
        for k in ("qar", "kvar", "wor", "qgam", "kgam", "cos2", "sin2",
                  "perm1", "perm2", "trilneg", "maskq"):
            m[k] = g[k]
        m["qbr"] = g[f"qbr_{c}"]
        m["kvbr"] = g[f"kvbr_{c}"]
        m["cos1"] = g[f"cos1_{c}"]
        m["sin1"] = g[f"sin1_{c}"]
        m["coff"] = g[f"coff_{c}"]
    return m


def kernel(**inputs):
    g = _prep(inputs)
    xT = g["xT"]
    shards = [np.ascontiguousarray(xT[:, c * TOK:(c + 1) * TOK])
              for c in range(NCORES)]

    for parts in LAUNCH_GROUPS:
        nc = build_group(parts)
        base = _inputs_for(parts, g)
        in_maps = [dict(base, x_in=shards[c], **_inputs_for_core(parts, g, c))
                   for c in range(NCORES)]
        res = run_bass_kernel_spmd(nc, in_maps, core_ids=list(range(NCORES)))
        shards = [np.ascontiguousarray(res.results[c]["x_out"])
                  for c in range(NCORES)]

    out = np.concatenate(shards, axis=1)
    return np.ascontiguousarray(out.T).reshape(1, T, D)
